# version 01
# baseline (speedup 1.0000x reference)
import numpy as np
import ml_dtypes
import concourse.bass as bass
import concourse.mybir as mybir
from concourse.bass_utils import run_bass_kernel_spmd

F32 = mybir.dt.float32
BF16 = mybir.dt.bfloat16
I32 = mybir.dt.int32
AF = mybir.ActivationFunctionType
ALU = mybir.AluOpType
AXX = mybir.AxisListType.X
BF = ml_dtypes.bfloat16

D = 2048
DK = 16
HFF = 5632
CFF = 44
NH = 16
QKV = 6144
CQK = 48
HALO = 64
C = 128
EPS_LN = 1e-5
EPS_Q = 1e-8
NCORES = 8
CLAMP = 80.0
INVCAP = 1e12



class Em:

    ENGINES = ("sync", "tensor", "vector", "scalar", "gpsimd")

    CHAINED = ("vector", "scalar", "gpsimd")
    CHAIN_OPS = frozenset("""tensor_copy tensor_tensor tensor_add tensor_sub
        tensor_mul tensor_max tensor_scalar tensor_single_scalar
        tensor_scalar_add tensor_scalar_sub tensor_scalar_mul tensor_scalar_max
        tensor_scalar_min tensor_relu tensor_reduce reduce_max reduce_sum
        scalar_tensor_tensor tensor_tensor_scan memset memzero reciprocal
        activation copy add mul sqrt square sign select bn_stats bn_aggr
        transpose pool pool_avg pool_max stream_shuffle""".split())

    class _MockInst:
        def then_inc(self, *a, **kw):
            return self

    class _CountEng:

        def __init__(self):
            self.n = 0

        def __getattr__(self, nm):
            if nm in Em.CHAIN_OPS:
                def c(*a, **kw):
                    self.n += 1
                    return Em._MockInst()
                return c
            return lambda *a, **kw: Em._MockInst()

    class _ChainEng:

        def __init__(self, eng, sem_h):
            self._e = eng
            self._h = sem_h
            self._n = 0

        def __getattr__(self, nm):
            attr = getattr(self._e, nm)
            if nm in Em.CHAIN_OPS:
                def call(*a, _attr=attr, **kw):
                    if self._n:
                        self._e.wait_ge(self._h, self._n)
                    inst = _attr(*a, **kw)
                    inst.then_inc(self._h, 1)
                    self._n += 1
                    return inst
                return call
            return attr

    def __init__(self, nc):
        self.nc = nc
        self.q = {e: [] for e in self.ENGINES}
        self._s = {}
        self.own = {}

    def sem(self, name):
        if name not in self._s:
            self._s[name] = [self.nc.alloc_semaphore(name), 0]
        return name

    def n(self, s):
        return self._s[s][1]

    def em(self, engine, fn, inc=None, k=1):
        if engine in self.CHAINED:
            mock = Em._CountEng()
            fn(mock)
            if mock.n > 0:
                own = self.own[engine]
                assert inc is None or inc == own, (engine, inc)
                self._s[own][1] += mock.n
                self.q[engine].append(fn)
                return self._s[own][1]
        if inc is None:
            self.q[engine].append(fn)
            return None
        h = self._s[inc][0]
        self._s[inc][1] += k

        def f2(eng, fn=fn, h=h, k=k):
            fn(eng).then_inc(h, k)
        self.q[engine].append(f2)
        return self._s[inc][1]

    def wt(self, engine, sem, val=None):
        v = self.n(sem) if val is None else val
        h = self._s[sem][0]
        self.q[engine].append(lambda eng, h=h, v=v: eng.wait_ge(h, v))

    def dma(self, engine, out, in_, sem, **kw):
        h = self._s[sem][0]
        self._s[sem][1] += 16

        def f(eng, out=out, in_=in_, h=h, kw=kw):
            eng.dma_start(out=out, in_=in_, **kw).then_inc(h, 16)
        self.q[engine].append(f)
        return self._s[sem][1]

    def barrier(self, tag):
        b = self.sem("bar_" + tag)
        for e in self.ENGINES:
            self.em(e, lambda eng: eng.nop(), inc=b)
        tgt = self.n(b)
        for e in self.ENGINES:
            self.wt(e, b, tgt)

    def run(self):
        with self.nc.Block() as block:
            for e in self.ENGINES:
                fns = self.q[e]
                own = self.own.get(e)

                def body(eng, fns=fns, own=own):
                    if own is not None:
                        eng = Em._ChainEng(eng, self._s[own][0])
                    for fn in fns:
                        fn(eng)
                getattr(block, e)(body)



def build_program(mode, T_local):
    nc = bass.Bass("TRN2", target_bir_lowering=False, debug=False)
    TH = T_local + HALO
    n_tg = (T_local + 511) // 512
    n_ch = T_local // C

    inp = {}

    def din(nm, shp):
        inp[nm] = nc.dram_tensor(nm, shp, F32, kind="ExternalInput").ap()
        return inp[nm]

    if mode == "A":
        din("xT", [D, TH])
        din("wqkvT", [D, QKV]); din("wabT", [D, 32])
        din("convw", [QKV, 4]); din("bab", [1, 32])
    else:
        din("x1T", [D, TH])
        din("wgT", [D, D]); din("woT", [D, D])
        din("spill_pt", [n_ch * NH * 128, 128])
        din("spill_op", [n_ch * NH * 128, 128])
        din("ghseq", [(NCORES - 1) * 2 * NH * 128, 128])
    din("w1T", [D, HFF]); din("w2T", [D, HFF]); din("w3T", [HFF, D])

    out = {}
    if mode == "A":
        out["x1T_out"] = nc.dram_tensor("x1T_out", [D, TH], F32, kind="ExternalOutput").ap()
        out["spill_pt"] = nc.dram_tensor("spill_pt", [n_ch * NH * 128, 128], F32, kind="ExternalOutput").ap()
        out["spill_op"] = nc.dram_tensor("spill_op", [n_ch * NH * 128, 128], F32, kind="ExternalOutput").ap()
        out["ght_out"] = nc.dram_tensor("ght_out", [NH * 128, 128], F32, kind="ExternalOutput").ap()
        out["h_out"] = nc.dram_tensor("h_out", [NH * 128, 128], F32, kind="ExternalOutput").ap()
    else:
        out["yT"] = nc.dram_tensor("yT", [D, T_local], F32, kind="ExternalOutput").ap()

    cw1 = nc.dram_tensor("cw1", [CFF, 128, DK * 128], BF16, kind="Internal")
    cw2 = nc.dram_tensor("cw2", [CFF, 128, DK * 128], BF16, kind="Internal")
    cw3 = nc.dram_tensor("cw3", [DK, 128, CFF * 128], BF16, kind="Internal")
    if mode == "A":
        cqkv = nc.dram_tensor("cqkv", [CQK, 128, DK * 128], BF16, kind="Internal")
        cab = nc.dram_tensor("cab", [DK, 128, 32], BF16, kind="Internal")
    else:
        cwg = nc.dram_tensor("cwg", [DK, 128, DK * 128], BF16, kind="Internal")
        cwo = nc.dram_tensor("cwo", [DK, 128, DK * 128], BF16, kind="Internal")

    KONST = {
        "idb": np.eye(128).astype(BF),
        "idf": np.eye(128, dtype=np.float32),
        "on1b": np.ones((1, 128)).astype(BF),
        "on1f": np.ones((1, 128), dtype=np.float32),
        "oncf": np.ones((128, 1), dtype=np.float32),
        "oncb": np.ones((128, 1)).astype(BF),
        "trii": np.triu(np.ones((128, 128), dtype=np.float32)),
        "trir": np.tril(np.ones((128, 128), dtype=np.float32), -1),
        "mlown": -np.tril(np.ones((128, 128), dtype=np.float32), -1),
        "mupn": -np.triu(np.ones((128, 128), dtype=np.float32), 1),
        "mupi": np.triu(np.ones((128, 128), dtype=np.float32)),
    }
    ktens = {nm: nc.inline_tensor(v, "k_" + nm) for nm, v in KONST.items()}
    cs = {}
    for nm, v in KONST.items():
        cs[nm] = nc.alloc_sbuf_tensor("c_" + nm, list(v.shape),
                                      BF16 if v.dtype == BF else F32)

    sbB = lambda n, s: nc.alloc_sbuf_tensor(n, s, BF16)
    sbF = lambda n, s: nc.alloc_sbuf_tensor(n, s, F32)

    XT = sbB("XT", [128, DK * TH])
    XN = sbB("XN", [128, DK * 512])
    RGN = nc.alloc_sbuf_tensor("RGN", [128, 45056 // 4], F32)
    off = nc.lookup_mloc(RGN).addr
    HS = nc.alloc_sbuf_tensor_at("HS", [128, CFF * 512], BF16, offset=off)
    WQf = nc.alloc_sbuf_tensor_at("WQf", [128, 2816], F32, offset=off)
    WQi = nc.alloc_sbuf_tensor_at("WQi", [128, 2816], I32, offset=off + 11264)
    WQb = nc.alloc_sbuf_tensor_at("WQb", [128, 2816], BF16, offset=off + 22528)
    SPF1 = nc.alloc_sbuf_tensor_at("SPF1", [128, NH * 128], F32, offset=off)
    SPF2 = nc.alloc_sbuf_tensor_at("SPF2", [128, NH * 128], F32, offset=off + 8192)

    W3S0 = sbB("W3S0", [128, CFF * 128])
    W3S1 = sbB("W3S1", [128, CFF * 128])
    WSL = [W3S0[:, 0:DK * 128], W3S0[:, DK * 128:2 * DK * 128]]
    WSL2 = [W3S1[:, 0:DK * 128], W3S1[:, DK * 128:2 * DK * 128]]
    W3S = [W3S0, W3S1]

    STAT = sbF("STAT", [1, 2048])
    ROWS = sbF("ROWS", [1, 128])
    GINV = sbF("GINV", [128, 4])
    GCOLS = sbF("GCOLS", [128, 4])
    BC1 = sbF("BC1", [128, 512])
    BC2 = sbF("BC2", [128, 512])
    SGT = sbF("SGT", [128, 512])
    JUNK = sbF("JUNK", [128, 512])

    if mode == "A":
        QKVC = sbB("QKVC", [128, CQK * 128])
        QKVP = sbB("QKVP", [128, CQK * 4])
        CONVW = sbF("CONVW", [128, CQK * 4])
        BABB = sbF("BABB", [128, 32])
        WAB = sbB("WAB", [128, DK * 32])
        TRN = sbB("TRN", [128, 3 * 128])
        XS = sbB("XS", [128, 256])
        PW_P = sbB("PW_P", [128, 128])
        PW_T = sbB("PW_T", [128, 128])
        EOT = sbB("EOT", [128, 128])
        QTT = sbB("QTT", [128, 128])
        RSOL = sbB("RSOL", [128, 256])
        KTL = sbB("KTL", [128, 128])
        GT_C = sbB("GT_C", [128, 128])
        GACC = sbB("GACC", [128, NH * 128])
        HACC = sbB("HACC", [128, NH * 128])
        GACCT = sbB("GACCT", [128, NH * 128])
        DNAT = sbF("DNAT", [128, 128])
        DTRN = sbF("DTRN", [128, 128])
        EOX = sbF("EOX", [128, 128])
        EARG = sbF("EARG", [128, 128])
        TT1 = sbF("TT1", [128, 128])
        TT2 = sbF("TT2", [128, 128])
        ABR = sbF("ABR", [32, 128])
        ASIG = sbF("ASIG", [128, 32])
        GAMS = sbF("GAMS", [128, 7 * NH])
        NK2 = sbF("NK2", [128, NH]); NQ2 = sbF("NQ2", [128, NH])
        LNK = sbF("LNK", [128, NH]); LNQ = sbF("LNQ", [128, NH])
        INVK = sbF("INVK", [128, NH]); INVQ = sbF("INVQ", [128, NH])
        GPC = sbF("GPC", [128, NH])
        BGIK = sbF("BGIK", [128, NH]); RTIK = sbF("RTIK", [128, NH])
        PACK = sbF("PACK", [128, 3 * NH])
        RROW = sbF("RROW", [1, 128])
    else:
        ZIN = sbB("ZIN", [128, NH * 128])
        GHS = sbB("GHS", [128, 2 * 128])
        GROW = sbB("GROW", [128, DK * 512])
        OFIX = sbB("OFIX", [128, NH * 512])
        SPB1 = nc.alloc_sbuf_tensor_at("SPB1", [128, NH * 128], BF16, offset=off + 16384)
        SPB2 = nc.alloc_sbuf_tensor_at("SPB2", [128, NH * 128], BF16, offset=off + 20480)

    PSB0 = nc.alloc_psum_tensor("PSB0", [128, 1024], BF16)
    PS = [None] + [nc.alloc_psum_tensor("PS%d" % i, [128, 512], F32) for i in range(1, 8)]

    P = Em(nc)
    s_ld = P.sem("s_ld"); s_wd = P.sem("s_wd"); s_mm = P.sem("s_mm")
    s_ve = P.sem("s_ve"); s_sc = P.sem("s_sc"); s_gp = P.sem("s_gp")
    s_st = P.sem("s_st"); s_fb = P.sem("s_fb")
    P.own = {"vector": s_ve, "scalar": s_sc, "gpsimd": s_gp}

    def full_barrier(tag):
        wt("tensor", s_mm)
        wt("vector", s_ve)
        wt("scalar", s_sc)
        wt("scalar", s_wd)
        wt("gpsimd", s_gp)
        wt("sync", s_ld)
        wt("sync", s_st)
        P.barrier(tag)
    em, wt, dm = P.em, P.wt, P.dma

    for nm in KONST:
        dm("sync", cs[nm][:, :], ktens[nm].ap()[:, :], s_ld)
    xin = inp["xT"] if mode == "A" else inp["x1T"]
    for k in range(DK):
        dm("gpsimd", XT[:, k * TH:(k + 1) * TH], xin[k * 128:(k + 1) * 128, :], s_gp)
    if mode == "A":
        dm("sync", BABB[0:1, 0:32], inp["bab"][:, :], s_ld)
        for t in range(CQK):
            dm("sync", CONVW[:, t * 4:(t + 1) * 4], inp["convw"][t * 128:(t + 1) * 128, :], s_ld)

    def stream_quant():
        wjobs = [(inp["w1T"], DK, HFF, cw1, 0), (inp["w2T"], DK, HFF, cw2, 1),
                 (inp["w3T"], CFF, D, cw3, 2)]
        for w, ntile, wid, cache, wi in wjobs:
            nh_ = (wid + 2815) // 2816
            col = 0
            for k in range(ntile):
                for hh in range(nh_):
                    c0 = hh * 2816
                    n = min(2816, wid - c0)
                    t = dm("scalar", WQf[:, 0:n], w[k * 128:(k + 1) * 128, c0:c0 + n], s_wd)
                    wt("vector", s_wd, t)
                    v = em("vector", lambda eng, n=n, col=col:
                           eng.reduce_sum(GCOLS[:, 3:4] if False else JUNK[:, col:col + 1],
                                          WQf[:, 0:n], axis=AXX,
                                          apply_absolute_value=True), inc=s_ve)
                    wt("scalar", s_ve, v)
                    col += 1
            em("vector", lambda eng, col=col, wi=wi:
               eng.reduce_sum(GCOLS[:, wi:wi + 1], JUNK[:, 0:col], axis=AXX), inc=s_ve)
        wt("tensor", s_ld)
        wt("tensor", s_ve)
        for wi in range(3):
            em("tensor", lambda eng, wi=wi:
               eng.matmul(PS[7][0:1, wi:wi + 1], GCOLS[:, wi:wi + 1], cs["oncf"][:, :],
                          start=True, stop=True), inc=s_mm)
        wt("vector", s_mm)

        def finv(eng):
            eng.tensor_scalar(ROWS[0:1, 0:3], PS[7][0:1, 0:3], 1.0 / (HFF * D), EPS_Q,
                              ALU.mult, ALU.add)
            return eng.reciprocal(ROWS[0:1, 0:3], ROWS[0:1, 0:3])
        v = em("vector", finv, inc=s_ve)
        wt("tensor", s_ve, v)
        m = em("tensor", lambda eng: eng.matmul(PS[7][:, 4:7], cs["on1f"][0:1, :],
                                                ROWS[0:1, 0:3], start=True, stop=True), inc=s_mm)
        wt("vector", s_mm, m)
        em("vector", lambda eng: eng.tensor_copy(GINV[:, 0:3], PS[7][:, 4:7]), inc=s_ve)

        for w, ntile, wid, cache, wi in wjobs:
            nh_ = (wid + 2815) // 2816
            for k in range(ntile):
                for hh in range(nh_):
                    c0 = hh * 2816
                    n = min(2816, wid - c0)
                    t = dm("scalar", WQf[:, 0:n], w[k * 128:(k + 1) * 128, c0:c0 + n], s_wd)
                    wt("vector", s_wd, t)

                    def fq(eng, n=n, wi=wi):
                        eng.tensor_scalar(WQf[:, 0:n], WQf[:, 0:n], GINV[:, wi:wi + 1],
                                          None, ALU.mult)
                        eng.tensor_scalar(JUNK[:, 0:n] if n <= 512 else WQi[:, 0:n].bitcast(F32),
                                          WQf[:, 0:n], -0.5, None, ALU.is_le)
                        eng.tensor_scalar(WQf[:, 0:n], WQf[:, 0:n], 0.5, None, ALU.is_ge)
                        return eng.tensor_tensor(WQb[:, 0:n], WQf[:, 0:n],
                                                 JUNK[:, 0:n] if n <= 512 else WQi[:, 0:n].bitcast(F32),
                                                 ALU.subtract)
                    v = em("vector", fq, inc=s_ve)
                    wt("sync", s_ve, v)
                    for b in range(n // 128):
                        dm("sync", cache.ap()[(c0 // 128) + b, :, k * 128:(k + 1) * 128],
                           WQb[:, b * 128:(b + 1) * 128], s_st)
                    wt("scalar", s_st)
        cjobs = ([(inp["wqkvT"], cqkv, DK, QKV), (inp["wabT"], cab, DK, 32)] if mode == "A"
                 else [(inp["wgT"], cwg, DK, D), (inp["woT"], cwo, DK, D)])
        for w, cache, ntile, wid in cjobs:
            nh_ = (wid + 2815) // 2816
            for k in range(ntile):
                for hh in range(nh_):
                    c0 = hh * 2816
                    n = min(2816, wid - c0)
                    t = dm("scalar", WQf[:, 0:n], w[k * 128:(k + 1) * 128, c0:c0 + n], s_wd)
                    wt("vector", s_wd, t)
                    v = em("vector", lambda eng, n=n:
                           eng.tensor_copy(WQb[:, 0:n], WQf[:, 0:n]), inc=s_ve)
                    wt("sync", s_ve, v)
                    if wid >= 128:
                        for b in range(n // 128):
                            dm("sync", cache.ap()[(c0 // 128) + b, :, k * 128:(k + 1) * 128],
                               WQb[:, b * 128:(b + 1) * 128], s_st)
                    else:
                        dm("sync", cache.ap()[k, :, 0:n], WQb[:, 0:n], s_st)
                    wt("scalar", s_st)

    stream_quant()
    wt("sync", s_ld)
    full_barrier("p0")

    def ln_stats(xbase, ncols):
        def mm_s1(eng, xbase=xbase, ncols=ncols):
            last = None
            for k in range(DK):
                last = eng.matmul(PS[5][0:1, 0:ncols], cs["oncb"][:, :],
                                  XT[:, k * TH + xbase:k * TH + xbase + ncols],
                                  start=(k == 0), stop=(k == DK - 1))
            return last
        em("tensor", mm_s1, inc=s_mm)
        wt("scalar", s_ve)
        for k in range(DK):
            sq = em("scalar", lambda eng, k=k, xbase=xbase, ncols=ncols:
                    eng.activation(JUNK[:, 0:ncols],
                                   XT[:, k * TH + xbase:k * TH + xbase + ncols], AF.Square),
                    inc=s_sc)
            wt("tensor", s_sc, sq)
            m = em("tensor", lambda eng, k=k, ncols=ncols:
                   eng.matmul(PS[6][0:1, 0:ncols], cs["oncf"][:, :], JUNK[:, 0:ncols],
                              start=(k == 0), stop=(k == DK - 1)), inc=s_mm)
            if k < DK - 1:
                wt("scalar", s_mm, m)
        wt("vector", s_mm)

        def fstat(eng, ncols=ncols):
            mu = STAT[0:1, 0:ncols]
            var = STAT[0:1, 512:512 + ncols]
            t1 = STAT[0:1, 1024:1024 + ncols]
            eng.tensor_scalar(mu, PS[5][0:1, 0:ncols], 1.0 / D, None, ALU.mult)
            eng.tensor_scalar(var, PS[6][0:1, 0:ncols], 1.0 / D, None, ALU.mult)
            eng.tensor_tensor(t1, mu, mu, ALU.mult)
            eng.tensor_sub(var, var, t1)
            return eng.tensor_scalar(var, var, EPS_LN, None, ALU.add)
        v = em("vector", fstat, inc=s_ve)
        wt("scalar", s_ve, v)
        sq = em("scalar", lambda eng, ncols=ncols:
                eng.activation(STAT[0:1, 512:512 + ncols], STAT[0:1, 512:512 + ncols],
                               AF.Sqrt), inc=s_sc)
        wt("vector", s_sc, sq)
        em("vector", lambda eng, ncols=ncols:
           eng.reciprocal(STAT[0:1, 512:512 + ncols], STAT[0:1, 512:512 + ncols]), inc=s_ve)

    def make_xn(xbase, ncols):
        wt("tensor", s_ve)
        em("tensor", lambda eng, ncols=ncols:
           eng.matmul(PS[5][:, 0:ncols], cs["on1f"][0:1, :], STAT[0:1, 0:ncols],
                      start=True, stop=True), inc=s_mm)
        m = em("tensor", lambda eng, ncols=ncols:
               eng.matmul(PS[6][:, 0:ncols], cs["on1f"][0:1, :], STAT[0:1, 512:512 + ncols],
                          start=True, stop=True), inc=s_mm)
        wt("vector", s_mm, m)
        em("vector", lambda eng, ncols=ncols:
           eng.tensor_copy(BC1[:, 0:ncols], PS[5][:, 0:ncols]), inc=s_ve)
        em("vector", lambda eng, ncols=ncols:
           eng.tensor_copy(BC2[:, 0:ncols], PS[6][:, 0:ncols]), inc=s_ve)
        for k in range(DK):
            def f(eng, k=k, xbase=xbase, ncols=ncols):
                xx = XT[:, k * TH + xbase:k * TH + xbase + ncols]
                xn = XN[:, k * 512:k * 512 + ncols]
                eng.tensor_sub(xn, xx, BC1[:, 0:ncols])
                return eng.tensor_mul(xn, xn, BC2[:, 0:ncols])
            em("vector", f, inc=s_ve)
        v = P.n(s_ve)
        wt("tensor", s_ve, v)
        wt("scalar", s_ve, v)

    def ffn(xbase, ncols, out_store=None):
        for cblk in range(CFF):
            bufA, bufB = WSL[cblk % 2], WSL2[cblk % 2]
            bA, bB = PS[1 + cblk % 2], PS[3 + cblk % 2]
            wt("scalar", s_ve)
            t = dm("scalar", bufA[:, :], cw1.ap()[cblk, :, :], s_wd)
            t = dm("scalar", bufB[:, :], cw2.ap()[cblk, :, :], s_wd)
            wt("tensor", s_wd, t)
            wt("tensor", s_sc)
            wt("tensor", s_ve)

            def mms(eng, bufA=bufA, bufB=bufB, bA=bA, bB=bB, ncols=ncols):
                for k in range(DK):
                    eng.matmul(bA[:, 0:ncols], bufA[:, k * 128:(k + 1) * 128],
                               XN[:, k * 512:k * 512 + ncols],
                               start=(k == 0), stop=(k == DK - 1))
                last = None
                for k in range(DK):
                    last = eng.matmul(bB[:, 0:ncols], bufB[:, k * 128:(k + 1) * 128],
                                      XN[:, k * 512:k * 512 + ncols],
                                      start=(k == 0), stop=(k == DK - 1))
                return last
            m = em("tensor", mms, inc=s_mm)
            wt("scalar", s_mm, m)
            wt("scalar", s_ve)
            sc = em("scalar", lambda eng, bA=bA, ncols=ncols:
                    eng.activation(SGT[:, 0:ncols], bA[:, 0:ncols], AF.Sigmoid), inc=s_sc)
            wt("vector", s_sc, sc)
            em("vector", lambda eng, bB=bB, cblk=cblk, ncols=ncols:
               eng.tensor_tensor(HS[:, cblk * 512:cblk * 512 + ncols], SGT[:, 0:ncols],
                                 bB[:, 0:ncols], ALU.mult), inc=s_ve)
        for dblk in range(DK):
            buf3 = W3S[dblk % 2]
            b3 = PS[1 + dblk % 2]
            wt("scalar", s_ve)
            t = dm("scalar", buf3[:, :], cw3.ap()[dblk, :, :], s_wd)
            wt("tensor", s_wd, t)
            wt("tensor", s_ve)

            def mm3(eng, buf3=buf3, b3=b3, ncols=ncols):
                last = None
                for k in range(CFF):
                    last = eng.matmul(b3[:, 0:ncols], buf3[:, k * 128:(k + 1) * 128],
                                      HS[:, k * 512:k * 512 + ncols],
                                      start=(k == 0), stop=(k == CFF - 1))
                return last
            m = em("tensor", mm3, inc=s_mm)
            wt("vector", s_mm, m)
            if out_store is None:
                def fr(eng, b3=b3, dblk=dblk, xbase=xbase, ncols=ncols):
                    xx = XT[:, dblk * TH + xbase:dblk * TH + xbase + ncols]
                    return eng.scalar_tensor_tensor(xx, b3[:, 0:ncols], 0.5, xx,
                                                    ALU.mult, ALU.add)
                em("vector", fr, inc=s_ve)
            else:
                def fr(eng, b3=b3, dblk=dblk, ncols=ncols, xbase=xbase):
                    xx = XT[:, dblk * TH + xbase:dblk * TH + xbase + ncols]
                    return eng.scalar_tensor_tensor(JUNK[:, 0:ncols], b3[:, 0:ncols], 0.5, xx,
                                                    ALU.mult, ALU.add)
                v = em("vector", fr, inc=s_ve)
                wt("sync", s_ve, v)
                dm("sync", out_store[dblk * 128:(dblk + 1) * 128,
                                     xbase - HALO:xbase - HALO + ncols], JUNK[:, 0:ncols], s_st)
                wt("vector", s_st)

    blocks = [(0, HALO)] + [(HALO + g * 512, min(512, T_local - g * 512)) for g in range(n_tg)]

    if mode == "A":
        wt("vector", s_gp)
        wt("tensor", s_gp)
        wt("scalar", s_gp)
        for (xb, n) in blocks:
            ln_stats(xb, n)
            make_xn(xb, n)
            ffn(xb, n)
        full_barrier("ffn1")
        for k in range(DK):
            dm("gpsimd", out["x1T_out"][k * 128:(k + 1) * 128, :],
               XT[:, k * TH:(k + 1) * TH], s_gp)

        emit_mixer_A(P, locals())
        wt("sync", s_st)
        wt("sync", s_gp)
        full_barrier("endA")
    else:
        emit_B(P, locals())
        wt("sync", s_st)
        full_barrier("endB")

    P.run()
    return nc


def emit_mixer_A(P, L):
    g = lambda n: L[n]
    em, wt, dm = P.em, P.wt, P.dma
    nc = g("nc")
    s_ld, s_wd, s_mm, s_ve, s_sc, s_gp, s_st = (g("s_ld"), g("s_wd"), g("s_mm"),
                                                g("s_ve"), g("s_sc"), g("s_gp"), g("s_st"))
    cs, PS, PSB0 = g("cs"), g("PS"), g("PSB0")
    XT, XN, JUNK = g("XT"), g("XN"), g("JUNK")
    TH, T_local, n_ch = g("TH"), g("T_local"), g("n_ch")
    ln_stats, make_xn = g("ln_stats"), g("make_xn")
    inp, out = g("inp"), g("out")
    cqkv, cab = g("cqkv"), g("cab")
    QKVC, QKVP, CONVW, BABB, WAB = g("QKVC"), g("QKVP"), g("CONVW"), g("BABB"), g("WAB")
    TRN, XS, PW_P, PW_T, EOT, QTT = g("TRN"), g("XS"), g("PW_P"), g("PW_T"), g("EOT"), g("QTT")
    RSOL, KTL, GT_C = g("RSOL"), g("KTL"), g("GT_C")
    GACC, HACC, GACCT = g("GACC"), g("HACC"), g("GACCT")
    DNAT, DTRN, EOX, EARG, TT1, TT2 = g("DNAT"), g("DTRN"), g("EOX"), g("EARG"), g("TT1"), g("TT2")
    ABR, ASIG, GAMS = g("ABR"), g("ASIG"), g("GAMS")
    NK2, NQ2, LNK, LNQ, INVK, INVQ = g("NK2"), g("NQ2"), g("LNK"), g("LNQ"), g("INVK"), g("INVQ")
    GPC, BGIK, RTIK, PACK, RROW = g("GPC"), g("BGIK"), g("RTIK"), g("PACK"), g("RROW")
    SPF1, SPF2 = g("SPF1"), g("SPF2")
    WSL = g("WSL")

    em("vector", lambda eng: eng.memset(HACC[:, :], 0.0), inc=s_ve)
    em("vector", lambda eng: eng.memset(QKVP[:, :], 0.0), inc=s_ve)
    for h in range(NH):
        em("vector", lambda eng, h=h: eng.tensor_copy(GACC[:, h * 128:(h + 1) * 128],
                                                      cs["idb"][:, :]), inc=s_ve)
        em("vector", lambda eng, h=h: eng.tensor_copy(GACCT[:, h * 128:(h + 1) * 128],
                                                      cs["idb"][:, :]), inc=s_ve)
    for k in range(16):
        dm("sync", WAB[:, k * 32:(k + 1) * 32], cab.ap()[k, :, 0:32], s_ld)
    wt("tensor", s_ld)
    wt("tensor", s_ve)
    m = em("tensor", lambda eng: eng.matmul(PS[7][:, 0:32], cs["on1f"][0:1, :],
                                            BABB[0:1, 0:32], start=True, stop=True), inc=s_mm)
    wt("vector", s_mm, m)
    em("vector", lambda eng: eng.tensor_copy(BABB[:, 0:32], PS[7][:, 0:32]), inc=s_ve)
    g("full_barrier")("mxinit")

    def qkv_block(xnoff, ncols, want_conv):
        for cb in range(48):
            buf = WSL[cb % 2]
            bk = PS[1 + cb % 2]
            wt("scalar", s_sc)
            wt("scalar", s_ve)
            t = dm("scalar", buf[:, :], cqkv.ap()[cb, :, :], s_wd)
            wt("tensor", s_wd, t)
            wt("tensor", s_sc)
            wt("tensor", s_ve)

            def mmq(eng, buf=buf, bk=bk, xnoff=xnoff, ncols=ncols):
                last = None
                for k in range(16):
                    last = eng.matmul(bk[:, 0:ncols], buf[:, k * 128:(k + 1) * 128],
                                      XN[:, k * 512 + xnoff:k * 512 + xnoff + ncols],
                                      start=(k == 0), stop=(k == 15))
                return last
            m = em("tensor", mmq, inc=s_mm)
            wt("vector", s_mm, m)
            wt("vector", s_sc)
            if want_conv:
                def fconv(eng, bk=bk, cb=cb):
                    o = JUNK[:, 0:128]
                    w3 = CONVW[:, cb * 4 + 3:cb * 4 + 4]
                    eng.tensor_scalar(o, bk[:, 0:128], w3, None, ALU.mult)
                    for j in range(3):
                        w = CONVW[:, cb * 4 + j:cb * 4 + j + 1]
                        sh = 3 - j
                        eng.scalar_tensor_tensor(o[:, sh:128], bk[:, 0:128 - sh], w,
                                                 o[:, sh:128], ALU.mult, ALU.add)
                        eng.scalar_tensor_tensor(o[:, 0:sh],
                                                 QKVP[:, cb * 4 + 4 - sh:cb * 4 + 4], w,
                                                 o[:, 0:sh], ALU.mult, ALU.add)
                    return eng.tensor_copy(QKVP[:, cb * 4 + 1:cb * 4 + 4], bk[:, 125:128])
                v = em("vector", fconv, inc=s_ve)
                wt("scalar", s_ve, v)
                em("scalar", lambda eng, cb=cb:
                   eng.activation(QKVC[:, cb * 128:(cb + 1) * 128], JUNK[:, 0:128], AF.Silu),
                   inc=s_sc)
            else:
                v = em("vector", lambda eng, bk=bk, cb=cb, ncols=ncols:
                       eng.tensor_copy(QKVP[:, cb * 4 + 1:cb * 4 + 4],
                                       bk[:, ncols - 3:ncols]), inc=s_ve)
                wt("scalar", s_ve, v)
                em("scalar", lambda eng: eng.copy(JUNK[0:1, 0:1], JUNK[0:1, 0:1]), inc=s_sc)

    ln_stats(0, HALO)
    make_xn(0, HALO)
    qkv_block(0, HALO, want_conv=False)

    for ch in range(n_ch):
        tg = (ch * C) // 512
        if (ch * C) % 512 == 0:
            xb = HALO + tg * 512
            n = min(512, T_local - tg * 512)
            ln_stats(xb, n)
            make_xn(xb, n)
        xnoff = (ch * C) % 512
        qkv_block(xnoff, C, want_conv=True)

        wt("tensor", s_sc)
        def mmab(eng, xnoff=xnoff):
            last = None
            for k in range(16):
                last = eng.matmul(PS[3][0:32, 0:C], WAB[:, k * 32:(k + 1) * 32],
                                  XN[:, k * 512 + xnoff:k * 512 + xnoff + C],
                                  start=(k == 0), stop=(k == 15))
            return last
        m = em("tensor", mmab, inc=s_mm)
        wt("vector", s_mm, m)
        v = em("vector", lambda eng: eng.tensor_copy(ABR[0:32, 0:C], PS[3][0:32, 0:C]), inc=s_ve)
        wt("tensor", s_ve, v)
        m = em("tensor", lambda eng: eng.transpose(PS[4][:, 0:32], ABR[0:32, 0:C],
                                                   cs["idf"][0:32, 0:32]), inc=s_mm)
        wt("vector", s_mm, m)
        wt("vector", s_sc)
        v = em("vector", lambda eng: eng.tensor_add(ASIG[:, 0:32], PS[4][:, 0:32],
                                                    BABB[:, 0:32]), inc=s_ve)
        wt("scalar", s_ve, v)
        sc = em("scalar", lambda eng: eng.activation(ASIG[:, 0:32], ASIG[:, 0:32],
                                                     AF.Sigmoid), inc=s_sc)
        sc = em("scalar", lambda eng: eng.activation(GAMS[:, 0:NH], ASIG[:, 0:NH], AF.Ln),
                inc=s_sc)
        wt("tensor", s_sc, sc)
        em("tensor", lambda eng: eng.matmul(PS[3][:, 0:NH], cs["trii"][:, :], GAMS[:, 0:NH],
                                            start=True, stop=True), inc=s_mm)
        m = em("tensor", lambda eng: eng.matmul(PS[4][:, 0:NH], cs["trir"][:, :], GAMS[:, 0:NH],
                                                start=True, stop=True), inc=s_mm)
        wt("vector", s_mm, m)
        wt("vector", s_sc)

        def fg(eng):
            eng.tensor_copy(GAMS[:, NH:2 * NH], PS[3][:, 0:NH])
            eng.tensor_copy(GAMS[:, 2 * NH:3 * NH], PS[4][:, 0:NH])
            return eng.tensor_add(GAMS[:, 3 * NH:4 * NH], GAMS[:, NH:2 * NH],
                                  GAMS[:, 2 * NH:3 * NH])
        v = em("vector", fg, inc=s_ve)
        wt("scalar", s_ve, v)
        em("scalar", lambda eng: eng.activation(GAMS[:, 4 * NH:5 * NH], GAMS[:, NH:2 * NH],
                                                AF.Exp), inc=s_sc)
        em("scalar", lambda eng: eng.activation(GAMS[:, 5 * NH:6 * NH], GAMS[:, 3 * NH:4 * NH],
                                                AF.Exp), inc=s_sc)
        sc = em("scalar", lambda eng: eng.activation(GAMS[:, 6 * NH:7 * NH],
                                                     GAMS[:, 2 * NH:3 * NH], AF.Exp), inc=s_sc)
        wt("tensor", s_sc, sc)
        wt("vector", s_sc, sc)

        for h in range(NH):
            kn = QKVC[:, (NH + h) * 128:(NH + h + 1) * 128]
            vn = QKVC[:, (2 * NH + h) * 128:(2 * NH + h + 1) * 128]
            qn = QKVC[:, h * 128:(h + 1) * 128]
            m = em("tensor", lambda eng, kn=kn: eng.transpose(PSB0[:, 0:128], kn,
                                                              cs["idb"][:, :]), inc=s_mm)
            m = em("tensor", lambda eng, vn=vn: eng.transpose(PSB0[:, 128:256], vn,
                                                              cs["idb"][:, :]), inc=s_mm)
            m = em("tensor", lambda eng, qn=qn: eng.transpose(PSB0[:, 256:384], qn,
                                                              cs["idb"][:, :]), inc=s_mm)
            wt("vector", s_mm, m)

            def fev(eng):
                eng.tensor_copy(TRN[:, 0:128], PSB0[:, 0:128])
                eng.tensor_copy(TRN[:, 128:256], PSB0[:, 128:256])
                return eng.tensor_copy(TRN[:, 256:384], PSB0[:, 256:384])
            v = em("vector", fev, inc=s_ve)
            wt("scalar", s_ve, v)
            wt("tensor", s_ve, v)
            sc = em("scalar", lambda eng, h=h:
                    eng.activation(JUNK[:, 0:128], TRN[:, 0:128], AF.Square,
                                   accum_out=NK2[:, h:h + 1]), inc=s_sc)
            sc = em("scalar", lambda eng, h=h:
                    eng.activation(JUNK[:, 128:256], TRN[:, 256:384], AF.Square,
                                   accum_out=NQ2[:, h:h + 1]), inc=s_sc)
            wt("vector", s_sc, sc)
            emit_head(P, L, ch, h)

        wt("sync", s_ve)
        dm("sync", out["spill_pt"][ch * NH * 128:(ch + 1) * NH * 128, :]
           .rearrange("(h p) t -> p h t", p=128),
           SPF1[:, :].rearrange("p (h t) -> p h t", h=NH), s_st)
        dm("sync", out["spill_op"][ch * NH * 128:(ch + 1) * NH * 128, :]
           .rearrange("(h p) t -> p h t", p=128),
           SPF2[:, :].rearrange("p (h t) -> p h t", h=NH), s_st)
        wt("vector", s_st)

    for h in range(NH):
        v = em("vector", lambda eng, h=h: eng.tensor_copy(SPF1[:, h * 128:(h + 1) * 128],
                                                          GACCT[:, h * 128:(h + 1) * 128]), inc=s_ve)
        v = em("vector", lambda eng, h=h: eng.tensor_copy(SPF2[:, h * 128:(h + 1) * 128],
                                                          HACC[:, h * 128:(h + 1) * 128]), inc=s_ve)
    wt("sync", s_ve)
    dm("sync", out["ght_out"].rearrange("(h p) t -> p h t", p=128),
       SPF1[:, :].rearrange("p (h t) -> p h t", h=NH), s_st)
    dm("sync", out["h_out"].rearrange("(h p) t -> p h t", p=128),
       SPF2[:, :].rearrange("p (h t) -> p h t", h=NH), s_st)


def emit_head(P, L, ch, h):
    g = lambda n: L[n]
    em, wt = P.em, P.wt
    s_mm, s_ve, s_sc = g("s_mm"), g("s_ve"), g("s_sc")
    cs, PS, PSB0 = g("cs"), g("PS"), g("PSB0")
    QKVC, TRN, XS, PW_P, PW_T = g("QKVC"), g("TRN"), g("XS"), g("PW_P"), g("PW_T")
    EOT, QTT, RSOL, KTL, GT_C = g("EOT"), g("QTT"), g("RSOL"), g("KTL"), g("GT_C")
    GACC, HACC, GACCT = g("GACC"), g("HACC"), g("GACCT")
    DNAT, DTRN, EOX, EARG, TT1, TT2 = g("DNAT"), g("DTRN"), g("EOX"), g("EARG"), g("TT1"), g("TT2")
    ASIG, GAMS = g("ASIG"), g("GAMS")
    NK2, NQ2, LNK, LNQ, INVK, INVQ = g("NK2"), g("NQ2"), g("LNK"), g("LNQ"), g("INVK"), g("INVQ")
    GPC, BGIK, RTIK, PACK, RROW = g("GPC"), g("BGIK"), g("RTIK"), g("PACK"), g("RROW")
    SPF1, SPF2, JUNK = g("SPF1"), g("SPF2"), g("JUNK")

    Kt, Vt, Qt = TRN[:, 0:128], TRN[:, 128:256], TRN[:, 256:384]
    kn = QKVC[:, (g("NH") if False else NH + h) * 128:(NH + h + 1) * 128]
    qn = QKVC[:, h * 128:(h + 1) * 128]
    gcol = GAMS[:, NH + h:NH + h + 1]
    gamc = GAMS[:, 4 * NH + h:4 * NH + h + 1]
    gamCc = GAMS[:, 5 * NH + h:5 * NH + h + 1]
    ratc = GAMS[:, 6 * NH + h:6 * NH + h + 1]
    beta = ASIG[:, NH + h:NH + h + 1]

    def fc1(eng, h=h):
        eng.tensor_copy(LNK[:, h:h + 1], NK2[:, h:h + 1])
        return eng.tensor_copy(LNQ[:, h:h + 1], NQ2[:, h:h + 1])
    v = em("vector", fc1, inc=s_ve)
    wt("scalar", s_ve, v)
    sc = em("scalar", lambda eng, h=h: eng.activation(LNK[:, h:h + 1], LNK[:, h:h + 1], AF.Ln),
            inc=s_sc)
    sc = em("scalar", lambda eng, h=h: eng.activation(LNQ[:, h:h + 1], LNQ[:, h:h + 1], AF.Ln),
            inc=s_sc)
    sc = em("scalar", lambda eng, h=h: eng.activation(NK2[:, h:h + 1], NK2[:, h:h + 1], AF.Sqrt),
            inc=s_sc)
    sc = em("scalar", lambda eng, h=h: eng.activation(NQ2[:, h:h + 1], NQ2[:, h:h + 1], AF.Sqrt),
            inc=s_sc)
    sc = em("scalar", lambda eng, h=h: eng.activation(GPC[:, h:h + 1], ASIG[:, NH + h:NH + h + 1],
                                                      AF.Ln), inc=s_sc)
    wt("vector", s_sc, sc)

    def fc2(eng, h=h):
        eng.reciprocal(INVK[:, h:h + 1], NK2[:, h:h + 1])
        eng.tensor_scalar(INVK[:, h:h + 1], INVK[:, h:h + 1], INVCAP, None, ALU.min)
        eng.reciprocal(INVQ[:, h:h + 1], NQ2[:, h:h + 1])
        eng.tensor_scalar(INVQ[:, h:h + 1], INVQ[:, h:h + 1], INVCAP, None, ALU.min)
        eng.scalar_tensor_tensor(PACK[:, h:h + 1], LNK[:, h:h + 1], 0.5, gcol, ALU.mult, ALU.add)
        eng.tensor_add(GPC[:, h:h + 1], GPC[:, h:h + 1], gcol)
        eng.scalar_tensor_tensor(GPC[:, h:h + 1], LNK[:, h:h + 1], -0.5, GPC[:, h:h + 1],
                                 ALU.mult, ALU.add)
        eng.scalar_tensor_tensor(PACK[:, NH + h:NH + h + 1], LNQ[:, h:h + 1], -0.5, gcol,
                                 ALU.mult, ALU.add)
        eng.tensor_mul(PACK[:, 2 * NH + h:2 * NH + h + 1], gamc, INVQ[:, h:h + 1])
        eng.tensor_mul(BGIK[:, h:h + 1], beta, gamc)
        eng.tensor_mul(BGIK[:, h:h + 1], BGIK[:, h:h + 1], INVK[:, h:h + 1])
        return eng.tensor_mul(RTIK[:, h:h + 1], ratc, INVK[:, h:h + 1])
    v = em("vector", fc2, inc=s_ve)
    wt("tensor", s_ve, v)

    def bcast(col_ap, ps_out):
        wt("tensor", s_ve)
        m = em("tensor", lambda eng, col_ap=col_ap: eng.matmul(PS[6][0:1, 0:128], col_ap,
                                                               cs["idf"][:, :],
                                                               start=True, stop=True), inc=s_mm)
        wt("vector", s_mm, m)
        v = em("vector", lambda eng: eng.tensor_copy(RROW[0:1, 0:128], PS[6][0:1, 0:128]),
               inc=s_ve)
        wt("tensor", s_ve, v)
        return em("tensor", lambda eng, ps_out=ps_out:
                  eng.matmul(ps_out, cs["on1f"][0:1, :], RROW[0:1, 0:128],
                             start=True, stop=True), inc=s_mm)

    m = bcast(PACK[:, h:h + 1], PS[3][:, 0:128])
    wt("vector", s_mm, m)
    v = em("vector", lambda eng: eng.tensor_scalar(EARG[:, :], PS[3][:, 0:128],
                                                   GPC[:, h:h + 1], -CLAMP,
                                                   ALU.subtract, ALU.max), inc=s_ve)
    wt("scalar", s_ve, v)
    wt("tensor", s_ve, v)
    sc = em("scalar", lambda eng: eng.activation(DNAT[:, :], EARG[:, :], AF.Exp, scale=-1.0),
            inc=s_sc)
    m = em("tensor", lambda eng: eng.transpose(PS[4][:, 0:128], EARG[:, :], cs["idf"][:, :]),
           inc=s_mm)
    wt("scalar", s_mm, m)
    sc = em("scalar", lambda eng: eng.activation(DTRN[:, :], PS[4][:, 0:128], AF.Exp,
                                                 scale=-1.0), inc=s_sc)
    m = bcast(PACK[:, NH + h:NH + h + 1], PS[3][:, 0:128])
    wt("vector", s_mm, m)
    wt("vector", s_sc)
    v = em("vector", lambda eng, h=h: eng.tensor_scalar(EARG[:, :], PS[3][:, 0:128],
                                                        PACK[:, h:h + 1], CLAMP,
                                                        ALU.subtract, ALU.min), inc=s_ve)
    wt("scalar", s_ve, v)
    sc = em("scalar", lambda eng: eng.activation(EOX[:, :], EARG[:, :], AF.Exp), inc=s_sc)

    wt("tensor", s_sc)
    em("tensor", lambda eng, kn=kn: eng.matmul(PS[5][:, 0:128], kn, kn, start=True, stop=True),
       inc=s_mm)
    m = em("tensor", lambda eng, kn=kn, qn=qn: eng.matmul(PS[5][:, 128:256], kn, qn,
                                                          start=True, stop=True), inc=s_mm)
    wt("vector", s_mm, m)

    def fseed(eng):
        eng.tensor_tensor(TT1[:, :], DNAT[:, :], cs["mlown"][:, :], ALU.mult)
        eng.tensor_tensor(PW_P[:, :], TT1[:, :], PS[5][:, 0:128], ALU.mult)
        eng.tensor_tensor(TT1[:, :], DTRN[:, :], cs["mupn"][:, :], ALU.mult)
        eng.tensor_tensor(PW_T[:, :], TT1[:, :], PS[5][:, 0:128], ALU.mult)
        eng.tensor_tensor(TT1[:, :], EOX[:, :], cs["mupi"][:, :], ALU.mult)
        eng.tensor_tensor(EOT[:, :], TT1[:, :], PS[5][:, 128:256], ALU.mult)
        eng.tensor_scalar(RSOL[:, 0:128], Vt, ASIG[:, NH + h:NH + h + 1], None, ALU.mult)
        eng.tensor_scalar(RSOL[:, 128:256], Kt, BGIK[:, h:h + 1], None, ALU.mult)
        return eng.tensor_scalar(KTL[:, :], Kt, RTIK[:, h:h + 1], None, ALU.mult)
    wt("vector", s_sc)
    v = em("vector", fseed, inc=s_ve)
    wt("tensor", s_ve, v)

    src = RSOL
    for j in range(7):
        wt("tensor", s_ve)
        em("tensor", lambda eng, src=src: eng.matmul(PS[7][:, 0:256], cs["idb"][:, :],
                                                     src[:, :], start=True, stop=False),
           inc=s_mm)
        m = em("tensor", lambda eng, src=src: eng.matmul(PS[7][:, 0:256], PW_T[:, :],
                                                         src[:, :], start=False, stop=True),
               inc=s_mm)
        if j < 6:
            em("tensor", lambda eng: eng.matmul(PS[6][:, 0:128], PW_T[:, :], PW_P[:, :],
                                                start=True, stop=True), inc=s_mm)
            m = em("tensor", lambda eng: eng.matmul(PS[6][:, 128:256], PW_P[:, :], PW_T[:, :],
                                                    start=True, stop=True), inc=s_mm)
        wt("vector", s_mm, m)
        v = em("vector", lambda eng: eng.tensor_copy(XS[:, :], PS[7][:, 0:256]), inc=s_ve)
        if j < 6:
            v = em("vector", lambda eng: eng.tensor_copy(PW_P[:, :], PS[6][:, 0:128]), inc=s_ve)
            v = em("vector", lambda eng: eng.tensor_copy(PW_T[:, :], PS[6][:, 128:256]), inc=s_ve)
        wt("tensor", s_ve, v)
        src = XS

    m = bcast(PACK[:, 2 * NH + h:2 * NH + h + 1], PS[3][:, 0:128])
    m2 = em("tensor", lambda eng: eng.matmul(PS[4][:, 0:128], XS[:, 128:256], EOT[:, :],
                                             start=True, stop=True), inc=s_mm)
    wt("vector", s_mm, m2)

    def fqt(eng, qn=qn):
        eng.tensor_tensor(TT1[:, :], qn, PS[3][:, 0:128], ALU.mult)
        return eng.tensor_tensor(QTT[:, :], TT1[:, :], PS[4][:, 0:128], ALU.subtract)
    v = em("vector", fqt, inc=s_ve)
    wt("tensor", s_ve, v)

    hA = HACC[:, h * 128:(h + 1) * 128]
    gA = GACC[:, h * 128:(h + 1) * 128]
    gAT = GACCT[:, h * 128:(h + 1) * 128]
    em("tensor", lambda eng: eng.matmul(PS[3][:, 0:128], XS[:, 0:128], EOT[:, :],
                                        start=True, stop=False), inc=s_mm)
    em("tensor", lambda eng, hA=hA: eng.matmul(PS[3][:, 0:128], hA, QTT[:, :],
                                               start=False, stop=True), inc=s_mm)
    m = em("tensor", lambda eng, gA=gA: eng.matmul(PS[4][:, 0:128], gA, QTT[:, :],
                                                   start=True, stop=True), inc=s_mm)
    wt("vector", s_mm, m)
    v = em("vector", lambda eng, h=h: eng.tensor_copy(SPF2[:, h * 128:(h + 1) * 128],
                                                      PS[3][:, 0:128]), inc=s_ve)
    v = em("vector", lambda eng, h=h: eng.tensor_copy(SPF1[:, h * 128:(h + 1) * 128],
                                                      PS[4][:, 0:128]), inc=s_ve)
    wt("tensor", s_ve, v)

    em("tensor", lambda eng: eng.matmul(PS[3][:, 0:128], KTL[:, :], XS[:, 0:128],
                                        start=True, stop=False), inc=s_mm)
    m = em("tensor", lambda eng: eng.matmul(PS[4][:, 0:128], XS[:, 128:256], KTL[:, :],
                                            start=True, stop=True), inc=s_mm)
    wt("vector", s_mm, m)

    def fgt(eng):
        eng.tensor_scalar(TT1[:, :], PS[4][:, 0:128], -1.0, None, ALU.mult)
        eng.tensor_scalar(TT2[:, :], cs["idf"][:, :], gamCc, None, ALU.mult)
        return eng.tensor_tensor(GT_C[:, :], TT1[:, :], TT2[:, :], ALU.add)
    v = em("vector", fgt, inc=s_ve)
    wt("tensor", s_ve, v)
    em("tensor", lambda eng, hA=hA: eng.matmul(PS[3][:, 0:128], GT_C[:, :], hA,
                                               start=False, stop=True), inc=s_mm)
    em("tensor", lambda eng, gA=gA: eng.matmul(PS[5][:, 0:128], GT_C[:, :], gA,
                                               start=True, stop=True), inc=s_mm)
    m = em("tensor", lambda eng, gA=gA: eng.matmul(PS[5][:, 128:256], gA, GT_C[:, :],
                                                   start=True, stop=True), inc=s_mm)
    wt("vector", s_mm, m)
    v = em("vector", lambda eng, hA=hA: eng.tensor_copy(hA, PS[3][:, 0:128]), inc=s_ve)
    v = em("vector", lambda eng, gA=gA: eng.tensor_copy(gA, PS[5][:, 0:128]), inc=s_ve)
    v = em("vector", lambda eng, gAT=gAT: eng.tensor_copy(gAT, PS[5][:, 128:256]), inc=s_ve)
    wt("tensor", s_ve, v)
    wt("scalar", s_ve, v)


def emit_B(P, L):
    g = lambda n: L[n]
    em, wt, dm = P.em, P.wt, P.dma
    nc = g("nc")
    s_ld, s_wd, s_mm, s_ve, s_sc, s_gp, s_st = (g("s_ld"), g("s_wd"), g("s_mm"),
                                                g("s_ve"), g("s_sc"), g("s_gp"), g("s_st"))
    cs, PS = g("cs"), g("PS")
    XT, XN, JUNK = g("XT"), g("XN"), g("JUNK")
    TH, T_local, n_tg, n_ch = g("TH"), g("T_local"), g("n_tg"), g("n_ch")
    ln_stats, make_xn, ffn = g("ln_stats"), g("make_xn"), g("ffn")
    inp, out = g("inp"), g("out")
    cwg, cwo = g("cwg"), g("cwo")
    ZIN, GHS, GROW, OFIX = g("ZIN"), g("GHS"), g("GROW"), g("OFIX")
    SPB1, SPB2, SPF1, SPF2 = g("SPB1"), g("SPB2"), g("SPF1"), g("SPF2")
    WSL = g("WSL")
    blocks = g("blocks")

    wt("vector", s_gp)
    wt("tensor", s_gp)
    wt("scalar", s_gp)

    em("vector", lambda eng: eng.memset(ZIN[:, :], 0.0), inc=s_ve)
    for j in range(NCORES - 1):
        base = j * 2 * NH * 128
        for h in range(NH):
            t = dm("gpsimd", GHS[:, 0:128],
                   inp["ghseq"][base + h * 128:base + (h + 1) * 128, :], s_gp)
            t = dm("gpsimd", GHS[:, 128:256],
                   inp["ghseq"][base + NH * 128 + h * 128:base + NH * 128 + (h + 1) * 128, :],
                   s_gp)
            wt("tensor", s_gp, t)
            wt("tensor", s_ve)
            em("tensor", lambda eng, h=h: eng.matmul(PS[1][:, 0:128], GHS[:, 0:128],
                                                     ZIN[:, h * 128:(h + 1) * 128],
                                                     start=True, stop=False), inc=s_mm)
            m = em("tensor", lambda eng: eng.matmul(PS[1][:, 0:128], cs["idb"][:, :],
                                                    GHS[:, 128:256], start=False, stop=True),
                   inc=s_mm)
            wt("vector", s_mm, m)
            v = em("vector", lambda eng, h=h: eng.tensor_copy(ZIN[:, h * 128:(h + 1) * 128],
                                                              PS[1][:, 0:128]), inc=s_ve)
            wt("gpsimd", s_ve, v)
    g("full_barrier")("zin")

    for tg in range(n_tg):
        g("full_barrier")("tg%d" % tg)
        xb = HALO + tg * 512
        n = min(512, T_local - tg * 512)
        ln_stats(xb, n)
        make_xn(xb, n)
        for dblk in range(DK):
            buf = WSL[dblk % 2]
            wt("scalar", s_sc)
            wt("scalar", s_ve)
            t = dm("scalar", buf[:, :], cwg.ap()[dblk, :, :], s_wd)
            wt("tensor", s_wd, t)
            wt("tensor", s_sc)

            def mmg(eng, buf=buf, n=n):
                last = None
                for k in range(DK):
                    last = eng.matmul(PS[1][:, 0:n], buf[:, k * 128:(k + 1) * 128],
                                      XN[:, k * 512:k * 512 + n],
                                      start=(k == 0), stop=(k == DK - 1))
                return last
            m = em("tensor", mmg, inc=s_mm)
            wt("scalar", s_mm, m)
            em("scalar", lambda eng, dblk=dblk, n=n:
               eng.activation(GROW[:, dblk * 512:dblk * 512 + n], PS[1][:, 0:n], AF.Silu),
               inc=s_sc)
        sc = P.n(s_sc)
        wt("tensor", s_sc, sc)
        wt("vector", s_sc, sc)
        for cc in range(4):
            ch = tg * 4 + cc
            if ch >= n_ch:
                break
            t = dm("gpsimd", SPB1[:, :].rearrange("p (h t) -> p h t", h=NH),
                   inp["spill_pt"][ch * NH * 128:(ch + 1) * NH * 128, :]
                   .rearrange("(h p) t -> p h t", p=128), s_gp)
            t = dm("gpsimd", SPB2[:, :].rearrange("p (h t) -> p h t", h=NH),
                   inp["spill_op"][ch * NH * 128:(ch + 1) * NH * 128, :]
                   .rearrange("(h p) t -> p h t", p=128), s_gp)
            wt("tensor", s_gp, t)
            for h in range(NH):
                em("tensor", lambda eng, h=h:
                   eng.matmul(PS[2][:, 0:128], ZIN[:, h * 128:(h + 1) * 128],
                              SPB1[:, h * 128:(h + 1) * 128], start=True, stop=False), inc=s_mm)
                m = em("tensor", lambda eng, h=h:
                       eng.matmul(PS[2][:, 0:128], cs["idb"][:, :],
                                  SPB2[:, h * 128:(h + 1) * 128], start=False, stop=True),
                       inc=s_mm)
                wt("vector", s_mm, m)
                v = em("vector", lambda eng, h=h, cc=cc:
                       eng.tensor_tensor(OFIX[:, h * 512 + cc * 128:h * 512 + (cc + 1) * 128],
                                         PS[2][:, 0:128],
                                         GROW[:, h * 512 + cc * 128:h * 512 + (cc + 1) * 128],
                                         ALU.mult), inc=s_ve)
                wt("tensor", s_ve, v)
            wt("gpsimd", s_ve)
        for dblk in range(DK):
            buf = WSL[dblk % 2]
            wt("scalar", s_ve)
            t = dm("scalar", buf[:, :], cwo.ap()[dblk, :, :], s_wd)
            wt("tensor", s_wd, t)

            def mmo(eng, buf=buf, n=n):
                last = None
                for k in range(DK):
                    last = eng.matmul(PS[1][:, 0:n], buf[:, k * 128:(k + 1) * 128],
                                      OFIX[:, k * 512:k * 512 + n],
                                      start=(k == 0), stop=(k == DK - 1))
                return last
            m = em("tensor", mmo, inc=s_mm)
            wt("vector", s_mm, m)

            def fr(eng, dblk=dblk, xb=xb, n=n):
                xx = XT[:, dblk * TH + xb:dblk * TH + xb + n]
                return eng.tensor_add(xx, xx, PS[1][:, 0:n])
            v = em("vector", fr, inc=s_ve)
            wt("tensor", s_ve, v)
        g("full_barrier")("wo_tg%d" % tg)
        ln_stats(xb, n)
        make_xn(xb, n)
        ffn(xb, n, out_store=out["yT"])


_T_OVERRIDE = None
_NCORES_OVERRIDE = None


def kernel(L0_attn_x, L0_attn_W_qkv, L0_attn_conv_w, L0_attn_W_ab, L0_attn_b_ab,
           L0_attn_W_g, L0_attn_W_o, L0_ffn_w1, L0_ffn_w2, L0_ffn_w3):
    ncores = _NCORES_OVERRIDE or NCORES
    T = L0_attn_x.shape[1]
    Tl = _T_OVERRIDE or (T // ncores)
    x0T = np.ascontiguousarray(np.asarray(L0_attn_x)[0].T)
    w1T = np.ascontiguousarray(np.asarray(L0_ffn_w1).T)
    w2T = np.ascontiguousarray(np.asarray(L0_ffn_w2).T)
    w3T = np.ascontiguousarray(np.asarray(L0_ffn_w3).T)
    wqkvT = np.ascontiguousarray(np.asarray(L0_attn_W_qkv).T)
    wabT = np.ascontiguousarray(np.asarray(L0_attn_W_ab).T)
    wgT = np.ascontiguousarray(np.asarray(L0_attn_W_g).T)
    woT = np.ascontiguousarray(np.asarray(L0_attn_W_o).T)
    convw = np.ascontiguousarray(np.asarray(L0_attn_conv_w))
    bab = np.ascontiguousarray(np.asarray(L0_attn_b_ab)[None, :]).astype(np.float32)

    zh = np.zeros((D, HALO), dtype=np.float32)
    in_a = []
    for c in range(ncores):
        lo = c * Tl
        if lo >= HALO:
            xs = x0T[:, lo - HALO:lo + Tl]
        else:
            xs = np.concatenate([zh[:, :HALO - lo], x0T[:, 0:lo + Tl]], axis=1)
        in_a.append(dict(xT=np.ascontiguousarray(xs), w1T=w1T, w2T=w2T, w3T=w3T,
                         wqkvT=wqkvT, wabT=wabT, convw=convw, bab=bab))

    nc_a = build_program("A", Tl)
    res_a = run_bass_kernel_spmd(nc_a, in_a, core_ids=list(range(ncores)))

    ident = np.tile(np.eye(128, dtype=np.float32), (NH, 1))
    zero = np.zeros((NH * 128, 128), dtype=np.float32)
    in_b = []
    for c in range(ncores):
        seq = []
        for j in range(NCORES - 1):
            if j < c:
                seq.append(res_a.results[j]["ght_out"])
                seq.append(res_a.results[j]["h_out"])
            else:
                seq.append(ident)
                seq.append(zero)
        in_b.append(dict(x1T=res_a.results[c]["x1T_out"], w1T=w1T, w2T=w2T, w3T=w3T,
                         wgT=wgT, woT=woT,
                         spill_pt=res_a.results[c]["spill_pt"],
                         spill_op=res_a.results[c]["spill_op"],
                         ghseq=np.ascontiguousarray(np.concatenate(seq, axis=0))))

    nc_b = build_program("B", Tl)
    res_b = run_bass_kernel_spmd(nc_b, in_b, core_ids=list(range(ncores)))

    outv = np.concatenate([res_b.results[c]["yT"].T for c in range(ncores)], axis=0)
    return outv[None, :, :].astype(np.float32)


# version 03
# speedup vs baseline: 1.1680x; 1.1680x over previous
import numpy as np
import ml_dtypes
import concourse.bass as bass
import concourse.mybir as mybir
from concourse.bass_utils import run_bass_kernel_spmd

F32 = mybir.dt.float32
BF16 = mybir.dt.bfloat16
I32 = mybir.dt.int32
AF = mybir.ActivationFunctionType
ALU = mybir.AluOpType
AXX = mybir.AxisListType.X
BF = ml_dtypes.bfloat16

D = 2048
DK = 16
HFF = 5632
CFF = 44
NH = 16
QKV = 6144
CQK = 48
HALO = 64
C = 128
EPS_LN = 1e-5
EPS_Q = 1e-8
NCORES = 8
CLAMP = 80.0
INVCAP = 1e12



class Em:

    ENGINES = ("sync", "tensor", "vector", "scalar", "gpsimd")

    CHAINED = ("vector", "scalar", "gpsimd")
    CHAIN_OPS = frozenset("""tensor_copy tensor_tensor tensor_add tensor_sub
        tensor_mul tensor_max tensor_scalar tensor_single_scalar
        tensor_scalar_add tensor_scalar_sub tensor_scalar_mul tensor_scalar_max
        tensor_scalar_min tensor_relu tensor_reduce reduce_max reduce_sum
        scalar_tensor_tensor tensor_tensor_scan memset memzero reciprocal
        activation copy add mul sqrt square sign select bn_stats bn_aggr
        transpose pool pool_avg pool_max stream_shuffle""".split())

    class _MockInst:
        def then_inc(self, *a, **kw):
            return self

    class _CountEng:

        def __init__(self):
            self.n = 0

        def __getattr__(self, nm):
            if nm in Em.CHAIN_OPS:
                def c(*a, **kw):
                    self.n += 1
                    return Em._MockInst()
                return c
            return lambda *a, **kw: Em._MockInst()

    class _ChainEng:

        def __init__(self, eng, sem_h):
            self._e = eng
            self._h = sem_h
            self._n = 0

        def __getattr__(self, nm):
            attr = getattr(self._e, nm)
            if nm in Em.CHAIN_OPS:
                def call(*a, _attr=attr, **kw):
                    if self._n:
                        self._e.wait_ge(self._h, self._n)
                    inst = _attr(*a, **kw)
                    inst.then_inc(self._h, 1)
                    self._n += 1
                    return inst
                return call
            return attr

    def __init__(self, nc):
        self.nc = nc
        self.q = {e: [] for e in self.ENGINES}
        self._s = {}
        self.own = {}

    def sem(self, name):
        if name not in self._s:
            self._s[name] = [self.nc.alloc_semaphore(name), 0]
        return name

    def n(self, s):
        return self._s[s][1]

    def em(self, engine, fn, inc=None, k=1):
        if engine in self.CHAINED:
            mock = Em._CountEng()
            fn(mock)
            if mock.n > 0:
                own = self.own[engine]
                assert inc is None or inc == own, (engine, inc)
                self._s[own][1] += mock.n
                self.q[engine].append(fn)
                return self._s[own][1]
        if inc is None:
            self.q[engine].append(fn)
            return None
        h = self._s[inc][0]
        self._s[inc][1] += k

        def f2(eng, fn=fn, h=h, k=k):
            fn(eng).then_inc(h, k)
        self.q[engine].append(f2)
        return self._s[inc][1]

    def wt(self, engine, sem, val=None):
        v = self.n(sem) if val is None else val
        h = self._s[sem][0]
        self.q[engine].append(lambda eng, h=h, v=v: eng.wait_ge(h, v))

    def dma(self, engine, out, in_, sem, **kw):
        h = self._s[sem][0]
        self._s[sem][1] += 16

        def f(eng, out=out, in_=in_, h=h, kw=kw):
            eng.dma_start(out=out, in_=in_, **kw).then_inc(h, 16)
        self.q[engine].append(f)
        return self._s[sem][1]

    def barrier(self, tag):
        b = self.sem("bar_" + tag)
        for e in self.ENGINES:
            self.em(e, lambda eng: eng.nop(), inc=b)
        tgt = self.n(b)
        for e in self.ENGINES:
            self.wt(e, b, tgt)

    def run(self):
        with self.nc.Block() as block:
            for e in self.ENGINES:
                fns = self.q[e]
                own = self.own.get(e)

                def body(eng, fns=fns, own=own):
                    if own is not None:
                        eng = Em._ChainEng(eng, self._s[own][0])
                    for fn in fns:
                        fn(eng)
                getattr(block, e)(body)



def build_program(mode, T_local):
    nc = bass.Bass("TRN2", target_bir_lowering=False, debug=False)
    TH = T_local + HALO
    n_tg = (T_local + 511) // 512
    n_ch = T_local // C

    inp = {}

    def din(nm, shp):
        inp[nm] = nc.dram_tensor(nm, shp, F32, kind="ExternalInput").ap()
        return inp[nm]

    if mode == "A":
        din("xT", [D, TH])
        din("wqkvT", [D, QKV]); din("wabT", [D, 32])
        din("convw", [QKV, 4]); din("bab", [1, 32])
    else:
        din("x1T", [D, TH])
        din("wgT", [D, D]); din("woT", [D, D])
        din("spill_pt", [n_ch * NH * 128, 128])
        din("spill_op", [n_ch * NH * 128, 128])
        din("ghseq", [(NCORES - 1) * 2 * NH * 128, 128])
    din("w1T", [D, HFF]); din("w2T", [D, HFF]); din("w3T", [HFF, D])

    out = {}
    if mode == "A":
        out["x1T_out"] = nc.dram_tensor("x1T_out", [D, TH], F32, kind="ExternalOutput").ap()
        out["spill_pt"] = nc.dram_tensor("spill_pt", [n_ch * NH * 128, 128], F32, kind="ExternalOutput").ap()
        out["spill_op"] = nc.dram_tensor("spill_op", [n_ch * NH * 128, 128], F32, kind="ExternalOutput").ap()
        out["ght_out"] = nc.dram_tensor("ght_out", [NH * 128, 128], F32, kind="ExternalOutput").ap()
        out["h_out"] = nc.dram_tensor("h_out", [NH * 128, 128], F32, kind="ExternalOutput").ap()
    else:
        out["yT"] = nc.dram_tensor("yT", [D, T_local], F32, kind="ExternalOutput").ap()

    cw1 = nc.dram_tensor("cw1", [CFF, 128, DK * 128], BF16, kind="Internal")
    cw2 = nc.dram_tensor("cw2", [CFF, 128, DK * 128], BF16, kind="Internal")
    cw3 = nc.dram_tensor("cw3", [DK, 128, CFF * 128], BF16, kind="Internal")
    if mode == "A":
        cqkv = nc.dram_tensor("cqkv", [CQK, 128, DK * 128], BF16, kind="Internal")
        cab = nc.dram_tensor("cab", [DK, 128, 32], BF16, kind="Internal")
    else:
        cwg = nc.dram_tensor("cwg", [DK, 128, DK * 128], BF16, kind="Internal")
        cwo = nc.dram_tensor("cwo", [DK, 128, DK * 128], BF16, kind="Internal")
        x2scr = nc.dram_tensor("x2scr", [D, T_local], F32, kind="Internal")

    KONST = {
        "idb": np.eye(128).astype(BF),
        "idf": np.eye(128, dtype=np.float32),
        "on1b": np.ones((1, 128)).astype(BF),
        "on1f": np.ones((1, 128), dtype=np.float32),
        "oncf": np.ones((128, 1), dtype=np.float32),
        "oncb": np.ones((128, 1)).astype(BF),
        "trii": np.triu(np.ones((128, 128), dtype=np.float32)),
        "trir": np.tril(np.ones((128, 128), dtype=np.float32), -1),
        "bias_nat": 1e6 * np.triu(np.ones((128, 128), dtype=np.float32)),
        "bias_eot": 1e6 * np.tril(np.ones((128, 128), dtype=np.float32), -1),
    }
    ktens = {nm: nc.inline_tensor(v, "k_" + nm) for nm, v in KONST.items()}
    cs = {}
    for nm, v in KONST.items():
        cs[nm] = nc.alloc_sbuf_tensor("c_" + nm, list(v.shape),
                                      BF16 if v.dtype == BF else F32)

    sbB = lambda n, s: nc.alloc_sbuf_tensor(n, s, BF16)
    sbF = lambda n, s: nc.alloc_sbuf_tensor(n, s, F32)

    XT = sbB("XT", [128, DK * TH])
    XN = sbB("XN", [128, DK * 512])
    RGN = nc.alloc_sbuf_tensor("RGN", [128, 45056 // 4], F32)
    off = nc.lookup_mloc(RGN).addr
    HS = nc.alloc_sbuf_tensor_at("HS", [128, CFF * 512], BF16, offset=off)
    WQf = nc.alloc_sbuf_tensor_at("WQf", [128, 2816], F32, offset=off)
    WQi = nc.alloc_sbuf_tensor_at("WQi", [128, 2816], I32, offset=off + 11264)
    WQb = nc.alloc_sbuf_tensor_at("WQb", [128, 2816], BF16, offset=off + 22528)
    SPF1 = nc.alloc_sbuf_tensor_at("SPF1", [128, NH * 128], F32, offset=off)
    SPF2 = nc.alloc_sbuf_tensor_at("SPF2", [128, NH * 128], F32, offset=off + 8192)

    W3S0 = sbB("W3S0", [128, CFF * 128])
    W3S1 = sbB("W3S1", [128, CFF * 128])
    WSL = [W3S0[:, 0:DK * 128], W3S0[:, DK * 128:2 * DK * 128]]
    WSL2 = [W3S1[:, 0:DK * 128], W3S1[:, DK * 128:2 * DK * 128]]
    W3S = [W3S0, W3S1]

    STAT = sbF("STAT", [1, 2048])
    ROWS = sbF("ROWS", [1, 128])
    GINV = sbF("GINV", [128, 4])
    GCOLS = sbF("GCOLS", [128, 4])
    BC1 = sbF("BC1", [128, 512])
    BC2 = sbF("BC2", [128, 512])
    SGT = sbF("SGT", [128, 512])
    JUNK = sbF("JUNK", [128, 512])

    if mode == "A":
        QKVC = sbB("QKVC", [128, CQK * 128])
        QKVP = sbB("QKVP", [128, CQK * 4])
        CONVW = sbF("CONVW", [128, CQK * 4])
        BABB = sbF("BABB", [128, 32])
        WAB = sbB("WAB", [128, DK * 32])
        TRN = sbB("TRN", [128, 3 * 128])
        XS = sbB("XS", [128, 256])
        PWB = sbB("PWB", [128, 256])
        PW_P = PWB[:, 0:128]
        PW_T = PWB[:, 128:256]
        EOT = sbB("EOT", [128, 128])
        QTT = sbB("QTT", [128, 128])
        RSOL = sbB("RSOL", [128, 256])
        KTL = sbB("KTL", [128, 128])
        GT_C = sbB("GT_C", [128, 128])
        GACC = sbB("GACC", [128, NH * 128])
        HACC = sbB("HACC", [128, NH * 128])
        GACCT = sbB("GACCT", [128, NH * 128])
        DNAT = sbF("DNAT", [128, 128])
        DTRN = sbF("DTRN", [128, 128])
        EOX = sbF("EOX", [128, 128])
        EARG = sbF("EARG", [128, 128])
        TT1 = sbF("TT1", [128, 128])
        TT2 = sbF("TT2", [128, 128])
        ABR = sbF("ABR", [32, 128])
        ASIG = sbF("ASIG", [128, 32])
        GAMS = sbF("GAMS", [128, 7 * NH])
        NK2 = sbF("NK2", [128, NH]); NQ2 = sbF("NQ2", [128, NH])
        LNK = sbF("LNK", [128, NH]); LNQ = sbF("LNQ", [128, NH])
        INVK = sbF("INVK", [128, NH]); INVQ = sbF("INVQ", [128, NH])
        GPC = sbF("GPC", [128, NH])
        BGIK = sbF("BGIK", [128, NH]); RTIK = sbF("RTIK", [128, NH])
        PACK = sbF("PACK", [128, 3 * NH])
        RROW = sbF("RROW", [1, 128])
    else:
        ZIN = sbB("ZIN", [128, NH * 128])
        GHS = sbB("GHS", [128, 2 * 128])
        GROW = sbB("GROW", [128, DK * 512])
        OFIX = sbB("OFIX", [128, NH * 512])
        SPB1 = nc.alloc_sbuf_tensor_at("SPB1", [128, NH * 128], BF16, offset=off + 16384)
        SPB2 = nc.alloc_sbuf_tensor_at("SPB2", [128, NH * 128], BF16, offset=off + 20480)

    PSB0 = nc.alloc_psum_tensor("PSB0", [128, 1024], BF16)
    PS = [None] + [nc.alloc_psum_tensor("PS%d" % i, [128, 512], F32) for i in range(1, 8)]

    P = Em(nc)
    s_ld = P.sem("s_ld"); s_wd = P.sem("s_wd"); s_mm = P.sem("s_mm")
    s_ve = P.sem("s_ve"); s_sc = P.sem("s_sc"); s_gp = P.sem("s_gp")
    s_st = P.sem("s_st"); s_fb = P.sem("s_fb")
    s_w2 = P.sem("s_w2"); s_l2 = P.sem("s_l2"); s_bc = P.sem("s_bc")
    s_g1 = P.sem("s_g1"); s_g2 = P.sem("s_g2")
    P.own = {"vector": s_ve, "scalar": s_sc, "gpsimd": s_gp}

    def full_barrier(tag):
        wt("tensor", s_mm)
        wt("vector", s_ve)
        wt("scalar", s_sc)
        wt("scalar", s_wd)
        wt("gpsimd", s_gp)
        wt("sync", s_ld)
        wt("sync", s_st)
        P.barrier(tag)
    em, wt, dm = P.em, P.wt, P.dma

    for nm in KONST:
        dm("sync", cs[nm][:, :], ktens[nm].ap()[:, :], s_ld)
    xin = inp["xT"] if mode == "A" else inp["x1T"]
    for k in range(DK):
        dm("gpsimd", XT[:, k * TH:(k + 1) * TH], xin[k * 128:(k + 1) * 128, :], s_gp)
    if mode == "A":
        dm("sync", BABB[0:1, 0:32], inp["bab"][:, :], s_ld)
        for t in range(CQK):
            dm("sync", CONVW[:, t * 4:(t + 1) * 4], inp["convw"][t * 128:(t + 1) * 128, :], s_ld)

    def stream_quant():
        wjobs = [(inp["w1T"], DK, HFF, cw1, 0), (inp["w2T"], DK, HFF, cw2, 1),
                 (inp["w3T"], CFF, D, cw3, 2)]
        for w, ntile, wid, cache, wi in wjobs:
            nh_ = (wid + 2815) // 2816
            col = 0
            for k in range(ntile):
                for hh in range(nh_):
                    c0 = hh * 2816
                    n = min(2816, wid - c0)
                    t = dm("scalar", WQf[:, 0:n], w[k * 128:(k + 1) * 128, c0:c0 + n], s_wd)
                    wt("vector", s_wd, t)
                    v = em("vector", lambda eng, n=n, col=col:
                           eng.reduce_sum(GCOLS[:, 3:4] if False else JUNK[:, col:col + 1],
                                          WQf[:, 0:n], axis=AXX,
                                          apply_absolute_value=True), inc=s_ve)
                    wt("scalar", s_ve, v)
                    col += 1
            em("vector", lambda eng, col=col, wi=wi:
               eng.reduce_sum(GCOLS[:, wi:wi + 1], JUNK[:, 0:col], axis=AXX), inc=s_ve)
        wt("tensor", s_ld)
        wt("tensor", s_ve)
        for wi in range(3):
            em("tensor", lambda eng, wi=wi:
               eng.matmul(PS[7][0:1, wi:wi + 1], GCOLS[:, wi:wi + 1], cs["oncf"][:, :],
                          start=True, stop=True), inc=s_mm)
        wt("vector", s_mm)

        def finv(eng):
            eng.tensor_scalar(ROWS[0:1, 0:3], PS[7][0:1, 0:3], 1.0 / (HFF * D), EPS_Q,
                              ALU.mult, ALU.add)
            return eng.reciprocal(ROWS[0:1, 0:3], ROWS[0:1, 0:3])
        v = em("vector", finv, inc=s_ve)
        wt("tensor", s_ve, v)
        m = em("tensor", lambda eng: eng.matmul(PS[7][:, 4:7], cs["on1f"][0:1, :],
                                                ROWS[0:1, 0:3], start=True, stop=True), inc=s_mm)
        wt("vector", s_mm, m)
        em("vector", lambda eng: eng.tensor_copy(GINV[:, 0:3], PS[7][:, 4:7]), inc=s_ve)

        for w, ntile, wid, cache, wi in wjobs:
            nh_ = (wid + 2815) // 2816
            for k in range(ntile):
                for hh in range(nh_):
                    c0 = hh * 2816
                    n = min(2816, wid - c0)
                    t = dm("scalar", WQf[:, 0:n], w[k * 128:(k + 1) * 128, c0:c0 + n], s_wd)
                    wt("vector", s_wd, t)

                    def fq(eng, n=n, wi=wi):
                        eng.tensor_scalar(WQf[:, 0:n], WQf[:, 0:n], GINV[:, wi:wi + 1],
                                          None, ALU.mult)
                        eng.tensor_scalar(JUNK[:, 0:n] if n <= 512 else WQi[:, 0:n].bitcast(F32),
                                          WQf[:, 0:n], -0.5, None, ALU.is_le)
                        eng.tensor_scalar(WQf[:, 0:n], WQf[:, 0:n], 0.5, None, ALU.is_ge)
                        return eng.tensor_tensor(WQb[:, 0:n], WQf[:, 0:n],
                                                 JUNK[:, 0:n] if n <= 512 else WQi[:, 0:n].bitcast(F32),
                                                 ALU.subtract)
                    v = em("vector", fq, inc=s_ve)
                    wt("sync", s_ve, v)
                    for b in range(n // 128):
                        dm("sync", cache.ap()[(c0 // 128) + b, :, k * 128:(k + 1) * 128],
                           WQb[:, b * 128:(b + 1) * 128], s_st)
                    wt("scalar", s_st)
        cjobs = ([(inp["wqkvT"], cqkv, DK, QKV), (inp["wabT"], cab, DK, 32)] if mode == "A"
                 else [(inp["wgT"], cwg, DK, D), (inp["woT"], cwo, DK, D)])
        for w, cache, ntile, wid in cjobs:
            nh_ = (wid + 2815) // 2816
            for k in range(ntile):
                for hh in range(nh_):
                    c0 = hh * 2816
                    n = min(2816, wid - c0)
                    t = dm("scalar", WQf[:, 0:n], w[k * 128:(k + 1) * 128, c0:c0 + n], s_wd)
                    wt("vector", s_wd, t)
                    v = em("vector", lambda eng, n=n:
                           eng.tensor_copy(WQb[:, 0:n], WQf[:, 0:n]), inc=s_ve)
                    wt("sync", s_ve, v)
                    if wid >= 128:
                        for b in range(n // 128):
                            dm("sync", cache.ap()[(c0 // 128) + b, :, k * 128:(k + 1) * 128],
                               WQb[:, b * 128:(b + 1) * 128], s_st)
                    else:
                        dm("sync", cache.ap()[k, :, 0:n], WQb[:, 0:n], s_st)
                    wt("scalar", s_st)

    stream_quant()
    wt("sync", s_ld)
    full_barrier("p0")

    def ln_stats(xbase, ncols):
        def mm_s1(eng, xbase=xbase, ncols=ncols):
            last = None
            for k in range(DK):
                last = eng.matmul(PS[5][0:1, 0:ncols], cs["oncb"][:, :],
                                  XT[:, k * TH + xbase:k * TH + xbase + ncols],
                                  start=(k == 0), stop=(k == DK - 1))
            return last
        em("tensor", mm_s1, inc=s_mm)
        wt("scalar", s_ve)
        wt("scalar", s_st)
        wt("scalar", s_gp)
        JB = (JUNK, SGT)
        sq_t = [0] * DK
        mmq_t = [0] * DK
        for k in range(DK):
            if k >= 2:
                wt("scalar", s_mm, mmq_t[k - 2])
            sq_t[k] = em("scalar", lambda eng, k=k, xbase=xbase, ncols=ncols:
                         eng.activation(JB[k % 2][:, 0:ncols],
                                        XT[:, k * TH + xbase:k * TH + xbase + ncols],
                                        AF.Square), inc=s_sc)
            wt("tensor", s_sc, sq_t[k])
            mmq_t[k] = em("tensor", lambda eng, k=k, ncols=ncols:
                          eng.matmul(PS[6][0:1, 0:ncols], cs["oncf"][:, :],
                                     JB[k % 2][:, 0:ncols],
                                     start=(k == 0), stop=(k == DK - 1)), inc=s_mm)
        wt("vector", s_mm)

        def fstat(eng, ncols=ncols):
            mu = STAT[0:1, 0:ncols]
            var = STAT[0:1, 512:512 + ncols]
            t1 = STAT[0:1, 1024:1024 + ncols]
            eng.tensor_scalar(mu, PS[5][0:1, 0:ncols], 1.0 / D, None, ALU.mult)
            eng.tensor_scalar(var, PS[6][0:1, 0:ncols], 1.0 / D, None, ALU.mult)
            eng.tensor_tensor(t1, mu, mu, ALU.mult)
            eng.tensor_sub(var, var, t1)
            return eng.tensor_scalar(var, var, EPS_LN, None, ALU.add)
        v = em("vector", fstat, inc=s_ve)
        wt("scalar", s_ve, v)
        sq = em("scalar", lambda eng, ncols=ncols:
                eng.activation(STAT[0:1, 512:512 + ncols], STAT[0:1, 512:512 + ncols],
                               AF.Sqrt), inc=s_sc)
        wt("vector", s_sc, sq)
        em("vector", lambda eng, ncols=ncols:
           eng.reciprocal(STAT[0:1, 512:512 + ncols], STAT[0:1, 512:512 + ncols]), inc=s_ve)

    def make_xn(xbase, ncols):
        wt("tensor", s_ve)
        em("tensor", lambda eng, ncols=ncols:
           eng.matmul(PS[5][:, 0:ncols], cs["on1f"][0:1, :], STAT[0:1, 0:ncols],
                      start=True, stop=True), inc=s_mm)
        m = em("tensor", lambda eng, ncols=ncols:
               eng.matmul(PS[6][:, 0:ncols], cs["on1f"][0:1, :], STAT[0:1, 512:512 + ncols],
                          start=True, stop=True), inc=s_mm)
        wt("vector", s_mm, m)
        wt("vector", s_ld)
        em("vector", lambda eng, ncols=ncols:
           eng.tensor_copy(BC1[:, 0:ncols], PS[5][:, 0:ncols]), inc=s_ve)
        em("vector", lambda eng, ncols=ncols:
           eng.tensor_copy(BC2[:, 0:ncols], PS[6][:, 0:ncols]), inc=s_ve)
        for k in range(DK):
            def f(eng, k=k, xbase=xbase, ncols=ncols):
                xx = XT[:, k * TH + xbase:k * TH + xbase + ncols]
                xn = XN[:, k * 512:k * 512 + ncols]
                eng.tensor_sub(xn, xx, BC1[:, 0:ncols])
                return eng.tensor_mul(xn, xn, BC2[:, 0:ncols])
            em("vector", f, inc=s_ve)
        v = P.n(s_ve)
        wt("tensor", s_ve, v)
        wt("scalar", s_ve, v)

    def ffn(xbase, ncols, out_store=None, x1_store=None):
        mm_t = [0] * CFF
        sc_t = [0] * CFF
        ve_t = [0] * CFF
        for cblk in range(CFF):
            bufA, bufB = WSL[cblk % 2], WSL2[cblk % 2]
            bA, bB = PS[1 + cblk % 2], PS[3 + cblk % 2]
            if cblk >= 2:
                wt("scalar", s_mm, mm_t[cblk - 2])
                wt("sync", s_mm, mm_t[cblk - 2])
            else:
                wt("scalar", s_mm)
                wt("sync", s_mm)
            sw = s_wd if cblk % 2 == 0 else s_w2
            sl = s_ld if cblk % 2 == 0 else s_l2
            wd = dm("scalar", bufA[:, :], cw1.ap()[cblk, :, :], sw)
            ldv = dm("sync", bufB[:, :], cw2.ap()[cblk, :, :], sl)
            wt("tensor", sw, wd)
            wt("tensor", sl, ldv)
            if cblk >= 2:
                wt("tensor", s_sc, sc_t[cblk - 2])
                wt("tensor", s_ve, ve_t[cblk - 2])

            def mms(eng, bufA=bufA, bufB=bufB, bA=bA, bB=bB, ncols=ncols):
                for k in range(DK):
                    eng.matmul(bA[:, 0:ncols], bufA[:, k * 128:(k + 1) * 128],
                               XN[:, k * 512:k * 512 + ncols],
                               start=(k == 0), stop=(k == DK - 1))
                last = None
                for k in range(DK):
                    last = eng.matmul(bB[:, 0:ncols], bufB[:, k * 128:(k + 1) * 128],
                                      XN[:, k * 512:k * 512 + ncols],
                                      start=(k == 0), stop=(k == DK - 1))
                return last
            mm_t[cblk] = em("tensor", mms, inc=s_mm)
            wt("scalar", s_mm, mm_t[cblk])
            if cblk >= 1:
                wt("scalar", s_ve, ve_t[cblk - 1])
            sc_t[cblk] = em("scalar", lambda eng, bA=bA, ncols=ncols:
                            eng.activation(SGT[:, 0:ncols], bA[:, 0:ncols], AF.Sigmoid),
                            inc=s_sc)
            wt("vector", s_sc, sc_t[cblk])
            ve_t[cblk] = em("vector", lambda eng, bB=bB, cblk=cblk, ncols=ncols:
                            eng.tensor_tensor(HS[:, cblk * 512:cblk * 512 + ncols],
                                              SGT[:, 0:ncols], bB[:, 0:ncols], ALU.mult),
                            inc=s_ve)
        m3_t = [0] * DK
        v3_t = [0] * DK
        g3_t = [0] * DK
        JB = (JUNK, SGT)
        for dblk in range(DK):
            buf3 = W3S[dblk % 2]
            b3 = PS[1 + dblk % 2]
            if dblk >= 2:
                wt("scalar" if dblk % 2 == 0 else "sync", s_mm, m3_t[dblk - 2])
            else:
                wt("scalar" if dblk % 2 == 0 else "sync", s_mm, mm_t[CFF - 1])
            wd = dm("scalar" if dblk % 2 == 0 else "sync", buf3[:, :],
                    cw3.ap()[dblk, :, :], s_wd if dblk % 2 == 0 else s_ld)
            wt("tensor", s_wd if dblk % 2 == 0 else s_ld, wd)
            if dblk == 0:
                wt("tensor", s_ve, ve_t[CFF - 1])
                wt("tensor", s_sc, sc_t[CFF - 1])
            if dblk >= 2:
                wt("tensor", s_ve, v3_t[dblk - 2])

            def mm3(eng, buf3=buf3, b3=b3, ncols=ncols):
                last = None
                for k in range(CFF):
                    last = eng.matmul(b3[:, 0:ncols], buf3[:, k * 128:(k + 1) * 128],
                                      HS[:, k * 512:k * 512 + ncols],
                                      start=(k == 0), stop=(k == CFF - 1))
                return last
            m3_t[dblk] = em("tensor", mm3, inc=s_mm)
            wt("vector", s_mm, m3_t[dblk])
            jb = JB[dblk % 2]
            sg = s_g1 if dblk % 2 == 0 else s_g2
            if dblk >= 2:
                wt("vector", sg, g3_t[dblk - 2])
            if out_store is None:
                def fr(eng, b3=b3, dblk=dblk, xbase=xbase, ncols=ncols, jb=jb):
                    xx = XT[:, dblk * TH + xbase:dblk * TH + xbase + ncols]
                    eng.scalar_tensor_tensor(jb[:, 0:ncols], b3[:, 0:ncols], 0.5, xx,
                                             ALU.mult, ALU.add)
                    return eng.tensor_copy(xx, jb[:, 0:ncols])
                v3_t[dblk] = em("vector", fr, inc=s_ve)
                if x1_store is not None:
                    wt("gpsimd", s_ve, v3_t[dblk])
                    g3_t[dblk] = dm("gpsimd", x1_store[dblk * 128:(dblk + 1) * 128,
                                                       xbase:xbase + ncols],
                                    jb[:, 0:ncols], sg)
                else:
                    g3_t[dblk] = P.n(sg)
            else:
                wt("sync", s_ve)
                t = dm("sync", BC1[:, 0:ncols],
                       x2scr.ap()[dblk * 128:(dblk + 1) * 128,
                                  xbase - HALO:xbase - HALO + ncols], s_bc)
                wt("vector", s_bc, t)

                def fr(eng, b3=b3, ncols=ncols, jb=jb):
                    return eng.scalar_tensor_tensor(jb[:, 0:ncols], b3[:, 0:ncols], 0.5,
                                                    BC1[:, 0:ncols], ALU.mult, ALU.add)
                v3_t[dblk] = em("vector", fr, inc=s_ve)
                wt("gpsimd", s_ve, v3_t[dblk])
                g3_t[dblk] = dm("gpsimd", out_store[dblk * 128:(dblk + 1) * 128,
                                                    xbase - HALO:xbase - HALO + ncols],
                                jb[:, 0:ncols], sg)
        wt("vector", s_g1)
        wt("vector", s_g2)
        wt("scalar", s_g1)
        wt("scalar", s_g2)

    blocks = [(0, HALO)] + [(HALO + g * 512, min(512, T_local - g * 512)) for g in range(n_tg)]

    if mode == "A":
        wt("vector", s_gp)
        wt("tensor", s_gp)
        wt("scalar", s_gp)
        for (xb, n) in blocks:
            ln_stats(xb, n)
            make_xn(xb, n)
            ffn(xb, n, x1_store=out["x1T_out"])
        full_barrier("ffn1")

        emit_mixer_A(P, locals())
        wt("sync", s_st)
        wt("sync", s_gp)
        full_barrier("endA")
    else:
        emit_B(P, locals())
        wt("sync", s_st)
        full_barrier("endB")

    P.run()
    return nc


def emit_mixer_A(P, L):
    g = lambda n: L[n]
    em, wt, dm = P.em, P.wt, P.dma
    nc = g("nc")
    s_ld, s_wd, s_mm, s_ve, s_sc, s_gp, s_st = (g("s_ld"), g("s_wd"), g("s_mm"),
                                                g("s_ve"), g("s_sc"), g("s_gp"), g("s_st"))
    cs, PS, PSB0 = g("cs"), g("PS"), g("PSB0")
    XT, XN, JUNK = g("XT"), g("XN"), g("JUNK")
    TH, T_local, n_ch = g("TH"), g("T_local"), g("n_ch")
    ln_stats, make_xn = g("ln_stats"), g("make_xn")
    inp, out = g("inp"), g("out")
    cqkv, cab = g("cqkv"), g("cab")
    QKVC, QKVP, CONVW, BABB, WAB = g("QKVC"), g("QKVP"), g("CONVW"), g("BABB"), g("WAB")
    TRN, XS, PW_P, PW_T, EOT, QTT = g("TRN"), g("XS"), g("PW_P"), g("PW_T"), g("EOT"), g("QTT")
    RSOL, KTL, GT_C, PWB = g("RSOL"), g("KTL"), g("GT_C"), g("PWB")
    GACC, HACC, GACCT = g("GACC"), g("HACC"), g("GACCT")
    DNAT, DTRN, EOX, EARG, TT1, TT2 = g("DNAT"), g("DTRN"), g("EOX"), g("EARG"), g("TT1"), g("TT2")
    ABR, ASIG, GAMS = g("ABR"), g("ASIG"), g("GAMS")
    NK2, NQ2, LNK, LNQ, INVK, INVQ = g("NK2"), g("NQ2"), g("LNK"), g("LNQ"), g("INVK"), g("INVQ")
    GPC, BGIK, RTIK, PACK, RROW = g("GPC"), g("BGIK"), g("RTIK"), g("PACK"), g("RROW")
    SPF1, SPF2 = g("SPF1"), g("SPF2")
    WSL = g("WSL")

    em("vector", lambda eng: eng.memset(HACC[:, :], 0.0), inc=s_ve)
    em("vector", lambda eng: eng.memset(QKVP[:, :], 0.0), inc=s_ve)
    for h in range(NH):
        em("vector", lambda eng, h=h: eng.tensor_copy(GACC[:, h * 128:(h + 1) * 128],
                                                      cs["idb"][:, :]), inc=s_ve)
        em("vector", lambda eng, h=h: eng.tensor_copy(GACCT[:, h * 128:(h + 1) * 128],
                                                      cs["idb"][:, :]), inc=s_ve)
    for k in range(16):
        dm("sync", WAB[:, k * 32:(k + 1) * 32], cab.ap()[k, :, 0:32], s_ld)
    wt("tensor", s_ld)
    wt("tensor", s_ve)
    m = em("tensor", lambda eng: eng.matmul(PS[7][:, 0:32], cs["on1f"][0:1, :],
                                            BABB[0:1, 0:32], start=True, stop=True), inc=s_mm)
    wt("vector", s_mm, m)
    em("vector", lambda eng: eng.tensor_copy(BABB[:, 0:32], PS[7][:, 0:32]), inc=s_ve)
    g("full_barrier")("mxinit")

    def qkv_block(xnoff, ncols, want_conv):
        mm_t = [0] * 48
        cv_t = [0] * 48
        si_t = [0] * 48
        wt("vector", s_sc)
        wt("vector", s_gp)
        for cb in range(48):
            buf = WSL[cb % 2]
            bk = PS[1 + cb % 2]
            if cb >= 2:
                wt("scalar" if cb % 2 == 0 else "sync", s_mm, mm_t[cb - 2])
            else:
                wt("scalar" if cb % 2 == 0 else "sync", s_mm)
            wd = dm("scalar" if cb % 2 == 0 else "sync", buf[:, :], cqkv.ap()[cb, :, :],
                    s_wd if cb % 2 == 0 else s_ld)
            wt("tensor", s_wd if cb % 2 == 0 else s_ld, wd)
            if cb >= 2:
                wt("tensor", s_ve, cv_t[cb - 2])

            def mmq(eng, buf=buf, bk=bk, xnoff=xnoff, ncols=ncols):
                last = None
                for k in range(16):
                    last = eng.matmul(bk[:, 0:ncols], buf[:, k * 128:(k + 1) * 128],
                                      XN[:, k * 512 + xnoff:k * 512 + xnoff + ncols],
                                      start=(k == 0), stop=(k == 15))
                return last
            mm_t[cb] = em("tensor", mmq, inc=s_mm)
            wt("vector", s_mm, mm_t[cb])
            if cb >= 2:
                wt("vector", s_sc, si_t[cb - 2])
            if want_conv:
                def fconv(eng, bk=bk, cb=cb):
                    o = JUNK[:, (cb % 2) * 128:(cb % 2) * 128 + 128]
                    w3 = CONVW[:, cb * 4 + 3:cb * 4 + 4]
                    eng.tensor_scalar(o, bk[:, 0:128], w3, None, ALU.mult)
                    for j in range(3):
                        w = CONVW[:, cb * 4 + j:cb * 4 + j + 1]
                        sh = 3 - j
                        eng.scalar_tensor_tensor(o[:, sh:128], bk[:, 0:128 - sh], w,
                                                 o[:, sh:128], ALU.mult, ALU.add)
                        eng.scalar_tensor_tensor(o[:, 0:sh],
                                                 QKVP[:, cb * 4 + 4 - sh:cb * 4 + 4], w,
                                                 o[:, 0:sh], ALU.mult, ALU.add)
                    return eng.tensor_copy(QKVP[:, cb * 4 + 1:cb * 4 + 4], bk[:, 125:128])
                cv_t[cb] = em("vector", fconv, inc=s_ve)
                wt("scalar", s_ve, cv_t[cb])
                si_t[cb] = em("scalar", lambda eng, cb=cb:
                              eng.activation(QKVC[:, cb * 128:(cb + 1) * 128],
                                             JUNK[:, (cb % 2) * 128:(cb % 2) * 128 + 128],
                                             AF.Silu), inc=s_sc)
            else:
                cv_t[cb] = em("vector", lambda eng, bk=bk, cb=cb, ncols=ncols:
                              eng.tensor_copy(QKVP[:, cb * 4 + 1:cb * 4 + 4],
                                              bk[:, ncols - 3:ncols]), inc=s_ve)
                si_t[cb] = P.n(s_sc)
        wt("tensor", s_sc)
        wt("tensor", s_ve)

    ln_stats(0, HALO)
    make_xn(0, HALO)
    qkv_block(0, HALO, want_conv=False)

    for ch in range(n_ch):
        tg = (ch * C) // 512
        if (ch * C) % 512 == 0:
            xb = HALO + tg * 512
            n = min(512, T_local - tg * 512)
            ln_stats(xb, n)
            make_xn(xb, n)
        xnoff = (ch * C) % 512
        qkv_block(xnoff, C, want_conv=True)

        wt("tensor", s_sc)
        def mmab(eng, xnoff=xnoff):
            last = None
            for k in range(16):
                last = eng.matmul(PS[3][0:32, 0:C], WAB[:, k * 32:(k + 1) * 32],
                                  XN[:, k * 512 + xnoff:k * 512 + xnoff + C],
                                  start=(k == 0), stop=(k == 15))
            return last
        m = em("tensor", mmab, inc=s_mm)
        wt("vector", s_mm, m)
        v = em("vector", lambda eng: eng.tensor_copy(ABR[0:32, 0:C], PS[3][0:32, 0:C]), inc=s_ve)
        wt("tensor", s_ve, v)
        m = em("tensor", lambda eng: eng.transpose(PS[4][:, 0:32], ABR[0:32, 0:C],
                                                   cs["idf"][0:32, 0:32]), inc=s_mm)
        wt("vector", s_mm, m)
        wt("vector", s_sc)
        v = em("vector", lambda eng: eng.tensor_add(ASIG[:, 0:32], PS[4][:, 0:32],
                                                    BABB[:, 0:32]), inc=s_ve)
        wt("scalar", s_ve, v)
        sc = em("scalar", lambda eng: eng.activation(ASIG[:, 0:32], ASIG[:, 0:32],
                                                     AF.Sigmoid), inc=s_sc)
        sc = em("scalar", lambda eng: eng.activation(GAMS[:, 0:NH], ASIG[:, 0:NH], AF.Ln),
                inc=s_sc)
        wt("tensor", s_sc, sc)
        em("tensor", lambda eng: eng.matmul(PS[3][:, 0:NH], cs["trii"][:, :], GAMS[:, 0:NH],
                                            start=True, stop=True), inc=s_mm)
        m = em("tensor", lambda eng: eng.matmul(PS[4][:, 0:NH], cs["trir"][:, :], GAMS[:, 0:NH],
                                                start=True, stop=True), inc=s_mm)
        wt("vector", s_mm, m)
        wt("vector", s_sc)

        def fg(eng):
            eng.tensor_copy(GAMS[:, NH:2 * NH], PS[3][:, 0:NH])
            eng.tensor_copy(GAMS[:, 2 * NH:3 * NH], PS[4][:, 0:NH])
            return eng.tensor_add(GAMS[:, 3 * NH:4 * NH], GAMS[:, NH:2 * NH],
                                  GAMS[:, 2 * NH:3 * NH])
        v = em("vector", fg, inc=s_ve)
        wt("scalar", s_ve, v)
        em("scalar", lambda eng: eng.activation(GAMS[:, 4 * NH:5 * NH], GAMS[:, NH:2 * NH],
                                                AF.Exp), inc=s_sc)
        em("scalar", lambda eng: eng.activation(GAMS[:, 5 * NH:6 * NH], GAMS[:, 3 * NH:4 * NH],
                                                AF.Exp), inc=s_sc)
        sc = em("scalar", lambda eng: eng.activation(GAMS[:, 6 * NH:7 * NH],
                                                     GAMS[:, 2 * NH:3 * NH], AF.Exp), inc=s_sc)
        wt("tensor", s_sc, sc)
        wt("vector", s_sc, sc)

        for h in range(NH):
            kn = QKVC[:, (NH + h) * 128:(NH + h + 1) * 128]
            vn = QKVC[:, (2 * NH + h) * 128:(2 * NH + h + 1) * 128]
            qn = QKVC[:, h * 128:(h + 1) * 128]
            m = em("tensor", lambda eng, kn=kn: eng.transpose(PSB0[:, 0:128], kn,
                                                              cs["idb"][:, :]), inc=s_mm)
            m = em("tensor", lambda eng, vn=vn: eng.transpose(PSB0[:, 128:256], vn,
                                                              cs["idb"][:, :]), inc=s_mm)
            m = em("tensor", lambda eng, qn=qn: eng.transpose(PSB0[:, 256:384], qn,
                                                              cs["idb"][:, :]), inc=s_mm)
            wt("vector", s_mm, m)

            def fev(eng):
                eng.tensor_copy(TRN[:, 0:128], PSB0[:, 0:128])
                eng.tensor_copy(TRN[:, 128:256], PSB0[:, 128:256])
                return eng.tensor_copy(TRN[:, 256:384], PSB0[:, 256:384])
            v = em("vector", fev, inc=s_ve)
            wt("scalar", s_ve, v)
            wt("tensor", s_ve, v)
            sc = em("scalar", lambda eng, h=h:
                    eng.activation(JUNK[:, 0:128], TRN[:, 0:128], AF.Square,
                                   accum_out=NK2[:, h:h + 1]), inc=s_sc)
            sc = em("scalar", lambda eng, h=h:
                    eng.activation(JUNK[:, 128:256], TRN[:, 256:384], AF.Square,
                                   accum_out=NQ2[:, h:h + 1]), inc=s_sc)
            wt("vector", s_sc, sc)
            emit_head(P, L, ch, h)

        wt("sync", s_ve)
        dm("sync", out["spill_pt"][ch * NH * 128:(ch + 1) * NH * 128, :]
           .rearrange("(h p) t -> p h t", p=128),
           SPF1[:, :].rearrange("p (h t) -> p h t", h=NH), s_st)
        dm("sync", out["spill_op"][ch * NH * 128:(ch + 1) * NH * 128, :]
           .rearrange("(h p) t -> p h t", p=128),
           SPF2[:, :].rearrange("p (h t) -> p h t", h=NH), s_st)
        wt("vector", s_st)

    for h in range(NH):
        v = em("vector", lambda eng, h=h: eng.tensor_copy(SPF1[:, h * 128:(h + 1) * 128],
                                                          GACCT[:, h * 128:(h + 1) * 128]), inc=s_ve)
        v = em("vector", lambda eng, h=h: eng.tensor_copy(SPF2[:, h * 128:(h + 1) * 128],
                                                          HACC[:, h * 128:(h + 1) * 128]), inc=s_ve)
    wt("sync", s_ve)
    dm("sync", out["ght_out"].rearrange("(h p) t -> p h t", p=128),
       SPF1[:, :].rearrange("p (h t) -> p h t", h=NH), s_st)
    dm("sync", out["h_out"].rearrange("(h p) t -> p h t", p=128),
       SPF2[:, :].rearrange("p (h t) -> p h t", h=NH), s_st)


def emit_head(P, L, ch, h):
    g = lambda n: L[n]
    em, wt = P.em, P.wt
    s_mm, s_ve, s_sc = g("s_mm"), g("s_ve"), g("s_sc")
    cs, PS, PSB0 = g("cs"), g("PS"), g("PSB0")
    QKVC, TRN, XS, PW_P, PW_T = g("QKVC"), g("TRN"), g("XS"), g("PW_P"), g("PW_T")
    EOT, QTT, RSOL, KTL, GT_C = g("EOT"), g("QTT"), g("RSOL"), g("KTL"), g("GT_C")
    GACC, HACC, GACCT, PWB = g("GACC"), g("HACC"), g("GACCT"), g("PWB")
    DNAT, DTRN, EOX, EARG, TT1, TT2 = g("DNAT"), g("DTRN"), g("EOX"), g("EARG"), g("TT1"), g("TT2")
    ASIG, GAMS = g("ASIG"), g("GAMS")
    NK2, NQ2, LNK, LNQ, INVK, INVQ = g("NK2"), g("NQ2"), g("LNK"), g("LNQ"), g("INVK"), g("INVQ")
    GPC, BGIK, RTIK, PACK, RROW = g("GPC"), g("BGIK"), g("RTIK"), g("PACK"), g("RROW")
    SPF1, SPF2, JUNK = g("SPF1"), g("SPF2"), g("JUNK")

    Kt, Vt, Qt = TRN[:, 0:128], TRN[:, 128:256], TRN[:, 256:384]
    kn = QKVC[:, (g("NH") if False else NH + h) * 128:(NH + h + 1) * 128]
    qn = QKVC[:, h * 128:(h + 1) * 128]
    gcol = GAMS[:, NH + h:NH + h + 1]
    gamc = GAMS[:, 4 * NH + h:4 * NH + h + 1]
    gamCc = GAMS[:, 5 * NH + h:5 * NH + h + 1]
    ratc = GAMS[:, 6 * NH + h:6 * NH + h + 1]
    beta = ASIG[:, NH + h:NH + h + 1]

    def fc1(eng, h=h):
        eng.tensor_copy(LNK[:, h:h + 1], NK2[:, h:h + 1])
        return eng.tensor_copy(LNQ[:, h:h + 1], NQ2[:, h:h + 1])
    v = em("vector", fc1, inc=s_ve)
    wt("scalar", s_ve, v)
    sc = em("scalar", lambda eng, h=h: eng.activation(LNK[:, h:h + 1], LNK[:, h:h + 1], AF.Ln),
            inc=s_sc)
    sc = em("scalar", lambda eng, h=h: eng.activation(LNQ[:, h:h + 1], LNQ[:, h:h + 1], AF.Ln),
            inc=s_sc)
    sc = em("scalar", lambda eng, h=h: eng.activation(NK2[:, h:h + 1], NK2[:, h:h + 1], AF.Sqrt),
            inc=s_sc)
    sc = em("scalar", lambda eng, h=h: eng.activation(NQ2[:, h:h + 1], NQ2[:, h:h + 1], AF.Sqrt),
            inc=s_sc)
    sc = em("scalar", lambda eng, h=h: eng.activation(GPC[:, h:h + 1], ASIG[:, NH + h:NH + h + 1],
                                                      AF.Ln), inc=s_sc)
    wt("vector", s_sc, sc)

    def fc2(eng, h=h):
        eng.reciprocal(INVK[:, h:h + 1], NK2[:, h:h + 1])
        eng.tensor_scalar(INVK[:, h:h + 1], INVK[:, h:h + 1], INVCAP, None, ALU.min)
        eng.reciprocal(INVQ[:, h:h + 1], NQ2[:, h:h + 1])
        eng.tensor_scalar(INVQ[:, h:h + 1], INVQ[:, h:h + 1], INVCAP, None, ALU.min)
        eng.scalar_tensor_tensor(PACK[:, h:h + 1], LNK[:, h:h + 1], 0.5, gcol, ALU.mult, ALU.add)
        eng.tensor_add(GPC[:, h:h + 1], GPC[:, h:h + 1], gcol)
        eng.scalar_tensor_tensor(GPC[:, h:h + 1], LNK[:, h:h + 1], -0.5, GPC[:, h:h + 1],
                                 ALU.mult, ALU.add)
        eng.scalar_tensor_tensor(PACK[:, NH + h:NH + h + 1], LNQ[:, h:h + 1], -0.5, gcol,
                                 ALU.mult, ALU.add)
        eng.tensor_mul(PACK[:, 2 * NH + h:2 * NH + h + 1], gamc, INVQ[:, h:h + 1])
        eng.tensor_mul(BGIK[:, h:h + 1], beta, gamc)
        eng.tensor_mul(BGIK[:, h:h + 1], BGIK[:, h:h + 1], INVK[:, h:h + 1])
        return eng.tensor_mul(RTIK[:, h:h + 1], ratc, INVK[:, h:h + 1])
    v = em("vector", fc2, inc=s_ve)
    wt("tensor", s_ve, v)

    def bcast(col_ap, ps_out):
        wt("tensor", s_ve)
        m = em("tensor", lambda eng, col_ap=col_ap: eng.matmul(PS[6][0:1, 0:128], col_ap,
                                                               cs["idf"][:, :],
                                                               start=True, stop=True), inc=s_mm)
        wt("vector", s_mm, m)
        v = em("vector", lambda eng: eng.tensor_copy(RROW[0:1, 0:128], PS[6][0:1, 0:128]),
               inc=s_ve)
        wt("tensor", s_ve, v)
        return em("tensor", lambda eng, ps_out=ps_out:
                  eng.matmul(ps_out, cs["on1f"][0:1, :], RROW[0:1, 0:128],
                             start=True, stop=True), inc=s_mm)

    m = bcast(PACK[:, h:h + 1], PS[3][:, 0:128])
    wt("vector", s_mm, m)
    wt("vector", s_sc)

    def fearg(eng):
        eng.tensor_scalar(EARG[:, :], PS[3][:, 0:128], GPC[:, h:h + 1], -CLAMP,
                          ALU.subtract, ALU.max)
        return eng.tensor_tensor(EARG[:, :], EARG[:, :], cs["bias_nat"][:, :], ALU.add)
    v = em("vector", fearg, inc=s_ve)
    wt("scalar", s_ve, v)
    wt("tensor", s_ve, v)
    sc = em("scalar", lambda eng: eng.activation(DNAT[:, :], EARG[:, :], AF.Exp, scale=-1.0),
            inc=s_sc)
    m = em("tensor", lambda eng: eng.transpose(PS[4][:, 0:128], EARG[:, :], cs["idf"][:, :]),
           inc=s_mm)
    wt("scalar", s_mm, m)
    sc = em("scalar", lambda eng: eng.activation(DTRN[:, :], PS[4][:, 0:128], AF.Exp,
                                                 scale=-1.0), inc=s_sc)
    m = bcast(PACK[:, NH + h:NH + h + 1], PS[3][:, 0:128])
    wt("vector", s_mm, m)
    wt("vector", s_sc)

    def feargo(eng, h=h):
        eng.tensor_scalar(EARG[:, :], PS[3][:, 0:128], PACK[:, h:h + 1], CLAMP,
                          ALU.subtract, ALU.min)
        return eng.tensor_tensor(EARG[:, :], EARG[:, :], cs["bias_eot"][:, :],
                                 ALU.subtract)
    v = em("vector", feargo, inc=s_ve)
    wt("scalar", s_ve, v)
    sc = em("scalar", lambda eng: eng.activation(EOX[:, :], EARG[:, :], AF.Exp), inc=s_sc)

    wt("tensor", s_sc)
    em("tensor", lambda eng, kn=kn: eng.matmul(PS[5][:, 0:128], kn, kn, start=True, stop=True),
       inc=s_mm)
    m = em("tensor", lambda eng, kn=kn, qn=qn: eng.matmul(PS[5][:, 128:256], kn, qn,
                                                          start=True, stop=True), inc=s_mm)
    wt("vector", s_mm, m)

    wt("vector", s_sc)

    def fseed(eng):
        eng.scalar_tensor_tensor(PW_P[:, :], PS[5][:, 0:128], -1.0, DNAT[:, :],
                                 ALU.mult, ALU.mult)
        eng.scalar_tensor_tensor(PW_T[:, :], PS[5][:, 0:128], -1.0, DTRN[:, :],
                                 ALU.mult, ALU.mult)
        eng.scalar_tensor_tensor(EOT[:, :], PS[5][:, 128:256], 1.0, EOX[:, :],
                                 ALU.mult, ALU.mult)
        eng.tensor_scalar(RSOL[:, 0:128], Vt, ASIG[:, NH + h:NH + h + 1], None, ALU.mult)
        eng.tensor_scalar(RSOL[:, 128:256], Kt, BGIK[:, h:h + 1], None, ALU.mult)
        return eng.tensor_scalar(KTL[:, :], Kt, RTIK[:, h:h + 1], None, ALU.mult)
    v = em("vector", fseed, inc=s_ve)
    wt("tensor", s_ve, v)

    src = RSOL
    for j in range(7):
        wt("tensor", s_ve)
        wt("tensor", s_sc)
        em("tensor", lambda eng, src=src: eng.matmul(PS[7][:, 0:256], cs["idb"][:, :],
                                                     src[:, :], start=True, stop=False),
           inc=s_mm)
        m = em("tensor", lambda eng, src=src: eng.matmul(PS[7][:, 0:256], PW_T[:, :],
                                                         src[:, :], start=False, stop=True),
               inc=s_mm)
        if j < 6:
            em("tensor", lambda eng: eng.matmul(PS[6][:, 0:128], PW_T[:, :], PW_P[:, :],
                                                start=True, stop=True), inc=s_mm)
            m = em("tensor", lambda eng: eng.matmul(PS[6][:, 128:256], PW_P[:, :], PW_T[:, :],
                                                    start=True, stop=True), inc=s_mm)
        wt("scalar", s_mm, m)
        sc = em("scalar", lambda eng: eng.activation(XS[:, :], PS[7][:, 0:256], AF.Copy),
                inc=s_sc)
        if j < 6:
            sc = em("scalar", lambda eng: eng.activation(PWB[:, :], PS[6][:, 0:256],
                                                         AF.Copy), inc=s_sc)
        src = XS
    wt("tensor", s_sc)

    m = bcast(PACK[:, 2 * NH + h:2 * NH + h + 1], PS[3][:, 0:128])
    m2 = em("tensor", lambda eng: eng.matmul(PS[4][:, 0:128], XS[:, 128:256], EOT[:, :],
                                             start=True, stop=True), inc=s_mm)
    wt("vector", s_mm, m2)

    def fqt(eng, qn=qn):
        eng.tensor_tensor(TT1[:, :], qn, PS[3][:, 0:128], ALU.mult)
        return eng.tensor_tensor(QTT[:, :], TT1[:, :], PS[4][:, 0:128], ALU.subtract)
    v = em("vector", fqt, inc=s_ve)
    wt("tensor", s_ve, v)

    hA = HACC[:, h * 128:(h + 1) * 128]
    gA = GACC[:, h * 128:(h + 1) * 128]
    gAT = GACCT[:, h * 128:(h + 1) * 128]
    em("tensor", lambda eng: eng.matmul(PS[3][:, 0:128], XS[:, 0:128], EOT[:, :],
                                        start=True, stop=False), inc=s_mm)
    em("tensor", lambda eng, hA=hA: eng.matmul(PS[3][:, 0:128], hA, QTT[:, :],
                                               start=False, stop=True), inc=s_mm)
    m = em("tensor", lambda eng, gA=gA: eng.matmul(PS[4][:, 0:128], gA, QTT[:, :],
                                                   start=True, stop=True), inc=s_mm)
    wt("vector", s_mm, m)
    v = em("vector", lambda eng, h=h: eng.tensor_copy(SPF2[:, h * 128:(h + 1) * 128],
                                                      PS[3][:, 0:128]), inc=s_ve)
    v = em("vector", lambda eng, h=h: eng.tensor_copy(SPF1[:, h * 128:(h + 1) * 128],
                                                      PS[4][:, 0:128]), inc=s_ve)
    wt("tensor", s_ve, v)

    em("tensor", lambda eng: eng.matmul(PS[3][:, 0:128], KTL[:, :], XS[:, 0:128],
                                        start=True, stop=False), inc=s_mm)
    m = em("tensor", lambda eng: eng.matmul(PS[4][:, 0:128], XS[:, 128:256], KTL[:, :],
                                            start=True, stop=True), inc=s_mm)
    wt("vector", s_mm, m)

    def fgt(eng):
        eng.tensor_scalar(TT2[:, :], cs["idf"][:, :], gamCc, None, ALU.mult)
        return eng.scalar_tensor_tensor(GT_C[:, :], PS[4][:, 0:128], -1.0, TT2[:, :],
                                        ALU.mult, ALU.add)
    v = em("vector", fgt, inc=s_ve)
    wt("tensor", s_ve, v)
    em("tensor", lambda eng, hA=hA: eng.matmul(PS[3][:, 0:128], GT_C[:, :], hA,
                                               start=False, stop=True), inc=s_mm)
    em("tensor", lambda eng, gA=gA: eng.matmul(PS[5][:, 0:128], GT_C[:, :], gA,
                                               start=True, stop=True), inc=s_mm)
    m = em("tensor", lambda eng, gA=gA: eng.matmul(PS[5][:, 128:256], gA, GT_C[:, :],
                                                   start=True, stop=True), inc=s_mm)
    wt("vector", s_mm, m)
    v = em("vector", lambda eng, hA=hA: eng.tensor_copy(hA, PS[3][:, 0:128]), inc=s_ve)
    v = em("vector", lambda eng, gA=gA: eng.tensor_copy(gA, PS[5][:, 0:128]), inc=s_ve)
    v = em("vector", lambda eng, gAT=gAT: eng.tensor_copy(gAT, PS[5][:, 128:256]), inc=s_ve)
    wt("tensor", s_ve, v)
    wt("scalar", s_ve, v)


def emit_B(P, L):
    g = lambda n: L[n]
    em, wt, dm = P.em, P.wt, P.dma
    nc = g("nc")
    s_ld, s_wd, s_mm, s_ve, s_sc, s_gp, s_st = (g("s_ld"), g("s_wd"), g("s_mm"),
                                                g("s_ve"), g("s_sc"), g("s_gp"), g("s_st"))
    cs, PS = g("cs"), g("PS")
    XT, XN, JUNK = g("XT"), g("XN"), g("JUNK")
    TH, T_local, n_tg, n_ch = g("TH"), g("T_local"), g("n_tg"), g("n_ch")
    ln_stats, make_xn, ffn = g("ln_stats"), g("make_xn"), g("ffn")
    inp, out = g("inp"), g("out")
    cwg, cwo = g("cwg"), g("cwo")
    ZIN, GHS, GROW, OFIX = g("ZIN"), g("GHS"), g("GROW"), g("OFIX")
    BC1, x2scr = g("BC1"), g("x2scr")
    SPB1, SPB2, SPF1, SPF2 = g("SPB1"), g("SPB2"), g("SPF1"), g("SPF2")
    WSL = g("WSL")
    blocks = g("blocks")

    wt("vector", s_gp)
    wt("tensor", s_gp)
    wt("scalar", s_gp)

    em("vector", lambda eng: eng.memset(ZIN[:, :], 0.0), inc=s_ve)
    for j in range(NCORES - 1):
        base = j * 2 * NH * 128
        for h in range(NH):
            t = dm("gpsimd", GHS[:, 0:128],
                   inp["ghseq"][base + h * 128:base + (h + 1) * 128, :], s_gp)
            t = dm("gpsimd", GHS[:, 128:256],
                   inp["ghseq"][base + NH * 128 + h * 128:base + NH * 128 + (h + 1) * 128, :],
                   s_gp)
            wt("tensor", s_gp, t)
            wt("tensor", s_ve)
            em("tensor", lambda eng, h=h: eng.matmul(PS[1][:, 0:128], GHS[:, 0:128],
                                                     ZIN[:, h * 128:(h + 1) * 128],
                                                     start=True, stop=False), inc=s_mm)
            m = em("tensor", lambda eng: eng.matmul(PS[1][:, 0:128], cs["idb"][:, :],
                                                    GHS[:, 128:256], start=False, stop=True),
                   inc=s_mm)
            wt("vector", s_mm, m)
            v = em("vector", lambda eng, h=h: eng.tensor_copy(ZIN[:, h * 128:(h + 1) * 128],
                                                              PS[1][:, 0:128]), inc=s_ve)
            wt("gpsimd", s_ve, v)
    g("full_barrier")("zin")

    for tg in range(n_tg):
        g("full_barrier")("tg%d" % tg)
        xb = HALO + tg * 512
        n = min(512, T_local - tg * 512)
        ln_stats(xb, n)
        make_xn(xb, n)
        for dblk in range(DK):
            buf = WSL[dblk % 2]
            wt("scalar", s_sc)
            wt("scalar", s_ve)
            t = dm("scalar", buf[:, :], cwg.ap()[dblk, :, :], s_wd)
            wt("tensor", s_wd, t)
            wt("tensor", s_sc)

            def mmg(eng, buf=buf, n=n):
                last = None
                for k in range(DK):
                    last = eng.matmul(PS[1][:, 0:n], buf[:, k * 128:(k + 1) * 128],
                                      XN[:, k * 512:k * 512 + n],
                                      start=(k == 0), stop=(k == DK - 1))
                return last
            m = em("tensor", mmg, inc=s_mm)
            wt("scalar", s_mm, m)
            em("scalar", lambda eng, dblk=dblk, n=n:
               eng.activation(GROW[:, dblk * 512:dblk * 512 + n], PS[1][:, 0:n], AF.Silu),
               inc=s_sc)
        sc = P.n(s_sc)
        wt("tensor", s_sc, sc)
        wt("vector", s_sc, sc)
        for cc in range(4):
            ch = tg * 4 + cc
            if ch >= n_ch:
                break
            t = dm("gpsimd", SPB1[:, :].rearrange("p (h t) -> p h t", h=NH),
                   inp["spill_pt"][ch * NH * 128:(ch + 1) * NH * 128, :]
                   .rearrange("(h p) t -> p h t", p=128), s_gp)
            t = dm("gpsimd", SPB2[:, :].rearrange("p (h t) -> p h t", h=NH),
                   inp["spill_op"][ch * NH * 128:(ch + 1) * NH * 128, :]
                   .rearrange("(h p) t -> p h t", p=128), s_gp)
            wt("tensor", s_gp, t)
            for h in range(NH):
                em("tensor", lambda eng, h=h:
                   eng.matmul(PS[2][:, 0:128], ZIN[:, h * 128:(h + 1) * 128],
                              SPB1[:, h * 128:(h + 1) * 128], start=True, stop=False), inc=s_mm)
                m = em("tensor", lambda eng, h=h:
                       eng.matmul(PS[2][:, 0:128], cs["idb"][:, :],
                                  SPB2[:, h * 128:(h + 1) * 128], start=False, stop=True),
                       inc=s_mm)
                wt("vector", s_mm, m)
                v = em("vector", lambda eng, h=h, cc=cc:
                       eng.tensor_tensor(OFIX[:, h * 512 + cc * 128:h * 512 + (cc + 1) * 128],
                                         PS[2][:, 0:128],
                                         GROW[:, h * 512 + cc * 128:h * 512 + (cc + 1) * 128],
                                         ALU.mult), inc=s_ve)
                wt("tensor", s_ve, v)
            wt("gpsimd", s_ve)
        for dblk in range(DK):
            buf = WSL[dblk % 2]
            wt("scalar", s_ve)
            t = dm("scalar", buf[:, :], cwo.ap()[dblk, :, :], s_wd)
            wt("tensor", s_wd, t)

            def mmo(eng, buf=buf, n=n):
                last = None
                for k in range(DK):
                    last = eng.matmul(PS[1][:, 0:n], buf[:, k * 128:(k + 1) * 128],
                                      OFIX[:, k * 512:k * 512 + n],
                                      start=(k == 0), stop=(k == DK - 1))
                return last
            m = em("tensor", mmo, inc=s_mm)
            wt("vector", s_mm, m)

            wt("sync", s_ve)
            t = dm("sync", BC1[:, 0:n], inp["x1T"][dblk * 128:(dblk + 1) * 128,
                                                    xb:xb + n], s_ld)
            wt("vector", s_ld, t)

            def fr(eng, dblk=dblk, xb=xb, n=n):
                xx = XT[:, dblk * TH + xb:dblk * TH + xb + n]
                eng.tensor_add(JUNK[:, 0:n], BC1[:, 0:n], PS[1][:, 0:n])
                return eng.tensor_copy(xx, JUNK[:, 0:n])
            v = em("vector", fr, inc=s_ve)
            wt("sync", s_ve, v)
            dm("sync", x2scr.ap()[dblk * 128:(dblk + 1) * 128, xb - HALO:xb - HALO + n],
               JUNK[:, 0:n], s_st)
            wt("vector", s_st)
            wt("tensor", s_ve, v)
        g("full_barrier")("wo_tg%d" % tg)
        ln_stats(xb, n)
        make_xn(xb, n)
        ffn(xb, n, out_store=out["yT"])


_T_OVERRIDE = None
_NCORES_OVERRIDE = None


def kernel(L0_attn_x, L0_attn_W_qkv, L0_attn_conv_w, L0_attn_W_ab, L0_attn_b_ab,
           L0_attn_W_g, L0_attn_W_o, L0_ffn_w1, L0_ffn_w2, L0_ffn_w3):
    ncores = _NCORES_OVERRIDE or NCORES
    T = L0_attn_x.shape[1]
    Tl = _T_OVERRIDE or (T // ncores)
    x0T = np.ascontiguousarray(np.asarray(L0_attn_x)[0].T)
    w1T = np.ascontiguousarray(np.asarray(L0_ffn_w1).T)
    w2T = np.ascontiguousarray(np.asarray(L0_ffn_w2).T)
    w3T = np.ascontiguousarray(np.asarray(L0_ffn_w3).T)
    wqkvT = np.ascontiguousarray(np.asarray(L0_attn_W_qkv).T)
    wabT = np.ascontiguousarray(np.asarray(L0_attn_W_ab).T)
    wgT = np.ascontiguousarray(np.asarray(L0_attn_W_g).T)
    woT = np.ascontiguousarray(np.asarray(L0_attn_W_o).T)
    convw = np.ascontiguousarray(np.asarray(L0_attn_conv_w))
    bab = np.ascontiguousarray(np.asarray(L0_attn_b_ab)[None, :]).astype(np.float32)

    zh = np.zeros((D, HALO), dtype=np.float32)
    in_a = []
    for c in range(ncores):
        lo = c * Tl
        if lo >= HALO:
            xs = x0T[:, lo - HALO:lo + Tl]
        else:
            xs = np.concatenate([zh[:, :HALO - lo], x0T[:, 0:lo + Tl]], axis=1)
        in_a.append(dict(xT=np.ascontiguousarray(xs), w1T=w1T, w2T=w2T, w3T=w3T,
                         wqkvT=wqkvT, wabT=wabT, convw=convw, bab=bab))

    nc_a = build_program("A", Tl)
    res_a = run_bass_kernel_spmd(nc_a, in_a, core_ids=list(range(ncores)))

    ident = np.tile(np.eye(128, dtype=np.float32), (NH, 1))
    zero = np.zeros((NH * 128, 128), dtype=np.float32)
    in_b = []
    for c in range(ncores):
        seq = []
        for j in range(NCORES - 1):
            if j < c:
                seq.append(res_a.results[j]["ght_out"])
                seq.append(res_a.results[j]["h_out"])
            else:
                seq.append(ident)
                seq.append(zero)
        in_b.append(dict(x1T=res_a.results[c]["x1T_out"], w1T=w1T, w2T=w2T, w3T=w3T,
                         wgT=wgT, woT=woT,
                         spill_pt=res_a.results[c]["spill_pt"],
                         spill_op=res_a.results[c]["spill_op"],
                         ghseq=np.ascontiguousarray(np.concatenate(seq, axis=0))))

    nc_b = build_program("B", Tl)
    res_b = run_bass_kernel_spmd(nc_b, in_b, core_ids=list(range(ncores)))

    outv = np.concatenate([res_b.results[c]["yT"].T for c in range(ncores)], axis=0)
    return outv[None, :, :].astype(np.float32)


# version 04
# speedup vs baseline: 1.3376x; 1.1452x over previous
import numpy as np
import ml_dtypes
import concourse.bass as bass
import concourse.mybir as mybir
from concourse.bass_utils import run_bass_kernel_spmd

F32 = mybir.dt.float32
BF16 = mybir.dt.bfloat16
I32 = mybir.dt.int32
AF = mybir.ActivationFunctionType
ALU = mybir.AluOpType
AXX = mybir.AxisListType.X
BF = ml_dtypes.bfloat16

D = 2048
DK = 16
HFF = 5632
CFF = 44
NH = 16
QKV = 6144
CQK = 48
HALO = 64
C = 128
EPS_LN = 1e-5
EPS_Q = 1e-8
NCORES = 8
CLAMP = 80.0
INVCAP = 1e12



class Em:

    ENGINES = ("sync", "tensor", "vector", "scalar", "gpsimd")

    CHAINED = ("vector", "scalar", "gpsimd")
    CHAIN_OPS = frozenset("""tensor_copy tensor_tensor tensor_add tensor_sub
        tensor_mul tensor_max tensor_scalar tensor_single_scalar
        tensor_scalar_add tensor_scalar_sub tensor_scalar_mul tensor_scalar_max
        tensor_scalar_min tensor_relu tensor_reduce reduce_max reduce_sum
        scalar_tensor_tensor tensor_tensor_scan memset memzero reciprocal
        activation copy add mul sqrt square sign select bn_stats bn_aggr
        transpose pool pool_avg pool_max stream_shuffle""".split())

    class _MockInst:
        def then_inc(self, *a, **kw):
            return self

    class _CountEng:

        def __init__(self):
            self.n = 0

        def __getattr__(self, nm):
            if nm in Em.CHAIN_OPS:
                def c(*a, **kw):
                    self.n += 1
                    return Em._MockInst()
                return c
            return lambda *a, **kw: Em._MockInst()

    class _ChainEng:

        def __init__(self, eng, sem_h):
            self._e = eng
            self._h = sem_h
            self._n = 0

        def __getattr__(self, nm):
            attr = getattr(self._e, nm)
            if nm in Em.CHAIN_OPS:
                def call(*a, _attr=attr, **kw):
                    if self._n:
                        self._e.wait_ge(self._h, self._n)
                    inst = _attr(*a, **kw)
                    inst.then_inc(self._h, 1)
                    self._n += 1
                    return inst
                return call
            return attr

    def __init__(self, nc):
        self.nc = nc
        self.q = {e: [] for e in self.ENGINES}
        self._s = {}
        self.own = {}

    def sem(self, name):
        if name not in self._s:
            self._s[name] = [self.nc.alloc_semaphore(name), 0]
        return name

    def n(self, s):
        return self._s[s][1]

    def em(self, engine, fn, inc=None, k=1):
        if engine in self.CHAINED:
            mock = Em._CountEng()
            fn(mock)
            if mock.n > 0:
                own = self.own[engine]
                assert inc is None or inc == own, (engine, inc)
                self._s[own][1] += mock.n
                self.q[engine].append(fn)
                return self._s[own][1]
        if inc is None:
            self.q[engine].append(fn)
            return None
        h = self._s[inc][0]
        self._s[inc][1] += k

        def f2(eng, fn=fn, h=h, k=k):
            fn(eng).then_inc(h, k)
        self.q[engine].append(f2)
        return self._s[inc][1]

    def wt(self, engine, sem, val=None):
        v = self.n(sem) if val is None else val
        h = self._s[sem][0]
        self.q[engine].append(lambda eng, h=h, v=v: eng.wait_ge(h, v))

    def dma(self, engine, out, in_, sem, **kw):
        h = self._s[sem][0]
        self._s[sem][1] += 16

        def f(eng, out=out, in_=in_, h=h, kw=kw):
            eng.dma_start(out=out, in_=in_, **kw).then_inc(h, 16)
        self.q[engine].append(f)
        return self._s[sem][1]

    def barrier(self, tag):
        b = self.sem("bar_" + tag)
        for e in self.ENGINES:
            self.em(e, lambda eng: eng.nop(), inc=b)
        tgt = self.n(b)
        for e in self.ENGINES:
            self.wt(e, b, tgt)

    def run(self):
        with self.nc.Block() as block:
            for e in self.ENGINES:
                fns = self.q[e]
                own = self.own.get(e)

                def body(eng, fns=fns, own=own):
                    if own is not None:
                        eng = Em._ChainEng(eng, self._s[own][0])
                    for fn in fns:
                        fn(eng)
                getattr(block, e)(body)



def build_program(mode, T_local):
    nc = bass.Bass("TRN2", target_bir_lowering=False, debug=False)
    TH = T_local + HALO
    n_tg = (T_local + 511) // 512
    n_ch = T_local // C

    inp = {}

    def din(nm, shp):
        inp[nm] = nc.dram_tensor(nm, shp, F32, kind="ExternalInput").ap()
        return inp[nm]

    if mode == "A":
        din("xT", [D, TH])
        din("wqkvT", [D, QKV]); din("wabT", [D, 32])
        din("convw", [QKV, 4]); din("bab", [1, 32])
        din("wgT", [D, D]); din("woT", [D, D])
        din("w1T", [D, HFF]); din("w2T", [D, HFF]); din("w3T", [HFF, D])
    else:
        din("x1T", [D, TH])
        din("spill_pt", [n_ch * NH * 128, 128])
        din("spill_op", [n_ch * NH * 128, 128])
        din("ghseq", [(NCORES - 1) * 2 * NH * 128, 128])

    out = {}
    if mode == "A":
        out["x1T_out"] = nc.dram_tensor("x1T_out", [D, TH], F32, kind="ExternalOutput").ap()
        out["spill_pt"] = nc.dram_tensor("spill_pt", [n_ch * NH * 128, 128], F32, kind="ExternalOutput").ap()
        out["spill_op"] = nc.dram_tensor("spill_op", [n_ch * NH * 128, 128], F32, kind="ExternalOutput").ap()
        out["ght_out"] = nc.dram_tensor("ght_out", [NH * 128, 128], F32, kind="ExternalOutput").ap()
        out["h_out"] = nc.dram_tensor("h_out", [NH * 128, 128], F32, kind="ExternalOutput").ap()
    else:
        out["yT"] = nc.dram_tensor("yT", [D, T_local], F32, kind="ExternalOutput").ap()

    ckind = "ExternalOutput" if mode == "A" else "ExternalInput"
    cw1 = nc.dram_tensor("cw1", [CFF, 128, DK * 128], BF16, kind=ckind)
    cw2 = nc.dram_tensor("cw2", [CFF, 128, DK * 128], BF16, kind=ckind)
    cw3 = nc.dram_tensor("cw3", [DK, 128, CFF * 128], BF16, kind=ckind)
    cwg = nc.dram_tensor("cwg", [DK, 128, DK * 128], BF16, kind=ckind)
    cwo = nc.dram_tensor("cwo", [DK, 128, DK * 128], BF16, kind=ckind)
    if mode == "A":
        cqkv = nc.dram_tensor("cqkv", [CQK, 128, DK * 128], BF16, kind="Internal")
        cab = nc.dram_tensor("cab", [DK, 128, 32], BF16, kind="Internal")
        out["cw1"] = cw1.ap(); out["cw2"] = cw2.ap(); out["cw3"] = cw3.ap()
        out["cwg"] = cwg.ap(); out["cwo"] = cwo.ap()
    else:
        x2scr = nc.dram_tensor("x2scr", [D, T_local], F32, kind="Internal")

    KONST = {
        "idb": np.eye(128).astype(BF),
        "idf": np.eye(128, dtype=np.float32),
        "on1b": np.ones((1, 128)).astype(BF),
        "on1f": np.ones((1, 128), dtype=np.float32),
        "oncf": np.ones((128, 1), dtype=np.float32),
        "oncb": np.ones((128, 1)).astype(BF),
        "trii": np.triu(np.ones((128, 128), dtype=np.float32)),
        "trir": np.tril(np.ones((128, 128), dtype=np.float32), -1),
        "bias_nat": 1e6 * np.triu(np.ones((128, 128), dtype=np.float32)),
        "bias_eot": 1e6 * np.tril(np.ones((128, 128), dtype=np.float32), -1),
    }
    ktens = {nm: nc.inline_tensor(v, "k_" + nm) for nm, v in KONST.items()}
    cs = {}
    for nm, v in KONST.items():
        cs[nm] = nc.alloc_sbuf_tensor("c_" + nm, list(v.shape),
                                      BF16 if v.dtype == BF else F32)

    sbB = lambda n, s: nc.alloc_sbuf_tensor(n, s, BF16)
    sbF = lambda n, s: nc.alloc_sbuf_tensor(n, s, F32)

    XT = sbB("XT", [128, DK * TH])
    XN = sbB("XN", [128, DK * 512])
    RGN = nc.alloc_sbuf_tensor("RGN", [128, 45056 // 4], F32)
    off = nc.lookup_mloc(RGN).addr
    HS = nc.alloc_sbuf_tensor_at("HS", [128, CFF * 512], BF16, offset=off)
    WQF = [nc.alloc_sbuf_tensor_at("WQF%d" % i, [128, 2816], F32, offset=off + i * 11264)
           for i in range(2)]
    WQB = [nc.alloc_sbuf_tensor_at("WQB%d" % i, [128, 2816], BF16,
                                   offset=off + 22528 + i * 5632) for i in range(2)]
    WQC = [nc.alloc_sbuf_tensor_at("WQC%d" % i, [128, 2816], BF16,
                                   offset=off + 33792 + i * 5632) for i in range(2)]
    SPF1 = nc.alloc_sbuf_tensor_at("SPF1", [128, NH * 128], F32, offset=off)
    SPF2 = nc.alloc_sbuf_tensor_at("SPF2", [128, NH * 128], F32, offset=off + 8192)

    W3S0 = sbB("W3S0", [128, CFF * 128])
    W3S1 = sbB("W3S1", [128, CFF * 128])
    WSL = [W3S0[:, 0:DK * 128], W3S0[:, DK * 128:2 * DK * 128]]
    WSL2 = [W3S1[:, 0:DK * 128], W3S1[:, DK * 128:2 * DK * 128]]
    W3S = [W3S0, W3S1]

    STAT = sbF("STAT", [1, 2048])
    ROWS = sbF("ROWS", [1, 128])
    GINV = sbF("GINV", [128, 4])
    GCOLS = sbF("GCOLS", [128, 4])
    BC1 = sbF("BC1", [128, 512])
    BC2 = sbF("BC2", [128, 512])
    SGT = sbF("SGT", [128, 512])
    JUNK = sbF("JUNK", [128, 512])

    if mode == "A":
        QKVC = sbB("QKVC", [128, CQK * 128])
        QKVP = sbB("QKVP", [128, CQK * 4])
        CONVW = sbF("CONVW", [128, CQK * 4])
        BABB = sbF("BABB", [128, 32])
        WAB = sbB("WAB", [128, DK * 32])
        TRN = sbB("TRN", [128, 3 * 128])
        XS = sbB("XS", [128, 256])
        PWB = sbB("PWB", [128, 256])
        PW_P = PWB[:, 0:128]
        PW_T = PWB[:, 128:256]
        EOT = sbB("EOT", [128, 128])
        QTT = sbB("QTT", [128, 128])
        RSOL = sbB("RSOL", [128, 256])
        KTL = sbB("KTL", [128, 128])
        GT_C = sbB("GT_C", [128, 128])
        GACC = sbB("GACC", [128, NH * 128])
        HACC = sbB("HACC", [128, NH * 128])
        GACCT = sbB("GACCT", [128, NH * 128])
        DNAT = sbF("DNAT", [128, 128])
        DTRN = sbF("DTRN", [128, 128])
        EOX = sbF("EOX", [128, 128])
        EARG = sbF("EARG", [128, 128])
        TT1 = sbF("TT1", [128, 128])
        TT2 = sbF("TT2", [128, 128])
        ABR = sbF("ABR", [32, 128])
        ASIG = sbF("ASIG", [128, 32])
        GAMS = sbF("GAMS", [128, 7 * NH])
        NK2 = sbF("NK2", [128, NH]); NQ2 = sbF("NQ2", [128, NH])
        LNK = sbF("LNK", [128, NH]); LNQ = sbF("LNQ", [128, NH])
        INVK = sbF("INVK", [128, NH]); INVQ = sbF("INVQ", [128, NH])
        GPC = sbF("GPC", [128, NH])
        BGIK = sbF("BGIK", [128, NH]); RTIK = sbF("RTIK", [128, NH])
        PACK = sbF("PACK", [128, 3 * NH])
        RROW = sbF("RROW", [1, 128])
    else:
        ZIN = sbB("ZIN", [128, NH * 128])
        GHS = sbB("GHS", [128, 2 * 128])
        GROW = sbB("GROW", [128, DK * 512])
        OFIX = sbB("OFIX", [128, NH * 512])
        SPB1 = nc.alloc_sbuf_tensor_at("SPB1", [128, NH * 128], BF16, offset=off + 16384)
        SPB2 = nc.alloc_sbuf_tensor_at("SPB2", [128, NH * 128], BF16, offset=off + 20480)

    PSB0 = nc.alloc_psum_tensor("PSB0", [128, 1024], BF16)
    PS = [None] + [nc.alloc_psum_tensor("PS%d" % i, [128, 512], F32) for i in range(1, 8)]

    P = Em(nc)
    s_ld = P.sem("s_ld"); s_wd = P.sem("s_wd"); s_mm = P.sem("s_mm")
    s_ve = P.sem("s_ve"); s_sc = P.sem("s_sc"); s_gp = P.sem("s_gp")
    s_st = P.sem("s_st"); s_fb = P.sem("s_fb")
    s_w2 = P.sem("s_w2"); s_l2 = P.sem("s_l2"); s_bc = P.sem("s_bc")
    s_g1 = P.sem("s_g1"); s_g2 = P.sem("s_g2")
    s_q1 = P.sem("s_q1"); s_q2 = P.sem("s_q2")
    P.own = {"vector": s_ve, "scalar": s_sc, "gpsimd": s_gp}

    def full_barrier(tag):
        wt("tensor", s_mm)
        wt("vector", s_ve)
        wt("scalar", s_sc)
        wt("scalar", s_wd)
        wt("gpsimd", s_gp)
        wt("sync", s_ld)
        wt("sync", s_st)
        P.barrier(tag)
    em, wt, dm = P.em, P.wt, P.dma

    for nm in KONST:
        dm("sync", cs[nm][:, :], ktens[nm].ap()[:, :], s_ld)
    xin = inp["xT"] if mode == "A" else inp["x1T"]
    for k in range(DK):
        dm("gpsimd", XT[:, k * TH:(k + 1) * TH], xin[k * 128:(k + 1) * 128, :], s_gp)
    if mode == "A":
        dm("sync", BABB[0:1, 0:32], inp["bab"][:, :], s_ld)
        for t in range(CQK):
            dm("sync", CONVW[:, t * 4:(t + 1) * 4], inp["convw"][t * 128:(t + 1) * 128, :], s_ld)

    def stream_quant():
        wjobs = [(inp["w1T"], DK, HFF, cw1, 0), (inp["w2T"], DK, HFF, cw2, 1),
                 (inp["w3T"], CFF, D, cw3, 2)]
        tiles = []
        for w, ntile, wid, cache, wi in wjobs:
            nh_ = (wid + 2815) // 2816
            for k in range(ntile):
                for hh in range(nh_):
                    c0 = hh * 2816
                    tiles.append((w, cache, wi, k, c0, min(2816, wid - c0)))
        QE = ("scalar", "sync")
        SE = ("sync", "scalar")
        LS = (s_wd, s_ld)
        SS = (s_q1, s_q2)
        red_t = [0] * len(tiles)
        for ti, (w, cache, wi, k, c0, n) in enumerate(tiles):
            q = ti % 2
            if ti >= 2:
                wt(QE[q], s_ve, red_t[ti - 2])
            t = dm(QE[q], WQF[q][:, 0:n], w[k * 128:(k + 1) * 128, c0:c0 + n], LS[q])
            wt("vector", LS[q], t)
            red_t[ti] = em("vector", lambda eng, n=n, ti=ti, q=q:
                           eng.reduce_sum(JUNK[:, ti:ti + 1], WQF[q][:, 0:n], axis=AXX,
                                          apply_absolute_value=True), inc=s_ve)
        base = 0
        for w, ntile, wid, cache, wi in wjobs:
            cnt = ntile * ((wid + 2815) // 2816)
            em("vector", lambda eng, base=base, cnt=cnt, wi=wi:
               eng.reduce_sum(GCOLS[:, wi:wi + 1], JUNK[:, base:base + cnt], axis=AXX),
               inc=s_ve)
            base += cnt
        wt("tensor", s_ve)
        for wi in range(3):
            em("tensor", lambda eng, wi=wi:
               eng.matmul(PS[7][0:1, wi:wi + 1], GCOLS[:, wi:wi + 1], cs["oncf"][:, :],
                          start=True, stop=True), inc=s_mm)
        wt("vector", s_mm)

        def finv(eng):
            eng.tensor_scalar(ROWS[0:1, 0:3], PS[7][0:1, 0:3], 1.0 / (HFF * D), EPS_Q,
                              ALU.mult, ALU.add)
            return eng.reciprocal(ROWS[0:1, 0:3], ROWS[0:1, 0:3])
        v = em("vector", finv, inc=s_ve)
        wt("tensor", s_ve, v)
        m = em("tensor", lambda eng: eng.matmul(PS[7][:, 4:7], cs["on1f"][0:1, :],
                                                ROWS[0:1, 0:3], start=True, stop=True), inc=s_mm)
        wt("vector", s_mm, m)
        em("vector", lambda eng: eng.tensor_copy(GINV[:, 0:3], PS[7][:, 4:7]), inc=s_ve)

        cjobs = [(inp["wqkvT"], cqkv, DK, QKV), (inp["wabT"], cab, DK, 32),
                 (inp["wgT"], cwg, DK, D), (inp["woT"], cwo, DK, D)]
        for w, cache, ntile, wid in cjobs:
            nh_ = (wid + 2815) // 2816
            for k in range(ntile):
                for hh in range(nh_):
                    c0 = hh * 2816
                    tiles.append((w, cache, None, k, c0, min(2816, wid - c0)))
        fq_t = [0] * len(tiles)
        st_t = [0] * len(tiles)
        for ti, (w, cache, wi, k, c0, n) in enumerate(tiles):
            q = ti % 2
            if ti >= 2:
                wt(QE[q], s_ve, fq_t[ti - 2])
            else:
                wt(QE[q], s_ve)
            t = dm(QE[q], WQF[q][:, 0:n], w[k * 128:(k + 1) * 128, c0:c0 + n], LS[q])
            wt("vector", LS[q], t)
            if ti >= 2:
                wt("vector", SS[q], st_t[ti - 2])

            if wi is not None:
                def fop(eng, n=n, wi=wi, q=q):
                    eng.tensor_scalar(WQF[q][:, 0:n], WQF[q][:, 0:n], GINV[:, wi:wi + 1],
                                      None, ALU.mult)
                    eng.tensor_scalar(WQC[q][:, 0:n], WQF[q][:, 0:n], -0.5, None, ALU.is_le)
                    eng.tensor_scalar(WQB[q][:, 0:n], WQF[q][:, 0:n], 0.5, None, ALU.is_ge)
                    return eng.tensor_tensor(WQB[q][:, 0:n], WQB[q][:, 0:n], WQC[q][:, 0:n],
                                             ALU.subtract)
            else:
                def fop(eng, n=n, q=q):
                    return eng.tensor_copy(WQB[q][:, 0:n], WQF[q][:, 0:n])
            fq_t[ti] = em("vector", fop, inc=s_ve)
            wt(SE[q], s_ve, fq_t[ti])
            if n >= 128:
                for b in range(n // 128):
                    st_t[ti] = dm(SE[q], cache.ap()[(c0 // 128) + b, :, k * 128:(k + 1) * 128],
                                  WQB[q][:, b * 128:(b + 1) * 128], SS[q])
            else:
                st_t[ti] = dm(SE[q], cache.ap()[k, :, 0:n], WQB[q][:, 0:n], SS[q])

    if mode == "A":
        stream_quant()
        for e in ("scalar", "sync", "vector", "gpsimd", "tensor"):
            wt(e, s_q1)
            wt(e, s_q2)
    full_barrier("p0")

    def ln_stats(xbase, ncols):
        def mm_s1(eng, xbase=xbase, ncols=ncols):
            last = None
            for k in range(DK):
                last = eng.matmul(PS[5][0:1, 0:ncols], cs["oncb"][:, :],
                                  XT[:, k * TH + xbase:k * TH + xbase + ncols],
                                  start=(k == 0), stop=(k == DK - 1))
            return last
        em("tensor", mm_s1, inc=s_mm)
        wt("scalar", s_ve)
        wt("scalar", s_st)
        wt("scalar", s_gp)
        JB = (JUNK, SGT)
        sq_t = [0] * DK
        mmq_t = [0] * DK
        for k in range(DK):
            if k >= 2:
                wt("scalar", s_mm, mmq_t[k - 2])
            sq_t[k] = em("scalar", lambda eng, k=k, xbase=xbase, ncols=ncols:
                         eng.activation(JB[k % 2][:, 0:ncols],
                                        XT[:, k * TH + xbase:k * TH + xbase + ncols],
                                        AF.Square), inc=s_sc)
            wt("tensor", s_sc, sq_t[k])
            mmq_t[k] = em("tensor", lambda eng, k=k, ncols=ncols:
                          eng.matmul(PS[6][0:1, 0:ncols], cs["oncf"][:, :],
                                     JB[k % 2][:, 0:ncols],
                                     start=(k == 0), stop=(k == DK - 1)), inc=s_mm)
        wt("vector", s_mm)

        def fstat(eng, ncols=ncols):
            mu = STAT[0:1, 0:ncols]
            var = STAT[0:1, 512:512 + ncols]
            t1 = STAT[0:1, 1024:1024 + ncols]
            eng.tensor_scalar(mu, PS[5][0:1, 0:ncols], 1.0 / D, None, ALU.mult)
            eng.tensor_scalar(var, PS[6][0:1, 0:ncols], 1.0 / D, None, ALU.mult)
            eng.tensor_tensor(t1, mu, mu, ALU.mult)
            eng.tensor_sub(var, var, t1)
            return eng.tensor_scalar(var, var, EPS_LN, None, ALU.add)
        v = em("vector", fstat, inc=s_ve)
        wt("scalar", s_ve, v)
        sq = em("scalar", lambda eng, ncols=ncols:
                eng.activation(STAT[0:1, 512:512 + ncols], STAT[0:1, 512:512 + ncols],
                               AF.Sqrt), inc=s_sc)
        wt("vector", s_sc, sq)
        em("vector", lambda eng, ncols=ncols:
           eng.reciprocal(STAT[0:1, 512:512 + ncols], STAT[0:1, 512:512 + ncols]), inc=s_ve)

    def make_xn(xbase, ncols):
        wt("tensor", s_ve)
        em("tensor", lambda eng, ncols=ncols:
           eng.matmul(PS[5][:, 0:ncols], cs["on1f"][0:1, :], STAT[0:1, 0:ncols],
                      start=True, stop=True), inc=s_mm)
        m = em("tensor", lambda eng, ncols=ncols:
               eng.matmul(PS[6][:, 0:ncols], cs["on1f"][0:1, :], STAT[0:1, 512:512 + ncols],
                          start=True, stop=True), inc=s_mm)
        wt("vector", s_mm, m)
        wt("vector", s_ld)
        em("vector", lambda eng, ncols=ncols:
           eng.tensor_copy(BC1[:, 0:ncols], PS[5][:, 0:ncols]), inc=s_ve)
        em("vector", lambda eng, ncols=ncols:
           eng.tensor_copy(BC2[:, 0:ncols], PS[6][:, 0:ncols]), inc=s_ve)
        for k in range(DK):
            def f(eng, k=k, xbase=xbase, ncols=ncols):
                xx = XT[:, k * TH + xbase:k * TH + xbase + ncols]
                xn = XN[:, k * 512:k * 512 + ncols]
                eng.tensor_sub(xn, xx, BC1[:, 0:ncols])
                return eng.tensor_mul(xn, xn, BC2[:, 0:ncols])
            em("vector", f, inc=s_ve)
        v = P.n(s_ve)
        wt("tensor", s_ve, v)
        wt("scalar", s_ve, v)

    def ffn(xbase, ncols, out_store=None, x1_store=None):
        mm_t = [0] * CFF
        sc_t = [0] * CFF
        ve_t = [0] * CFF
        for cblk in range(CFF):
            bufA, bufB = WSL[cblk % 2], WSL2[cblk % 2]
            bA, bB = PS[1 + cblk % 2], PS[3 + cblk % 2]
            if cblk >= 2:
                wt("scalar", s_mm, mm_t[cblk - 2])
                wt("sync", s_mm, mm_t[cblk - 2])
            else:
                wt("scalar", s_mm)
                wt("sync", s_mm)
            sw = s_wd if cblk % 2 == 0 else s_w2
            sl = s_ld if cblk % 2 == 0 else s_l2
            wd = dm("scalar", bufA[:, :], cw1.ap()[cblk, :, :], sw)
            ldv = dm("sync", bufB[:, :], cw2.ap()[cblk, :, :], sl)
            wt("tensor", sw, wd)
            wt("tensor", sl, ldv)
            if cblk >= 2:
                wt("tensor", s_sc, sc_t[cblk - 2])
                wt("tensor", s_ve, ve_t[cblk - 2])

            def mms(eng, bufA=bufA, bufB=bufB, bA=bA, bB=bB, ncols=ncols):
                for k in range(DK):
                    eng.matmul(bA[:, 0:ncols], bufA[:, k * 128:(k + 1) * 128],
                               XN[:, k * 512:k * 512 + ncols],
                               start=(k == 0), stop=(k == DK - 1))
                last = None
                for k in range(DK):
                    last = eng.matmul(bB[:, 0:ncols], bufB[:, k * 128:(k + 1) * 128],
                                      XN[:, k * 512:k * 512 + ncols],
                                      start=(k == 0), stop=(k == DK - 1))
                return last
            mm_t[cblk] = em("tensor", mms, inc=s_mm)
            wt("scalar", s_mm, mm_t[cblk])
            if cblk >= 1:
                wt("scalar", s_ve, ve_t[cblk - 1])
            sc_t[cblk] = em("scalar", lambda eng, bA=bA, ncols=ncols:
                            eng.activation(SGT[:, 0:ncols], bA[:, 0:ncols], AF.Sigmoid),
                            inc=s_sc)
            wt("vector", s_sc, sc_t[cblk])
            ve_t[cblk] = em("vector", lambda eng, bB=bB, cblk=cblk, ncols=ncols:
                            eng.tensor_tensor(HS[:, cblk * 512:cblk * 512 + ncols],
                                              SGT[:, 0:ncols], bB[:, 0:ncols], ALU.mult),
                            inc=s_ve)
        m3_t = [0] * DK
        v3_t = [0] * DK
        g3_t = [0] * DK
        JB = (JUNK, SGT)
        for dblk in range(DK):
            buf3 = W3S[dblk % 2]
            b3 = PS[1 + dblk % 2]
            if dblk >= 2:
                wt("scalar" if dblk % 2 == 0 else "sync", s_mm, m3_t[dblk - 2])
            else:
                wt("scalar" if dblk % 2 == 0 else "sync", s_mm, mm_t[CFF - 1])
            wd = dm("scalar" if dblk % 2 == 0 else "sync", buf3[:, :],
                    cw3.ap()[dblk, :, :], s_wd if dblk % 2 == 0 else s_ld)
            wt("tensor", s_wd if dblk % 2 == 0 else s_ld, wd)
            if dblk == 0:
                wt("tensor", s_ve, ve_t[CFF - 1])
                wt("tensor", s_sc, sc_t[CFF - 1])
            if dblk >= 2:
                wt("tensor", s_ve, v3_t[dblk - 2])

            def mm3(eng, buf3=buf3, b3=b3, ncols=ncols):
                last = None
                for k in range(CFF):
                    last = eng.matmul(b3[:, 0:ncols], buf3[:, k * 128:(k + 1) * 128],
                                      HS[:, k * 512:k * 512 + ncols],
                                      start=(k == 0), stop=(k == CFF - 1))
                return last
            m3_t[dblk] = em("tensor", mm3, inc=s_mm)
            wt("vector", s_mm, m3_t[dblk])
            jb = JB[dblk % 2]
            sg = s_g1 if dblk % 2 == 0 else s_g2
            if dblk >= 2:
                wt("vector", sg, g3_t[dblk - 2])
            if out_store is None:
                def fr(eng, b3=b3, dblk=dblk, xbase=xbase, ncols=ncols, jb=jb):
                    xx = XT[:, dblk * TH + xbase:dblk * TH + xbase + ncols]
                    eng.scalar_tensor_tensor(jb[:, 0:ncols], b3[:, 0:ncols], 0.5, xx,
                                             ALU.mult, ALU.add)
                    return eng.tensor_copy(xx, jb[:, 0:ncols])
                v3_t[dblk] = em("vector", fr, inc=s_ve)
                if x1_store is not None:
                    wt("gpsimd", s_ve, v3_t[dblk])
                    g3_t[dblk] = dm("gpsimd", x1_store[dblk * 128:(dblk + 1) * 128,
                                                       xbase:xbase + ncols],
                                    jb[:, 0:ncols], sg)
                else:
                    g3_t[dblk] = P.n(sg)
            else:
                wt("sync", s_ve)
                t = dm("sync", BC1[:, 0:ncols],
                       x2scr.ap()[dblk * 128:(dblk + 1) * 128,
                                  xbase - HALO:xbase - HALO + ncols], s_bc)
                wt("vector", s_bc, t)

                def fr(eng, b3=b3, ncols=ncols, jb=jb):
                    return eng.scalar_tensor_tensor(jb[:, 0:ncols], b3[:, 0:ncols], 0.5,
                                                    BC1[:, 0:ncols], ALU.mult, ALU.add)
                v3_t[dblk] = em("vector", fr, inc=s_ve)
                wt("gpsimd", s_ve, v3_t[dblk])
                g3_t[dblk] = dm("gpsimd", out_store[dblk * 128:(dblk + 1) * 128,
                                                    xbase - HALO:xbase - HALO + ncols],
                                jb[:, 0:ncols], sg)
        wt("vector", s_g1)
        wt("vector", s_g2)
        wt("scalar", s_g1)
        wt("scalar", s_g2)

    blocks = [(0, HALO)] + [(HALO + g * 512, min(512, T_local - g * 512)) for g in range(n_tg)]

    if mode == "A":
        wt("vector", s_gp)
        wt("tensor", s_gp)
        wt("scalar", s_gp)
        for (xb, n) in blocks:
            ln_stats(xb, n)
            make_xn(xb, n)
            ffn(xb, n, x1_store=out["x1T_out"])
        full_barrier("ffn1")

        emit_mixer_A(P, locals())
        wt("sync", s_st)
        wt("sync", s_gp)
        full_barrier("endA")
    else:
        emit_B(P, locals())
        wt("sync", s_st)
        full_barrier("endB")

    P.run()
    return nc


def emit_mixer_A(P, L):
    g = lambda n: L[n]
    em, wt, dm = P.em, P.wt, P.dma
    nc = g("nc")
    s_ld, s_wd, s_mm, s_ve, s_sc, s_gp, s_st = (g("s_ld"), g("s_wd"), g("s_mm"),
                                                g("s_ve"), g("s_sc"), g("s_gp"), g("s_st"))
    cs, PS, PSB0 = g("cs"), g("PS"), g("PSB0")
    XT, XN, JUNK = g("XT"), g("XN"), g("JUNK")
    TH, T_local, n_ch = g("TH"), g("T_local"), g("n_ch")
    ln_stats, make_xn = g("ln_stats"), g("make_xn")
    inp, out = g("inp"), g("out")
    cqkv, cab = g("cqkv"), g("cab")
    QKVC, QKVP, CONVW, BABB, WAB = g("QKVC"), g("QKVP"), g("CONVW"), g("BABB"), g("WAB")
    TRN, XS, PW_P, PW_T, EOT, QTT = g("TRN"), g("XS"), g("PW_P"), g("PW_T"), g("EOT"), g("QTT")
    RSOL, KTL, GT_C, PWB = g("RSOL"), g("KTL"), g("GT_C"), g("PWB")
    GACC, HACC, GACCT = g("GACC"), g("HACC"), g("GACCT")
    DNAT, DTRN, EOX, EARG, TT1, TT2 = g("DNAT"), g("DTRN"), g("EOX"), g("EARG"), g("TT1"), g("TT2")
    ABR, ASIG, GAMS = g("ABR"), g("ASIG"), g("GAMS")
    NK2, NQ2, LNK, LNQ, INVK, INVQ = g("NK2"), g("NQ2"), g("LNK"), g("LNQ"), g("INVK"), g("INVQ")
    GPC, BGIK, RTIK, PACK, RROW = g("GPC"), g("BGIK"), g("RTIK"), g("PACK"), g("RROW")
    SPF1, SPF2 = g("SPF1"), g("SPF2")
    WSL = g("WSL")

    em("vector", lambda eng: eng.memset(HACC[:, :], 0.0), inc=s_ve)
    em("vector", lambda eng: eng.memset(QKVP[:, :], 0.0), inc=s_ve)
    for h in range(NH):
        em("vector", lambda eng, h=h: eng.tensor_copy(GACC[:, h * 128:(h + 1) * 128],
                                                      cs["idb"][:, :]), inc=s_ve)
        em("vector", lambda eng, h=h: eng.tensor_copy(GACCT[:, h * 128:(h + 1) * 128],
                                                      cs["idb"][:, :]), inc=s_ve)
    for k in range(16):
        dm("sync", WAB[:, k * 32:(k + 1) * 32], cab.ap()[k, :, 0:32], s_ld)
    wt("tensor", s_ld)
    wt("tensor", s_ve)
    m = em("tensor", lambda eng: eng.matmul(PS[7][:, 0:32], cs["on1f"][0:1, :],
                                            BABB[0:1, 0:32], start=True, stop=True), inc=s_mm)
    wt("vector", s_mm, m)
    em("vector", lambda eng: eng.tensor_copy(BABB[:, 0:32], PS[7][:, 0:32]), inc=s_ve)
    g("full_barrier")("mxinit")

    def qkv_block(xnoff, ncols, want_conv):
        mm_t = [0] * 48
        cv_t = [0] * 48
        si_t = [0] * 48
        wt("vector", s_sc)
        wt("vector", s_gp)
        for cb in range(48):
            buf = WSL[cb % 2]
            bk = PS[1 + cb % 2]
            if cb >= 2:
                wt("scalar" if cb % 2 == 0 else "sync", s_mm, mm_t[cb - 2])
            else:
                wt("scalar" if cb % 2 == 0 else "sync", s_mm)
            wd = dm("scalar" if cb % 2 == 0 else "sync", buf[:, :], cqkv.ap()[cb, :, :],
                    s_wd if cb % 2 == 0 else s_ld)
            wt("tensor", s_wd if cb % 2 == 0 else s_ld, wd)
            if cb >= 2:
                wt("tensor", s_ve, cv_t[cb - 2])

            def mmq(eng, buf=buf, bk=bk, xnoff=xnoff, ncols=ncols):
                last = None
                for k in range(16):
                    last = eng.matmul(bk[:, 0:ncols], buf[:, k * 128:(k + 1) * 128],
                                      XN[:, k * 512 + xnoff:k * 512 + xnoff + ncols],
                                      start=(k == 0), stop=(k == 15))
                return last
            mm_t[cb] = em("tensor", mmq, inc=s_mm)
            wt("vector", s_mm, mm_t[cb])
            if cb >= 2:
                wt("vector", s_sc, si_t[cb - 2])
            if want_conv:
                def fconv(eng, bk=bk, cb=cb):
                    o = JUNK[:, (cb % 2) * 128:(cb % 2) * 128 + 128]
                    w3 = CONVW[:, cb * 4 + 3:cb * 4 + 4]
                    eng.tensor_scalar(o, bk[:, 0:128], w3, None, ALU.mult)
                    for j in range(3):
                        w = CONVW[:, cb * 4 + j:cb * 4 + j + 1]
                        sh = 3 - j
                        eng.scalar_tensor_tensor(o[:, sh:128], bk[:, 0:128 - sh], w,
                                                 o[:, sh:128], ALU.mult, ALU.add)
                        eng.scalar_tensor_tensor(o[:, 0:sh],
                                                 QKVP[:, cb * 4 + 4 - sh:cb * 4 + 4], w,
                                                 o[:, 0:sh], ALU.mult, ALU.add)
                    return eng.tensor_copy(QKVP[:, cb * 4 + 1:cb * 4 + 4], bk[:, 125:128])
                cv_t[cb] = em("vector", fconv, inc=s_ve)
                wt("scalar", s_ve, cv_t[cb])
                si_t[cb] = em("scalar", lambda eng, cb=cb:
                              eng.activation(QKVC[:, cb * 128:(cb + 1) * 128],
                                             JUNK[:, (cb % 2) * 128:(cb % 2) * 128 + 128],
                                             AF.Silu), inc=s_sc)
            else:
                cv_t[cb] = em("vector", lambda eng, bk=bk, cb=cb, ncols=ncols:
                              eng.tensor_copy(QKVP[:, cb * 4 + 1:cb * 4 + 4],
                                              bk[:, ncols - 3:ncols]), inc=s_ve)
                si_t[cb] = P.n(s_sc)
        wt("tensor", s_sc)
        wt("tensor", s_ve)

    ln_stats(0, HALO)
    make_xn(0, HALO)
    qkv_block(0, HALO, want_conv=False)

    for ch in range(n_ch):
        tg = (ch * C) // 512
        if (ch * C) % 512 == 0:
            xb = HALO + tg * 512
            n = min(512, T_local - tg * 512)
            ln_stats(xb, n)
            make_xn(xb, n)
        xnoff = (ch * C) % 512
        qkv_block(xnoff, C, want_conv=True)

        wt("tensor", s_sc)
        def mmab(eng, xnoff=xnoff):
            last = None
            for k in range(16):
                last = eng.matmul(PS[3][0:32, 0:C], WAB[:, k * 32:(k + 1) * 32],
                                  XN[:, k * 512 + xnoff:k * 512 + xnoff + C],
                                  start=(k == 0), stop=(k == 15))
            return last
        m = em("tensor", mmab, inc=s_mm)
        wt("vector", s_mm, m)
        v = em("vector", lambda eng: eng.tensor_copy(ABR[0:32, 0:C], PS[3][0:32, 0:C]), inc=s_ve)
        wt("tensor", s_ve, v)
        m = em("tensor", lambda eng: eng.transpose(PS[4][:, 0:32], ABR[0:32, 0:C],
                                                   cs["idf"][0:32, 0:32]), inc=s_mm)
        wt("vector", s_mm, m)
        wt("vector", s_sc)
        v = em("vector", lambda eng: eng.tensor_add(ASIG[:, 0:32], PS[4][:, 0:32],
                                                    BABB[:, 0:32]), inc=s_ve)
        wt("scalar", s_ve, v)
        sc = em("scalar", lambda eng: eng.activation(ASIG[:, 0:32], ASIG[:, 0:32],
                                                     AF.Sigmoid), inc=s_sc)
        sc = em("scalar", lambda eng: eng.activation(GAMS[:, 0:NH], ASIG[:, 0:NH], AF.Ln),
                inc=s_sc)
        wt("tensor", s_sc, sc)
        em("tensor", lambda eng: eng.matmul(PS[3][:, 0:NH], cs["trii"][:, :], GAMS[:, 0:NH],
                                            start=True, stop=True), inc=s_mm)
        m = em("tensor", lambda eng: eng.matmul(PS[4][:, 0:NH], cs["trir"][:, :], GAMS[:, 0:NH],
                                                start=True, stop=True), inc=s_mm)
        wt("vector", s_mm, m)
        wt("vector", s_sc)

        def fg(eng):
            eng.tensor_copy(GAMS[:, NH:2 * NH], PS[3][:, 0:NH])
            eng.tensor_copy(GAMS[:, 2 * NH:3 * NH], PS[4][:, 0:NH])
            return eng.tensor_add(GAMS[:, 3 * NH:4 * NH], GAMS[:, NH:2 * NH],
                                  GAMS[:, 2 * NH:3 * NH])
        v = em("vector", fg, inc=s_ve)
        wt("scalar", s_ve, v)
        em("scalar", lambda eng: eng.activation(GAMS[:, 4 * NH:5 * NH], GAMS[:, NH:2 * NH],
                                                AF.Exp), inc=s_sc)
        em("scalar", lambda eng: eng.activation(GAMS[:, 5 * NH:6 * NH], GAMS[:, 3 * NH:4 * NH],
                                                AF.Exp), inc=s_sc)
        sc = em("scalar", lambda eng: eng.activation(GAMS[:, 6 * NH:7 * NH],
                                                     GAMS[:, 2 * NH:3 * NH], AF.Exp), inc=s_sc)
        wt("tensor", s_sc, sc)
        wt("vector", s_sc, sc)

        for h in range(NH):
            kn = QKVC[:, (NH + h) * 128:(NH + h + 1) * 128]
            vn = QKVC[:, (2 * NH + h) * 128:(2 * NH + h + 1) * 128]
            qn = QKVC[:, h * 128:(h + 1) * 128]
            m = em("tensor", lambda eng, kn=kn: eng.transpose(PSB0[:, 0:128], kn,
                                                              cs["idb"][:, :]), inc=s_mm)
            m = em("tensor", lambda eng, vn=vn: eng.transpose(PSB0[:, 128:256], vn,
                                                              cs["idb"][:, :]), inc=s_mm)
            m = em("tensor", lambda eng, qn=qn: eng.transpose(PSB0[:, 256:384], qn,
                                                              cs["idb"][:, :]), inc=s_mm)
            wt("vector", s_mm, m)

            def fev(eng):
                eng.tensor_copy(TRN[:, 0:128], PSB0[:, 0:128])
                eng.tensor_copy(TRN[:, 128:256], PSB0[:, 128:256])
                return eng.tensor_copy(TRN[:, 256:384], PSB0[:, 256:384])
            v = em("vector", fev, inc=s_ve)
            wt("scalar", s_ve, v)
            wt("tensor", s_ve, v)
            sc = em("scalar", lambda eng, h=h:
                    eng.activation(JUNK[:, 0:128], TRN[:, 0:128], AF.Square,
                                   accum_out=NK2[:, h:h + 1]), inc=s_sc)
            sc = em("scalar", lambda eng, h=h:
                    eng.activation(JUNK[:, 128:256], TRN[:, 256:384], AF.Square,
                                   accum_out=NQ2[:, h:h + 1]), inc=s_sc)
            wt("vector", s_sc, sc)
            emit_head(P, L, ch, h)

        wt("sync", s_ve)
        dm("sync", out["spill_pt"][ch * NH * 128:(ch + 1) * NH * 128, :]
           .rearrange("(h p) t -> p h t", p=128),
           SPF1[:, :].rearrange("p (h t) -> p h t", h=NH), s_st)
        dm("sync", out["spill_op"][ch * NH * 128:(ch + 1) * NH * 128, :]
           .rearrange("(h p) t -> p h t", p=128),
           SPF2[:, :].rearrange("p (h t) -> p h t", h=NH), s_st)
        wt("vector", s_st)

    for h in range(NH):
        v = em("vector", lambda eng, h=h: eng.tensor_copy(SPF1[:, h * 128:(h + 1) * 128],
                                                          GACCT[:, h * 128:(h + 1) * 128]), inc=s_ve)
        v = em("vector", lambda eng, h=h: eng.tensor_copy(SPF2[:, h * 128:(h + 1) * 128],
                                                          HACC[:, h * 128:(h + 1) * 128]), inc=s_ve)
    wt("sync", s_ve)
    dm("sync", out["ght_out"].rearrange("(h p) t -> p h t", p=128),
       SPF1[:, :].rearrange("p (h t) -> p h t", h=NH), s_st)
    dm("sync", out["h_out"].rearrange("(h p) t -> p h t", p=128),
       SPF2[:, :].rearrange("p (h t) -> p h t", h=NH), s_st)


def emit_head(P, L, ch, h):
    g = lambda n: L[n]
    em, wt = P.em, P.wt
    s_mm, s_ve, s_sc = g("s_mm"), g("s_ve"), g("s_sc")
    cs, PS, PSB0 = g("cs"), g("PS"), g("PSB0")
    QKVC, TRN, XS, PW_P, PW_T = g("QKVC"), g("TRN"), g("XS"), g("PW_P"), g("PW_T")
    EOT, QTT, RSOL, KTL, GT_C = g("EOT"), g("QTT"), g("RSOL"), g("KTL"), g("GT_C")
    GACC, HACC, GACCT, PWB = g("GACC"), g("HACC"), g("GACCT"), g("PWB")
    DNAT, DTRN, EOX, EARG, TT1, TT2 = g("DNAT"), g("DTRN"), g("EOX"), g("EARG"), g("TT1"), g("TT2")
    ASIG, GAMS = g("ASIG"), g("GAMS")
    NK2, NQ2, LNK, LNQ, INVK, INVQ = g("NK2"), g("NQ2"), g("LNK"), g("LNQ"), g("INVK"), g("INVQ")
    GPC, BGIK, RTIK, PACK, RROW = g("GPC"), g("BGIK"), g("RTIK"), g("PACK"), g("RROW")
    SPF1, SPF2, JUNK = g("SPF1"), g("SPF2"), g("JUNK")

    Kt, Vt, Qt = TRN[:, 0:128], TRN[:, 128:256], TRN[:, 256:384]
    kn = QKVC[:, (g("NH") if False else NH + h) * 128:(NH + h + 1) * 128]
    qn = QKVC[:, h * 128:(h + 1) * 128]
    gcol = GAMS[:, NH + h:NH + h + 1]
    gamc = GAMS[:, 4 * NH + h:4 * NH + h + 1]
    gamCc = GAMS[:, 5 * NH + h:5 * NH + h + 1]
    ratc = GAMS[:, 6 * NH + h:6 * NH + h + 1]
    beta = ASIG[:, NH + h:NH + h + 1]

    def fc1(eng, h=h):
        eng.tensor_copy(LNK[:, h:h + 1], NK2[:, h:h + 1])
        return eng.tensor_copy(LNQ[:, h:h + 1], NQ2[:, h:h + 1])
    v = em("vector", fc1, inc=s_ve)
    wt("scalar", s_ve, v)
    sc = em("scalar", lambda eng, h=h: eng.activation(LNK[:, h:h + 1], LNK[:, h:h + 1], AF.Ln),
            inc=s_sc)
    sc = em("scalar", lambda eng, h=h: eng.activation(LNQ[:, h:h + 1], LNQ[:, h:h + 1], AF.Ln),
            inc=s_sc)
    sc = em("scalar", lambda eng, h=h: eng.activation(NK2[:, h:h + 1], NK2[:, h:h + 1], AF.Sqrt),
            inc=s_sc)
    sc = em("scalar", lambda eng, h=h: eng.activation(NQ2[:, h:h + 1], NQ2[:, h:h + 1], AF.Sqrt),
            inc=s_sc)
    sc = em("scalar", lambda eng, h=h: eng.activation(GPC[:, h:h + 1], ASIG[:, NH + h:NH + h + 1],
                                                      AF.Ln), inc=s_sc)
    wt("vector", s_sc, sc)

    def fc2(eng, h=h):
        eng.reciprocal(INVK[:, h:h + 1], NK2[:, h:h + 1])
        eng.tensor_scalar(INVK[:, h:h + 1], INVK[:, h:h + 1], INVCAP, None, ALU.min)
        eng.reciprocal(INVQ[:, h:h + 1], NQ2[:, h:h + 1])
        eng.tensor_scalar(INVQ[:, h:h + 1], INVQ[:, h:h + 1], INVCAP, None, ALU.min)
        eng.scalar_tensor_tensor(PACK[:, h:h + 1], LNK[:, h:h + 1], 0.5, gcol, ALU.mult, ALU.add)
        eng.tensor_add(GPC[:, h:h + 1], GPC[:, h:h + 1], gcol)
        eng.scalar_tensor_tensor(GPC[:, h:h + 1], LNK[:, h:h + 1], -0.5, GPC[:, h:h + 1],
                                 ALU.mult, ALU.add)
        eng.scalar_tensor_tensor(PACK[:, NH + h:NH + h + 1], LNQ[:, h:h + 1], -0.5, gcol,
                                 ALU.mult, ALU.add)
        eng.tensor_mul(PACK[:, 2 * NH + h:2 * NH + h + 1], gamc, INVQ[:, h:h + 1])
        eng.tensor_mul(BGIK[:, h:h + 1], beta, gamc)
        eng.tensor_mul(BGIK[:, h:h + 1], BGIK[:, h:h + 1], INVK[:, h:h + 1])
        return eng.tensor_mul(RTIK[:, h:h + 1], ratc, INVK[:, h:h + 1])
    v = em("vector", fc2, inc=s_ve)
    wt("tensor", s_ve, v)

    def bcast(col_ap, ps_out):
        wt("tensor", s_ve)
        m = em("tensor", lambda eng, col_ap=col_ap: eng.matmul(PS[6][0:1, 0:128], col_ap,
                                                               cs["idf"][:, :],
                                                               start=True, stop=True), inc=s_mm)
        wt("vector", s_mm, m)
        v = em("vector", lambda eng: eng.tensor_copy(RROW[0:1, 0:128], PS[6][0:1, 0:128]),
               inc=s_ve)
        wt("tensor", s_ve, v)
        return em("tensor", lambda eng, ps_out=ps_out:
                  eng.matmul(ps_out, cs["on1f"][0:1, :], RROW[0:1, 0:128],
                             start=True, stop=True), inc=s_mm)

    m = bcast(PACK[:, h:h + 1], PS[3][:, 0:128])
    wt("vector", s_mm, m)
    wt("vector", s_sc)

    def fearg(eng):
        eng.tensor_scalar(EARG[:, :], PS[3][:, 0:128], GPC[:, h:h + 1], -CLAMP,
                          ALU.subtract, ALU.max)
        return eng.tensor_tensor(EARG[:, :], EARG[:, :], cs["bias_nat"][:, :], ALU.add)
    v = em("vector", fearg, inc=s_ve)
    wt("scalar", s_ve, v)
    wt("tensor", s_ve, v)
    sc = em("scalar", lambda eng: eng.activation(DNAT[:, :], EARG[:, :], AF.Exp, scale=-1.0),
            inc=s_sc)
    m = em("tensor", lambda eng: eng.transpose(PS[4][:, 0:128], EARG[:, :], cs["idf"][:, :]),
           inc=s_mm)
    wt("scalar", s_mm, m)
    sc = em("scalar", lambda eng: eng.activation(DTRN[:, :], PS[4][:, 0:128], AF.Exp,
                                                 scale=-1.0), inc=s_sc)
    m = bcast(PACK[:, NH + h:NH + h + 1], PS[3][:, 0:128])
    wt("vector", s_mm, m)
    wt("vector", s_sc)

    def feargo(eng, h=h):
        eng.tensor_scalar(EARG[:, :], PS[3][:, 0:128], PACK[:, h:h + 1], CLAMP,
                          ALU.subtract, ALU.min)
        return eng.tensor_tensor(EARG[:, :], EARG[:, :], cs["bias_eot"][:, :],
                                 ALU.subtract)
    v = em("vector", feargo, inc=s_ve)
    wt("scalar", s_ve, v)
    sc = em("scalar", lambda eng: eng.activation(EOX[:, :], EARG[:, :], AF.Exp), inc=s_sc)

    wt("tensor", s_sc)
    em("tensor", lambda eng, kn=kn: eng.matmul(PS[5][:, 0:128], kn, kn, start=True, stop=True),
       inc=s_mm)
    m = em("tensor", lambda eng, kn=kn, qn=qn: eng.matmul(PS[5][:, 128:256], kn, qn,
                                                          start=True, stop=True), inc=s_mm)
    wt("vector", s_mm, m)

    wt("vector", s_sc)

    def fseed(eng):
        eng.scalar_tensor_tensor(PW_P[:, :], PS[5][:, 0:128], -1.0, DNAT[:, :],
                                 ALU.mult, ALU.mult)
        eng.scalar_tensor_tensor(PW_T[:, :], PS[5][:, 0:128], -1.0, DTRN[:, :],
                                 ALU.mult, ALU.mult)
        eng.scalar_tensor_tensor(EOT[:, :], PS[5][:, 128:256], 1.0, EOX[:, :],
                                 ALU.mult, ALU.mult)
        eng.tensor_scalar(RSOL[:, 0:128], Vt, ASIG[:, NH + h:NH + h + 1], None, ALU.mult)
        eng.tensor_scalar(RSOL[:, 128:256], Kt, BGIK[:, h:h + 1], None, ALU.mult)
        return eng.tensor_scalar(KTL[:, :], Kt, RTIK[:, h:h + 1], None, ALU.mult)
    v = em("vector", fseed, inc=s_ve)
    wt("tensor", s_ve, v)

    src = RSOL
    for j in range(7):
        wt("tensor", s_ve)
        wt("tensor", s_sc)
        em("tensor", lambda eng, src=src: eng.matmul(PS[7][:, 0:256], cs["idb"][:, :],
                                                     src[:, :], start=True, stop=False),
           inc=s_mm)
        m = em("tensor", lambda eng, src=src: eng.matmul(PS[7][:, 0:256], PW_T[:, :],
                                                         src[:, :], start=False, stop=True),
               inc=s_mm)
        if j < 6:
            em("tensor", lambda eng: eng.matmul(PS[6][:, 0:128], PW_T[:, :], PW_P[:, :],
                                                start=True, stop=True), inc=s_mm)
            m = em("tensor", lambda eng: eng.matmul(PS[6][:, 128:256], PW_P[:, :], PW_T[:, :],
                                                    start=True, stop=True), inc=s_mm)
        wt("scalar", s_mm, m)
        sc = em("scalar", lambda eng: eng.activation(XS[:, :], PS[7][:, 0:256], AF.Copy),
                inc=s_sc)
        if j < 6:
            sc = em("scalar", lambda eng: eng.activation(PWB[:, :], PS[6][:, 0:256],
                                                         AF.Copy), inc=s_sc)
        src = XS
    wt("tensor", s_sc)

    m = bcast(PACK[:, 2 * NH + h:2 * NH + h + 1], PS[3][:, 0:128])
    m2 = em("tensor", lambda eng: eng.matmul(PS[4][:, 0:128], XS[:, 128:256], EOT[:, :],
                                             start=True, stop=True), inc=s_mm)
    wt("vector", s_mm, m2)

    def fqt(eng, qn=qn):
        eng.tensor_tensor(TT1[:, :], qn, PS[3][:, 0:128], ALU.mult)
        return eng.tensor_tensor(QTT[:, :], TT1[:, :], PS[4][:, 0:128], ALU.subtract)
    v = em("vector", fqt, inc=s_ve)
    wt("tensor", s_ve, v)

    hA = HACC[:, h * 128:(h + 1) * 128]
    gA = GACC[:, h * 128:(h + 1) * 128]
    gAT = GACCT[:, h * 128:(h + 1) * 128]
    em("tensor", lambda eng: eng.matmul(PS[3][:, 0:128], XS[:, 0:128], EOT[:, :],
                                        start=True, stop=False), inc=s_mm)
    em("tensor", lambda eng, hA=hA: eng.matmul(PS[3][:, 0:128], hA, QTT[:, :],
                                               start=False, stop=True), inc=s_mm)
    m = em("tensor", lambda eng, gA=gA: eng.matmul(PS[4][:, 0:128], gA, QTT[:, :],
                                                   start=True, stop=True), inc=s_mm)
    wt("vector", s_mm, m)
    v = em("vector", lambda eng, h=h: eng.tensor_copy(SPF2[:, h * 128:(h + 1) * 128],
                                                      PS[3][:, 0:128]), inc=s_ve)
    v = em("vector", lambda eng, h=h: eng.tensor_copy(SPF1[:, h * 128:(h + 1) * 128],
                                                      PS[4][:, 0:128]), inc=s_ve)
    wt("tensor", s_ve, v)

    em("tensor", lambda eng: eng.matmul(PS[3][:, 0:128], KTL[:, :], XS[:, 0:128],
                                        start=True, stop=False), inc=s_mm)
    m = em("tensor", lambda eng: eng.matmul(PS[4][:, 0:128], XS[:, 128:256], KTL[:, :],
                                            start=True, stop=True), inc=s_mm)
    wt("vector", s_mm, m)

    def fgt(eng):
        eng.tensor_scalar(TT2[:, :], cs["idf"][:, :], gamCc, None, ALU.mult)
        return eng.scalar_tensor_tensor(GT_C[:, :], PS[4][:, 0:128], -1.0, TT2[:, :],
                                        ALU.mult, ALU.add)
    v = em("vector", fgt, inc=s_ve)
    wt("tensor", s_ve, v)
    em("tensor", lambda eng, hA=hA: eng.matmul(PS[3][:, 0:128], GT_C[:, :], hA,
                                               start=False, stop=True), inc=s_mm)
    em("tensor", lambda eng, gA=gA: eng.matmul(PS[5][:, 0:128], GT_C[:, :], gA,
                                               start=True, stop=True), inc=s_mm)
    m = em("tensor", lambda eng, gA=gA: eng.matmul(PS[5][:, 128:256], gA, GT_C[:, :],
                                                   start=True, stop=True), inc=s_mm)
    wt("vector", s_mm, m)
    v = em("vector", lambda eng, hA=hA: eng.tensor_copy(hA, PS[3][:, 0:128]), inc=s_ve)
    v = em("vector", lambda eng, gA=gA: eng.tensor_copy(gA, PS[5][:, 0:128]), inc=s_ve)
    v = em("vector", lambda eng, gAT=gAT: eng.tensor_copy(gAT, PS[5][:, 128:256]), inc=s_ve)
    wt("tensor", s_ve, v)
    wt("scalar", s_ve, v)


def emit_B(P, L):
    g = lambda n: L[n]
    em, wt, dm = P.em, P.wt, P.dma
    nc = g("nc")
    s_ld, s_wd, s_mm, s_ve, s_sc, s_gp, s_st = (g("s_ld"), g("s_wd"), g("s_mm"),
                                                g("s_ve"), g("s_sc"), g("s_gp"), g("s_st"))
    cs, PS = g("cs"), g("PS")
    XT, XN, JUNK = g("XT"), g("XN"), g("JUNK")
    TH, T_local, n_tg, n_ch = g("TH"), g("T_local"), g("n_tg"), g("n_ch")
    ln_stats, make_xn, ffn = g("ln_stats"), g("make_xn"), g("ffn")
    inp, out = g("inp"), g("out")
    cwg, cwo = g("cwg"), g("cwo")
    ZIN, GHS, GROW, OFIX = g("ZIN"), g("GHS"), g("GROW"), g("OFIX")
    BC1, x2scr = g("BC1"), g("x2scr")
    SPB1, SPB2, SPF1, SPF2 = g("SPB1"), g("SPB2"), g("SPF1"), g("SPF2")
    WSL = g("WSL")
    blocks = g("blocks")

    wt("vector", s_gp)
    wt("tensor", s_gp)
    wt("scalar", s_gp)

    em("vector", lambda eng: eng.memset(ZIN[:, :], 0.0), inc=s_ve)
    for j in range(NCORES - 1):
        base = j * 2 * NH * 128
        for h in range(NH):
            t = dm("gpsimd", GHS[:, 0:128],
                   inp["ghseq"][base + h * 128:base + (h + 1) * 128, :], s_gp)
            t = dm("gpsimd", GHS[:, 128:256],
                   inp["ghseq"][base + NH * 128 + h * 128:base + NH * 128 + (h + 1) * 128, :],
                   s_gp)
            wt("tensor", s_gp, t)
            wt("tensor", s_ve)
            em("tensor", lambda eng, h=h: eng.matmul(PS[1][:, 0:128], GHS[:, 0:128],
                                                     ZIN[:, h * 128:(h + 1) * 128],
                                                     start=True, stop=False), inc=s_mm)
            m = em("tensor", lambda eng: eng.matmul(PS[1][:, 0:128], cs["idb"][:, :],
                                                    GHS[:, 128:256], start=False, stop=True),
                   inc=s_mm)
            wt("vector", s_mm, m)
            v = em("vector", lambda eng, h=h: eng.tensor_copy(ZIN[:, h * 128:(h + 1) * 128],
                                                              PS[1][:, 0:128]), inc=s_ve)
            wt("gpsimd", s_ve, v)
    g("full_barrier")("zin")

    for tg in range(n_tg):
        g("full_barrier")("tg%d" % tg)
        xb = HALO + tg * 512
        n = min(512, T_local - tg * 512)
        ln_stats(xb, n)
        make_xn(xb, n)
        for dblk in range(DK):
            buf = WSL[dblk % 2]
            wt("scalar", s_sc)
            wt("scalar", s_ve)
            t = dm("scalar", buf[:, :], cwg.ap()[dblk, :, :], s_wd)
            wt("tensor", s_wd, t)
            wt("tensor", s_sc)

            def mmg(eng, buf=buf, n=n):
                last = None
                for k in range(DK):
                    last = eng.matmul(PS[1][:, 0:n], buf[:, k * 128:(k + 1) * 128],
                                      XN[:, k * 512:k * 512 + n],
                                      start=(k == 0), stop=(k == DK - 1))
                return last
            m = em("tensor", mmg, inc=s_mm)
            wt("scalar", s_mm, m)
            em("scalar", lambda eng, dblk=dblk, n=n:
               eng.activation(GROW[:, dblk * 512:dblk * 512 + n], PS[1][:, 0:n], AF.Silu),
               inc=s_sc)
        sc = P.n(s_sc)
        wt("tensor", s_sc, sc)
        wt("vector", s_sc, sc)
        for cc in range(4):
            ch = tg * 4 + cc
            if ch >= n_ch:
                break
            t = dm("gpsimd", SPB1[:, :].rearrange("p (h t) -> p h t", h=NH),
                   inp["spill_pt"][ch * NH * 128:(ch + 1) * NH * 128, :]
                   .rearrange("(h p) t -> p h t", p=128), s_gp)
            t = dm("gpsimd", SPB2[:, :].rearrange("p (h t) -> p h t", h=NH),
                   inp["spill_op"][ch * NH * 128:(ch + 1) * NH * 128, :]
                   .rearrange("(h p) t -> p h t", p=128), s_gp)
            wt("tensor", s_gp, t)
            for h in range(NH):
                em("tensor", lambda eng, h=h:
                   eng.matmul(PS[2][:, 0:128], ZIN[:, h * 128:(h + 1) * 128],
                              SPB1[:, h * 128:(h + 1) * 128], start=True, stop=False), inc=s_mm)
                m = em("tensor", lambda eng, h=h:
                       eng.matmul(PS[2][:, 0:128], cs["idb"][:, :],
                                  SPB2[:, h * 128:(h + 1) * 128], start=False, stop=True),
                       inc=s_mm)
                wt("vector", s_mm, m)
                v = em("vector", lambda eng, h=h, cc=cc:
                       eng.tensor_tensor(OFIX[:, h * 512 + cc * 128:h * 512 + (cc + 1) * 128],
                                         PS[2][:, 0:128],
                                         GROW[:, h * 512 + cc * 128:h * 512 + (cc + 1) * 128],
                                         ALU.mult), inc=s_ve)
                wt("tensor", s_ve, v)
            wt("gpsimd", s_ve)
        for dblk in range(DK):
            buf = WSL[dblk % 2]
            wt("scalar", s_ve)
            t = dm("scalar", buf[:, :], cwo.ap()[dblk, :, :], s_wd)
            wt("tensor", s_wd, t)

            def mmo(eng, buf=buf, n=n):
                last = None
                for k in range(DK):
                    last = eng.matmul(PS[1][:, 0:n], buf[:, k * 128:(k + 1) * 128],
                                      OFIX[:, k * 512:k * 512 + n],
                                      start=(k == 0), stop=(k == DK - 1))
                return last
            m = em("tensor", mmo, inc=s_mm)
            wt("vector", s_mm, m)

            wt("sync", s_ve)
            t = dm("sync", BC1[:, 0:n], inp["x1T"][dblk * 128:(dblk + 1) * 128,
                                                    xb:xb + n], s_ld)
            wt("vector", s_ld, t)

            def fr(eng, dblk=dblk, xb=xb, n=n):
                xx = XT[:, dblk * TH + xb:dblk * TH + xb + n]
                eng.tensor_add(JUNK[:, 0:n], BC1[:, 0:n], PS[1][:, 0:n])
                return eng.tensor_copy(xx, JUNK[:, 0:n])
            v = em("vector", fr, inc=s_ve)
            wt("sync", s_ve, v)
            dm("sync", x2scr.ap()[dblk * 128:(dblk + 1) * 128, xb - HALO:xb - HALO + n],
               JUNK[:, 0:n], s_st)
            wt("vector", s_st)
            wt("tensor", s_ve, v)
        g("full_barrier")("wo_tg%d" % tg)
        ln_stats(xb, n)
        make_xn(xb, n)
        ffn(xb, n, out_store=out["yT"])


_T_OVERRIDE = None
_NCORES_OVERRIDE = None


def kernel(L0_attn_x, L0_attn_W_qkv, L0_attn_conv_w, L0_attn_W_ab, L0_attn_b_ab,
           L0_attn_W_g, L0_attn_W_o, L0_ffn_w1, L0_ffn_w2, L0_ffn_w3):
    ncores = _NCORES_OVERRIDE or NCORES
    T = L0_attn_x.shape[1]
    Tl = _T_OVERRIDE or (T // ncores)
    x0T = np.ascontiguousarray(np.asarray(L0_attn_x)[0].T)
    w1T = np.ascontiguousarray(np.asarray(L0_ffn_w1).T)
    w2T = np.ascontiguousarray(np.asarray(L0_ffn_w2).T)
    w3T = np.ascontiguousarray(np.asarray(L0_ffn_w3).T)
    wqkvT = np.ascontiguousarray(np.asarray(L0_attn_W_qkv).T)
    wabT = np.ascontiguousarray(np.asarray(L0_attn_W_ab).T)
    wgT = np.ascontiguousarray(np.asarray(L0_attn_W_g).T)
    woT = np.ascontiguousarray(np.asarray(L0_attn_W_o).T)
    convw = np.ascontiguousarray(np.asarray(L0_attn_conv_w))
    bab = np.ascontiguousarray(np.asarray(L0_attn_b_ab)[None, :]).astype(np.float32)

    zh = np.zeros((D, HALO), dtype=np.float32)
    in_a = []
    for c in range(ncores):
        lo = c * Tl
        if lo >= HALO:
            xs = x0T[:, lo - HALO:lo + Tl]
        else:
            xs = np.concatenate([zh[:, :HALO - lo], x0T[:, 0:lo + Tl]], axis=1)
        in_a.append(dict(xT=np.ascontiguousarray(xs), w1T=w1T, w2T=w2T, w3T=w3T,
                         wqkvT=wqkvT, wabT=wabT, convw=convw, bab=bab,
                         wgT=wgT, woT=woT))

    nc_a = build_program("A", Tl)
    res_a = run_bass_kernel_spmd(nc_a, in_a, core_ids=list(range(ncores)))

    ident = np.tile(np.eye(128, dtype=np.float32), (NH, 1))
    zero = np.zeros((NH * 128, 128), dtype=np.float32)
    in_b = []
    for c in range(ncores):
        seq = []
        for j in range(NCORES - 1):
            if j < c:
                seq.append(res_a.results[j]["ght_out"])
                seq.append(res_a.results[j]["h_out"])
            else:
                seq.append(ident)
                seq.append(zero)
        ra = res_a.results[c]
        in_b.append(dict(x1T=ra["x1T_out"],
                         cw1=ra["cw1"], cw2=ra["cw2"], cw3=ra["cw3"],
                         cwg=ra["cwg"], cwo=ra["cwo"],
                         spill_pt=ra["spill_pt"], spill_op=ra["spill_op"],
                         ghseq=np.ascontiguousarray(np.concatenate(seq, axis=0))))

    nc_b = build_program("B", Tl)
    res_b = run_bass_kernel_spmd(nc_b, in_b, core_ids=list(range(ncores)))

    outv = np.concatenate([res_b.results[c]["yT"].T for c in range(ncores)], axis=0)
    return outv[None, :, :].astype(np.float32)


# version 05 (final)
# speedup vs baseline: 1.4661x; 1.0961x over previous
import numpy as np
import ml_dtypes
import concourse.bass as bass
import concourse.mybir as mybir
from concourse.bass_utils import run_bass_kernel_spmd

F32 = mybir.dt.float32
BF16 = mybir.dt.bfloat16
I32 = mybir.dt.int32
AF = mybir.ActivationFunctionType
ALU = mybir.AluOpType
AXX = mybir.AxisListType.X
BF = ml_dtypes.bfloat16

D = 2048
DK = 16
HFF = 5632
CFF = 44
NH = 16
QKV = 6144
CQK = 48
HALO = 64
C = 128
EPS_LN = 1e-5
EPS_Q = 1e-8
NCORES = 8
CLAMP = 80.0
INVCAP = 1e12



class Em:

    ENGINES = ("sync", "tensor", "vector", "scalar", "gpsimd")

    CHAINED = ("vector", "scalar", "gpsimd")
    CHAIN_OPS = frozenset("""tensor_copy tensor_tensor tensor_add tensor_sub
        tensor_mul tensor_max tensor_scalar tensor_single_scalar
        tensor_scalar_add tensor_scalar_sub tensor_scalar_mul tensor_scalar_max
        tensor_scalar_min tensor_relu tensor_reduce reduce_max reduce_sum
        scalar_tensor_tensor tensor_tensor_scan memset memzero reciprocal
        activation copy add mul sqrt square sign select bn_stats bn_aggr
        transpose pool pool_avg pool_max stream_shuffle""".split())

    class _MockInst:
        def then_inc(self, *a, **kw):
            return self

    class _CountEng:

        def __init__(self):
            self.n = 0

        def __getattr__(self, nm):
            if nm in Em.CHAIN_OPS:
                def c(*a, **kw):
                    self.n += 1
                    return Em._MockInst()
                return c
            return lambda *a, **kw: Em._MockInst()

    class _ChainEng:

        def __init__(self, eng, sem_h):
            self._e = eng
            self._h = sem_h
            self._n = 0

        def __getattr__(self, nm):
            attr = getattr(self._e, nm)
            if nm in Em.CHAIN_OPS:
                def call(*a, _attr=attr, **kw):
                    if self._n:
                        self._e.wait_ge(self._h, self._n)
                    inst = _attr(*a, **kw)
                    inst.then_inc(self._h, 1)
                    self._n += 1
                    return inst
                return call
            return attr

    def __init__(self, nc):
        self.nc = nc
        self.q = {e: [] for e in self.ENGINES}
        self._s = {}
        self.own = {}

    def sem(self, name):
        if name not in self._s:
            self._s[name] = [self.nc.alloc_semaphore(name), 0]
        return name

    def n(self, s):
        return self._s[s][1]

    def em(self, engine, fn, inc=None, k=1):
        if engine in self.CHAINED:
            mock = Em._CountEng()
            fn(mock)
            if mock.n > 0:
                own = self.own[engine]
                assert inc is None or inc == own, (engine, inc)
                self._s[own][1] += mock.n
                self.q[engine].append(fn)
                return self._s[own][1]
        if inc is None:
            self.q[engine].append(fn)
            return None
        h = self._s[inc][0]
        self._s[inc][1] += k

        def f2(eng, fn=fn, h=h, k=k):
            fn(eng).then_inc(h, k)
        self.q[engine].append(f2)
        return self._s[inc][1]

    def wt(self, engine, sem, val=None):
        v = self.n(sem) if val is None else val
        h = self._s[sem][0]
        self.q[engine].append(lambda eng, h=h, v=v: eng.wait_ge(h, v))

    def dma(self, engine, out, in_, sem, **kw):
        h = self._s[sem][0]
        self._s[sem][1] += 16

        def f(eng, out=out, in_=in_, h=h, kw=kw):
            eng.dma_start(out=out, in_=in_, **kw).then_inc(h, 16)
        self.q[engine].append(f)
        return self._s[sem][1]

    def barrier(self, tag):
        b = self.sem("bar_" + tag)
        for e in self.ENGINES:
            self.em(e, lambda eng: eng.nop(), inc=b)
        tgt = self.n(b)
        for e in self.ENGINES:
            self.wt(e, b, tgt)

    def run(self):
        with self.nc.Block() as block:
            for e in self.ENGINES:
                fns = self.q[e]
                own = self.own.get(e)

                def body(eng, fns=fns, own=own):
                    if own is not None:
                        eng = Em._ChainEng(eng, self._s[own][0])
                    for fn in fns:
                        fn(eng)
                getattr(block, e)(body)



def build_program(mode, T_local):
    nc = bass.Bass("TRN2", target_bir_lowering=False, debug=False)
    TH = T_local + HALO
    n_tg = (T_local + 511) // 512
    n_ch = T_local // C

    inp = {}

    def din(nm, shp):
        inp[nm] = nc.dram_tensor(nm, shp, F32, kind="ExternalInput").ap()
        return inp[nm]

    if mode == "A":
        din("xT", [D, TH])
        din("wqkvT", [D, QKV]); din("wabT", [D, 32])
        din("convw", [QKV, 4]); din("bab", [1, 32])
        din("wgT", [D, D]); din("woT", [D, D])
        din("w1T", [D, HFF]); din("w2T", [D, HFF]); din("w3T", [HFF, D])
    else:
        din("x1T", [D, TH])
        din("spill_pt", [n_ch * NH * 128, 128])
        din("spill_op", [n_ch * NH * 128, 128])
        din("ghseq", [(NCORES - 1) * 2 * NH * 128, 128])

    out = {}
    if mode == "A":
        out["x1T_out"] = nc.dram_tensor("x1T_out", [D, TH], F32, kind="ExternalOutput").ap()
        out["spill_pt"] = nc.dram_tensor("spill_pt", [n_ch * NH * 128, 128], F32, kind="ExternalOutput").ap()
        out["spill_op"] = nc.dram_tensor("spill_op", [n_ch * NH * 128, 128], F32, kind="ExternalOutput").ap()
        out["ght_out"] = nc.dram_tensor("ght_out", [NH * 128, 128], F32, kind="ExternalOutput").ap()
        out["h_out"] = nc.dram_tensor("h_out", [NH * 128, 128], F32, kind="ExternalOutput").ap()
    else:
        out["yT"] = nc.dram_tensor("yT", [D, T_local], F32, kind="ExternalOutput").ap()

    ckind = "ExternalOutput" if mode == "A" else "ExternalInput"
    cw1 = nc.dram_tensor("cw1", [CFF, 128, DK * 128], BF16, kind=ckind)
    cw2 = nc.dram_tensor("cw2", [CFF, 128, DK * 128], BF16, kind=ckind)
    cw3 = nc.dram_tensor("cw3", [DK, 128, CFF * 128], BF16, kind=ckind)
    cwg = nc.dram_tensor("cwg", [DK, 128, DK * 128], BF16, kind=ckind)
    cwo = nc.dram_tensor("cwo", [DK, 128, DK * 128], BF16, kind=ckind)
    if mode == "A":
        cqkv = nc.dram_tensor("cqkv", [CQK, 128, DK * 128], BF16, kind="Internal")
        cab = nc.dram_tensor("cab", [DK, 128, 32], BF16, kind="Internal")
        out["cw1"] = cw1.ap(); out["cw2"] = cw2.ap(); out["cw3"] = cw3.ap()
        out["cwg"] = cwg.ap(); out["cwo"] = cwo.ap()
    else:
        x2scr = nc.dram_tensor("x2scr", [D, T_local], F32, kind="Internal")

    KONST = {
        "idb": np.eye(128).astype(BF),
        "idf": np.eye(128, dtype=np.float32),
        "on1b": np.ones((1, 128)).astype(BF),
        "on1f": np.ones((1, 128), dtype=np.float32),
        "oncf": np.ones((128, 1), dtype=np.float32),
        "oncb": np.ones((128, 1)).astype(BF),
        "trii": np.triu(np.ones((128, 128), dtype=np.float32)),
        "trir": np.tril(np.ones((128, 128), dtype=np.float32), -1),
        "bias_nat": 1e6 * np.triu(np.ones((128, 128), dtype=np.float32)),
        "bias_eot": 1e6 * np.tril(np.ones((128, 128), dtype=np.float32), -1),
    }
    ktens = {nm: nc.inline_tensor(v, "k_" + nm) for nm, v in KONST.items()}
    cs = {}
    for nm, v in KONST.items():
        cs[nm] = nc.alloc_sbuf_tensor("c_" + nm, list(v.shape),
                                      BF16 if v.dtype == BF else F32)

    sbB = lambda n, s: nc.alloc_sbuf_tensor(n, s, BF16)
    sbF = lambda n, s: nc.alloc_sbuf_tensor(n, s, F32)

    XT = sbB("XT", [128, DK * TH])
    XN = sbB("XN", [128, DK * 512])
    RGN = nc.alloc_sbuf_tensor("RGN", [128, 45056 // 4], F32)
    off = nc.lookup_mloc(RGN).addr
    HS = nc.alloc_sbuf_tensor_at("HS", [128, CFF * 512], BF16, offset=off)
    WQF = [nc.alloc_sbuf_tensor_at("WQF%d" % i, [128, 2816], F32, offset=off + i * 11264)
           for i in range(2)]
    WQB = [nc.alloc_sbuf_tensor_at("WQB%d" % i, [128, 2816], BF16,
                                   offset=off + 22528 + i * 5632) for i in range(2)]
    WQC = [nc.alloc_sbuf_tensor_at("WQC%d" % i, [128, 2816], BF16,
                                   offset=off + 33792 + i * 5632) for i in range(2)]
    PQ = [nc.alloc_sbuf_tensor_at("PQ%d" % i, [128, 5632], F32,
                                  offset=off + i * 22528) for i in range(2)]
    SPF1 = nc.alloc_sbuf_tensor_at("SPF1", [128, NH * 128], F32, offset=off)
    SPF2 = nc.alloc_sbuf_tensor_at("SPF2", [128, NH * 128], F32, offset=off + 8192)

    W3S0 = sbB("W3S0", [128, CFF * 128])
    W3S1 = sbB("W3S1", [128, CFF * 128])
    WSL = [W3S0[:, 0:DK * 128], W3S0[:, DK * 128:2 * DK * 128]]
    WSL2 = [W3S1[:, 0:DK * 128], W3S1[:, DK * 128:2 * DK * 128]]
    W3S = [W3S0, W3S1]

    STAT = sbF("STAT", [1, 2048])
    ROWS = sbF("ROWS", [1, 128])
    GINV = sbF("GINV", [128, 4])
    GCOLS = sbF("GCOLS", [128, 4])
    BC1 = sbF("BC1", [128, 512])
    BC2 = sbF("BC2", [128, 512])
    SGT = sbF("SGT", [128, 512])
    JUNK = sbF("JUNK", [128, 512])

    if mode == "A":
        QKVC = sbB("QKVC", [128, CQK * 128])
        QKVP = sbB("QKVP", [128, CQK * 4])
        CONVW = sbF("CONVW", [128, CQK * 4])
        BABB = sbF("BABB", [128, 32])
        WAB = sbB("WAB", [128, DK * 32])
        TRN = sbB("TRN", [128, 3 * 128])
        XS = sbB("XS", [128, 256])
        PWB = sbB("PWB", [128, 256])
        PW_P = PWB[:, 0:128]
        PW_T = PWB[:, 128:256]
        EOT = sbB("EOT", [128, 128])
        QTT = sbB("QTT", [128, 128])
        RSOL = sbB("RSOL", [128, 256])
        KTL = sbB("KTL", [128, 128])
        GT_C = sbB("GT_C", [128, 128])
        GACC = sbB("GACC", [128, NH * 128])
        HACC = sbB("HACC", [128, NH * 128])
        GACCT = sbB("GACCT", [128, NH * 128])
        DNAT = sbF("DNAT", [128, 128])
        DTRN = sbF("DTRN", [128, 128])
        EOX = sbF("EOX", [128, 128])
        EARG = sbF("EARG", [128, 128])
        TT1 = sbF("TT1", [128, 128])
        TT2 = sbF("TT2", [128, 128])
        ABR = sbF("ABR", [32, 128])
        ASIG = sbF("ASIG", [128, 32])
        GAMS = sbF("GAMS", [128, 7 * NH])
        NK2 = sbF("NK2", [128, NH]); NQ2 = sbF("NQ2", [128, NH])
        LNK = sbF("LNK", [128, NH]); LNQ = sbF("LNQ", [128, NH])
        INVK = sbF("INVK", [128, NH]); INVQ = sbF("INVQ", [128, NH])
        GPC = sbF("GPC", [128, NH])
        BGIK = sbF("BGIK", [128, NH]); RTIK = sbF("RTIK", [128, NH])
        PACK = sbF("PACK", [128, 3 * NH])
        RROW = sbF("RROW", [1, 128])
    else:
        ZIN = sbB("ZIN", [128, NH * 128])
        GHS = sbB("GHS", [128, 2 * 128])
        GROW = sbB("GROW", [128, DK * 512])
        OFIX = sbB("OFIX", [128, NH * 512])
        SPB1 = nc.alloc_sbuf_tensor_at("SPB1", [128, NH * 128], BF16, offset=off + 16384)
        SPB2 = nc.alloc_sbuf_tensor_at("SPB2", [128, NH * 128], BF16, offset=off + 20480)

    PSB0 = nc.alloc_psum_tensor("PSB0", [128, 1024], BF16)
    PS = [None] + [nc.alloc_psum_tensor("PS%d" % i, [128, 512], F32) for i in range(1, 8)]

    P = Em(nc)
    s_ld = P.sem("s_ld"); s_wd = P.sem("s_wd"); s_mm = P.sem("s_mm")
    s_ve = P.sem("s_ve"); s_sc = P.sem("s_sc"); s_gp = P.sem("s_gp")
    s_st = P.sem("s_st"); s_fb = P.sem("s_fb")
    s_w2 = P.sem("s_w2"); s_l2 = P.sem("s_l2"); s_bc = P.sem("s_bc")
    s_g1 = P.sem("s_g1"); s_g2 = P.sem("s_g2")
    s_q1 = P.sem("s_q1"); s_q2 = P.sem("s_q2")
    s_x1 = P.sem("s_x1"); s_x2 = P.sem("s_x2")
    s_y1 = P.sem("s_y1"); s_y2 = P.sem("s_y2")
    P.own = {"vector": s_ve, "scalar": s_sc, "gpsimd": s_gp}

    def full_barrier(tag):
        wt("tensor", s_mm)
        wt("vector", s_ve)
        wt("scalar", s_sc)
        wt("scalar", s_wd)
        wt("gpsimd", s_gp)
        wt("sync", s_ld)
        wt("sync", s_st)
        P.barrier(tag)
    em, wt, dm = P.em, P.wt, P.dma

    for nm in KONST:
        dm("sync", cs[nm][:, :], ktens[nm].ap()[:, :], s_ld)
    xin = inp["xT"] if mode == "A" else inp["x1T"]
    for k in range(DK):
        dm("gpsimd", XT[:, k * TH:(k + 1) * TH], xin[k * 128:(k + 1) * 128, :], s_gp)
    if mode == "A":
        dm("sync", BABB[0:1, 0:32], inp["bab"][:, :], s_ld)
        for t in range(CQK):
            dm("sync", CONVW[:, t * 4:(t + 1) * 4], inp["convw"][t * 128:(t + 1) * 128, :], s_ld)

    def stream_quant():
        wjobs = [(inp["w1T"], DK, HFF, cw1, 0), (inp["w2T"], DK, HFF, cw2, 1),
                 (inp["w3T"], CFF, D, cw3, 2)]
        QE = ("scalar", "sync")
        SE = ("sync", "scalar")
        LS = (s_wd, s_ld)
        SG = (s_x1, s_x2)
        SY = (s_y1, s_y2)
        SQ = (s_q1, s_q2)
        p1 = []
        for w, ntile, wid, cache, wi in wjobs:
            for k in range(ntile):
                p1.append((w, k, wid, wi))
        red_t = [0] * len(p1)
        for ti, (w, k, n, wi) in enumerate(p1):
            q = ti % 2
            if ti >= 2:
                wt(QE[q], s_ve, red_t[ti - 2])
            t = dm(QE[q], PQ[q][:, 0:n], w[k * 128:(k + 1) * 128, 0:n], LS[q])
            wt("vector", LS[q], t)
            red_t[ti] = em("vector", lambda eng, n=n, ti=ti, q=q:
                           eng.reduce_sum(JUNK[:, ti:ti + 1], PQ[q][:, 0:n], axis=AXX,
                                          apply_absolute_value=True), inc=s_ve)
        base = 0
        for w, ntile, wid, cache, wi in wjobs:
            em("vector", lambda eng, base=base, ntile=ntile, wi=wi:
               eng.reduce_sum(GCOLS[:, wi:wi + 1], JUNK[:, base:base + ntile], axis=AXX),
               inc=s_ve)
            base += ntile
        wt("tensor", s_ve)
        for wi in range(3):
            em("tensor", lambda eng, wi=wi:
               eng.matmul(PS[7][0:1, wi:wi + 1], GCOLS[:, wi:wi + 1], cs["oncf"][:, :],
                          start=True, stop=True), inc=s_mm)
        wt("vector", s_mm)

        def finv(eng):
            eng.tensor_scalar(ROWS[0:1, 0:3], PS[7][0:1, 0:3], 1.0 / (HFF * D), EPS_Q,
                              ALU.mult, ALU.add)
            return eng.reciprocal(ROWS[0:1, 0:3], ROWS[0:1, 0:3])
        v = em("vector", finv, inc=s_ve)
        wt("tensor", s_ve, v)
        m = em("tensor", lambda eng: eng.matmul(PS[7][:, 4:7], cs["on1f"][0:1, :],
                                                ROWS[0:1, 0:3], start=True, stop=True), inc=s_mm)
        wt("vector", s_mm, m)
        em("vector", lambda eng: eng.tensor_copy(GINV[:, 0:3], PS[7][:, 4:7]), inc=s_ve)

        tiles = []
        for w, ntile, wid, cache, wi in wjobs:
            nh_ = (wid + 2815) // 2816
            for k in range(ntile):
                for hh in range(nh_):
                    c0 = hh * 2816
                    tiles.append((w, cache, wi, k, c0, min(2816, wid - c0)))
        fq_t = [0] * len(tiles)
        st_t = [0] * len(tiles)
        for ti, (w, cache, wi, k, c0, n) in enumerate(tiles):
            q = ti % 2
            if ti >= 2:
                wt(QE[q], s_ve, fq_t[ti - 2])
            else:
                wt(QE[q], s_ve)
            t = dm(QE[q], WQF[q][:, 0:n], w[k * 128:(k + 1) * 128, c0:c0 + n], LS[q])
            wt("vector", LS[q], t)
            if ti >= 2:
                wt("vector", SG[q], st_t[ti - 2])

            def fop(eng, n=n, wi=wi, q=q):
                eng.tensor_scalar(WQB[q][:, 0:n], WQF[q][:, 0:n], GINV[:, wi:wi + 1],
                                  0.5, ALU.mult, ALU.is_ge)
                eng.tensor_scalar(WQC[q][:, 0:n], WQF[q][:, 0:n], GINV[:, wi:wi + 1],
                                  -0.5, ALU.mult, ALU.is_le)
                return eng.tensor_tensor(WQB[q][:, 0:n], WQB[q][:, 0:n], WQC[q][:, 0:n],
                                         ALU.subtract)
            fq_t[ti] = em("vector", fop, inc=s_ve)
            wt("gpsimd", s_ve, fq_t[ti])
            st_t[ti] = dm("gpsimd",
                          cache.ap()[c0 // 128:c0 // 128 + n // 128, :,
                                     k * 128:(k + 1) * 128]
                          .rearrange("b p c -> p b c"),
                          WQB[q][:, 0:n].rearrange("p (b c) -> p b c", c=128), SG[q])

        cjobs = [(inp["wqkvT"], cqkv, DK, QKV), (inp["wabT"], cab, DK, 32),
                 (inp["wgT"], cwg, DK, D), (inp["woT"], cwo, DK, D)]
        ctiles = []
        for w, cache, ntile, wid in cjobs:
            nh_ = (wid + 2815) // 2816
            for k in range(ntile):
                for hh in range(nh_):
                    c0 = hh * 2816
                    ctiles.append((w, cache, k, c0, min(2816, wid - c0)))
        cl_t = [0] * len(ctiles)
        cs_t = [0] * len(ctiles)
        for ci, (w, cache, k, c0, n) in enumerate(ctiles):
            q = ci % 2
            if ci >= 2:
                wt("gpsimd", SQ[q], cs_t[ci - 2])
            else:
                wt("gpsimd", SG[0])
                wt("gpsimd", SG[1])
            cl_t[ci] = dm("gpsimd", WQB[q][:, 0:n], w[k * 128:(k + 1) * 128, c0:c0 + n],
                          SY[q])
            wt(SE[q], SY[q], cl_t[ci])
            if n >= 128:
                cs_t[ci] = dm(SE[q],
                              cache.ap()[c0 // 128:c0 // 128 + n // 128, :,
                                         k * 128:(k + 1) * 128]
                              .rearrange("b p c -> p b c"),
                              WQB[q][:, 0:n].rearrange("p (b c) -> p b c", c=128), SQ[q])
            else:
                cs_t[ci] = dm(SE[q], cache.ap()[k, :, 0:n], WQB[q][:, 0:n], SQ[q])

    if mode == "A":
        stream_quant()
        for e in ("scalar", "sync", "vector", "gpsimd", "tensor"):
            for sm in (s_q1, s_q2, s_x1, s_x2, s_y1, s_y2):
                wt(e, sm)
    full_barrier("p0")

    def ln_stats(xbase, ncols):
        def mm_s1(eng, xbase=xbase, ncols=ncols):
            last = None
            for k in range(DK):
                last = eng.matmul(PS[5][0:1, 0:ncols], cs["oncb"][:, :],
                                  XT[:, k * TH + xbase:k * TH + xbase + ncols],
                                  start=(k == 0), stop=(k == DK - 1))
            return last
        em("tensor", mm_s1, inc=s_mm)
        wt("scalar", s_ve)
        wt("scalar", s_st)
        wt("scalar", s_gp)
        JB = (JUNK, SGT)
        sq_t = [0] * DK
        mmq_t = [0] * DK
        for k in range(DK):
            if k >= 2:
                wt("scalar", s_mm, mmq_t[k - 2])
            sq_t[k] = em("scalar", lambda eng, k=k, xbase=xbase, ncols=ncols:
                         eng.activation(JB[k % 2][:, 0:ncols],
                                        XT[:, k * TH + xbase:k * TH + xbase + ncols],
                                        AF.Square), inc=s_sc)
            wt("tensor", s_sc, sq_t[k])
            mmq_t[k] = em("tensor", lambda eng, k=k, ncols=ncols:
                          eng.matmul(PS[6][0:1, 0:ncols], cs["oncf"][:, :],
                                     JB[k % 2][:, 0:ncols],
                                     start=(k == 0), stop=(k == DK - 1)), inc=s_mm)
        wt("vector", s_mm)

        def fstat(eng, ncols=ncols):
            mu = STAT[0:1, 0:ncols]
            var = STAT[0:1, 512:512 + ncols]
            t1 = STAT[0:1, 1024:1024 + ncols]
            eng.tensor_scalar(mu, PS[5][0:1, 0:ncols], 1.0 / D, None, ALU.mult)
            eng.tensor_scalar(var, PS[6][0:1, 0:ncols], 1.0 / D, None, ALU.mult)
            eng.tensor_tensor(t1, mu, mu, ALU.mult)
            eng.tensor_sub(var, var, t1)
            return eng.tensor_scalar(var, var, EPS_LN, None, ALU.add)
        v = em("vector", fstat, inc=s_ve)
        wt("scalar", s_ve, v)
        sq = em("scalar", lambda eng, ncols=ncols:
                eng.activation(STAT[0:1, 512:512 + ncols], STAT[0:1, 512:512 + ncols],
                               AF.Sqrt), inc=s_sc)
        wt("vector", s_sc, sq)
        em("vector", lambda eng, ncols=ncols:
           eng.reciprocal(STAT[0:1, 512:512 + ncols], STAT[0:1, 512:512 + ncols]), inc=s_ve)

    def make_xn(xbase, ncols):
        wt("tensor", s_ve)
        em("tensor", lambda eng, ncols=ncols:
           eng.matmul(PS[5][:, 0:ncols], cs["on1f"][0:1, :], STAT[0:1, 0:ncols],
                      start=True, stop=True), inc=s_mm)
        m = em("tensor", lambda eng, ncols=ncols:
               eng.matmul(PS[6][:, 0:ncols], cs["on1f"][0:1, :], STAT[0:1, 512:512 + ncols],
                          start=True, stop=True), inc=s_mm)
        wt("vector", s_mm, m)
        wt("vector", s_ld)
        em("vector", lambda eng, ncols=ncols:
           eng.tensor_copy(BC1[:, 0:ncols], PS[5][:, 0:ncols]), inc=s_ve)
        em("vector", lambda eng, ncols=ncols:
           eng.tensor_copy(BC2[:, 0:ncols], PS[6][:, 0:ncols]), inc=s_ve)
        for k in range(DK):
            def f(eng, k=k, xbase=xbase, ncols=ncols):
                xx = XT[:, k * TH + xbase:k * TH + xbase + ncols]
                xn = XN[:, k * 512:k * 512 + ncols]
                eng.tensor_sub(xn, xx, BC1[:, 0:ncols])
                return eng.tensor_mul(xn, xn, BC2[:, 0:ncols])
            em("vector", f, inc=s_ve)
        v = P.n(s_ve)
        wt("tensor", s_ve, v)
        wt("scalar", s_ve, v)

    def ffn(xbase, ncols, out_store=None, x1_store=None):
        mm_t = [0] * CFF
        sc_t = [0] * CFF
        ve_t = [0] * CFF
        for cblk in range(CFF):
            bufA, bufB = WSL[cblk % 2], WSL2[cblk % 2]
            bA, bB = PS[1 + cblk % 2], PS[3 + cblk % 2]
            if cblk >= 2:
                wt("scalar", s_mm, mm_t[cblk - 2])
                wt("sync", s_mm, mm_t[cblk - 2])
            else:
                wt("scalar", s_mm)
                wt("sync", s_mm)
            sw = s_wd if cblk % 2 == 0 else s_w2
            sl = s_ld if cblk % 2 == 0 else s_l2
            wd = dm("scalar", bufA[:, :], cw1.ap()[cblk, :, :], sw)
            ldv = dm("sync", bufB[:, :], cw2.ap()[cblk, :, :], sl)
            wt("tensor", sw, wd)
            wt("tensor", sl, ldv)
            if cblk >= 2:
                wt("tensor", s_sc, sc_t[cblk - 2])
                wt("tensor", s_ve, ve_t[cblk - 2])

            def mms(eng, bufA=bufA, bufB=bufB, bA=bA, bB=bB, ncols=ncols):
                for k in range(DK):
                    eng.matmul(bA[:, 0:ncols], bufA[:, k * 128:(k + 1) * 128],
                               XN[:, k * 512:k * 512 + ncols],
                               start=(k == 0), stop=(k == DK - 1))
                last = None
                for k in range(DK):
                    last = eng.matmul(bB[:, 0:ncols], bufB[:, k * 128:(k + 1) * 128],
                                      XN[:, k * 512:k * 512 + ncols],
                                      start=(k == 0), stop=(k == DK - 1))
                return last
            mm_t[cblk] = em("tensor", mms, inc=s_mm)
            wt("scalar", s_mm, mm_t[cblk])
            if cblk >= 1:
                wt("scalar", s_ve, ve_t[cblk - 1])
            sc_t[cblk] = em("scalar", lambda eng, bA=bA, ncols=ncols:
                            eng.activation(SGT[:, 0:ncols], bA[:, 0:ncols], AF.Sigmoid),
                            inc=s_sc)
            wt("vector", s_sc, sc_t[cblk])
            ve_t[cblk] = em("vector", lambda eng, bB=bB, cblk=cblk, ncols=ncols:
                            eng.tensor_tensor(HS[:, cblk * 512:cblk * 512 + ncols],
                                              SGT[:, 0:ncols], bB[:, 0:ncols], ALU.mult),
                            inc=s_ve)
        m3_t = [0] * DK
        v3_t = [0] * DK
        g3_t = [0] * DK
        JB = (JUNK, SGT)
        for dblk in range(DK):
            buf3 = W3S[dblk % 2]
            b3 = PS[1 + dblk % 2]
            if dblk >= 2:
                wt("scalar" if dblk % 2 == 0 else "sync", s_mm, m3_t[dblk - 2])
            else:
                wt("scalar" if dblk % 2 == 0 else "sync", s_mm, mm_t[CFF - 1])
            wd = dm("scalar" if dblk % 2 == 0 else "sync", buf3[:, :],
                    cw3.ap()[dblk, :, :], s_wd if dblk % 2 == 0 else s_ld)
            wt("tensor", s_wd if dblk % 2 == 0 else s_ld, wd)
            if dblk == 0:
                wt("tensor", s_ve, ve_t[CFF - 1])
                wt("tensor", s_sc, sc_t[CFF - 1])
            if dblk >= 2:
                wt("tensor", s_ve, v3_t[dblk - 2])

            def mm3(eng, buf3=buf3, b3=b3, ncols=ncols):
                last = None
                for k in range(CFF):
                    last = eng.matmul(b3[:, 0:ncols], buf3[:, k * 128:(k + 1) * 128],
                                      HS[:, k * 512:k * 512 + ncols],
                                      start=(k == 0), stop=(k == CFF - 1))
                return last
            m3_t[dblk] = em("tensor", mm3, inc=s_mm)
            wt("vector", s_mm, m3_t[dblk])
            jb = JB[dblk % 2]
            sg = s_g1 if dblk % 2 == 0 else s_g2
            if dblk >= 2:
                wt("vector", sg, g3_t[dblk - 2])
            if out_store is None:
                def fr(eng, b3=b3, dblk=dblk, xbase=xbase, ncols=ncols, jb=jb):
                    xx = XT[:, dblk * TH + xbase:dblk * TH + xbase + ncols]
                    eng.scalar_tensor_tensor(jb[:, 0:ncols], b3[:, 0:ncols], 0.5, xx,
                                             ALU.mult, ALU.add)
                    return eng.tensor_copy(xx, jb[:, 0:ncols])
                v3_t[dblk] = em("vector", fr, inc=s_ve)
                if x1_store is not None:
                    wt("gpsimd", s_ve, v3_t[dblk])
                    g3_t[dblk] = dm("gpsimd", x1_store[dblk * 128:(dblk + 1) * 128,
                                                       xbase:xbase + ncols],
                                    jb[:, 0:ncols], sg)
                else:
                    g3_t[dblk] = P.n(sg)
            else:
                wt("sync", s_ve)
                t = dm("sync", BC1[:, 0:ncols],
                       x2scr.ap()[dblk * 128:(dblk + 1) * 128,
                                  xbase - HALO:xbase - HALO + ncols], s_bc)
                wt("vector", s_bc, t)

                def fr(eng, b3=b3, ncols=ncols, jb=jb):
                    return eng.scalar_tensor_tensor(jb[:, 0:ncols], b3[:, 0:ncols], 0.5,
                                                    BC1[:, 0:ncols], ALU.mult, ALU.add)
                v3_t[dblk] = em("vector", fr, inc=s_ve)
                wt("gpsimd", s_ve, v3_t[dblk])
                g3_t[dblk] = dm("gpsimd", out_store[dblk * 128:(dblk + 1) * 128,
                                                    xbase - HALO:xbase - HALO + ncols],
                                jb[:, 0:ncols], sg)
        wt("vector", s_g1)
        wt("vector", s_g2)
        wt("scalar", s_g1)
        wt("scalar", s_g2)

    blocks = [(0, HALO)] + [(HALO + g * 512, min(512, T_local - g * 512)) for g in range(n_tg)]

    if mode == "A":
        wt("vector", s_gp)
        wt("tensor", s_gp)
        wt("scalar", s_gp)
        for (xb, n) in blocks:
            ln_stats(xb, n)
            make_xn(xb, n)
            ffn(xb, n, x1_store=out["x1T_out"])
        full_barrier("ffn1")

        emit_mixer_A(P, locals())
        wt("sync", s_st)
        wt("sync", s_gp)
        full_barrier("endA")
    else:
        emit_B(P, locals())
        wt("sync", s_st)
        full_barrier("endB")

    P.run()
    return nc


def emit_mixer_A(P, L):
    g = lambda n: L[n]
    em, wt, dm = P.em, P.wt, P.dma
    nc = g("nc")
    s_ld, s_wd, s_mm, s_ve, s_sc, s_gp, s_st = (g("s_ld"), g("s_wd"), g("s_mm"),
                                                g("s_ve"), g("s_sc"), g("s_gp"), g("s_st"))
    cs, PS, PSB0 = g("cs"), g("PS"), g("PSB0")
    XT, XN, JUNK = g("XT"), g("XN"), g("JUNK")
    TH, T_local, n_ch = g("TH"), g("T_local"), g("n_ch")
    ln_stats, make_xn = g("ln_stats"), g("make_xn")
    inp, out = g("inp"), g("out")
    cqkv, cab = g("cqkv"), g("cab")
    QKVC, QKVP, CONVW, BABB, WAB = g("QKVC"), g("QKVP"), g("CONVW"), g("BABB"), g("WAB")
    TRN, XS, PW_P, PW_T, EOT, QTT = g("TRN"), g("XS"), g("PW_P"), g("PW_T"), g("EOT"), g("QTT")
    RSOL, KTL, GT_C, PWB = g("RSOL"), g("KTL"), g("GT_C"), g("PWB")
    GACC, HACC, GACCT = g("GACC"), g("HACC"), g("GACCT")
    DNAT, DTRN, EOX, EARG, TT1, TT2 = g("DNAT"), g("DTRN"), g("EOX"), g("EARG"), g("TT1"), g("TT2")
    ABR, ASIG, GAMS = g("ABR"), g("ASIG"), g("GAMS")
    NK2, NQ2, LNK, LNQ, INVK, INVQ = g("NK2"), g("NQ2"), g("LNK"), g("LNQ"), g("INVK"), g("INVQ")
    GPC, BGIK, RTIK, PACK, RROW = g("GPC"), g("BGIK"), g("RTIK"), g("PACK"), g("RROW")
    SPF1, SPF2 = g("SPF1"), g("SPF2")
    WSL = g("WSL")

    em("vector", lambda eng: eng.memset(HACC[:, :], 0.0), inc=s_ve)
    em("vector", lambda eng: eng.memset(QKVP[:, :], 0.0), inc=s_ve)
    for h in range(NH):
        em("vector", lambda eng, h=h: eng.tensor_copy(GACC[:, h * 128:(h + 1) * 128],
                                                      cs["idb"][:, :]), inc=s_ve)
        em("vector", lambda eng, h=h: eng.tensor_copy(GACCT[:, h * 128:(h + 1) * 128],
                                                      cs["idb"][:, :]), inc=s_ve)
    for k in range(16):
        dm("sync", WAB[:, k * 32:(k + 1) * 32], cab.ap()[k, :, 0:32], s_ld)
    wt("tensor", s_ld)
    wt("tensor", s_ve)
    m = em("tensor", lambda eng: eng.matmul(PS[7][:, 0:32], cs["on1f"][0:1, :],
                                            BABB[0:1, 0:32], start=True, stop=True), inc=s_mm)
    wt("vector", s_mm, m)
    em("vector", lambda eng: eng.tensor_copy(BABB[:, 0:32], PS[7][:, 0:32]), inc=s_ve)
    g("full_barrier")("mxinit")

    def qkv_block(xnoff, ncols, want_conv):
        mm_t = [0] * 48
        cv_t = [0] * 48
        si_t = [0] * 48
        wt("vector", s_sc)
        wt("vector", s_gp)
        for cb in range(48):
            buf = WSL[cb % 2]
            bk = PS[1 + cb % 2]
            if cb >= 2:
                wt("scalar" if cb % 2 == 0 else "sync", s_mm, mm_t[cb - 2])
            else:
                wt("scalar" if cb % 2 == 0 else "sync", s_mm)
            wd = dm("scalar" if cb % 2 == 0 else "sync", buf[:, :], cqkv.ap()[cb, :, :],
                    s_wd if cb % 2 == 0 else s_ld)
            wt("tensor", s_wd if cb % 2 == 0 else s_ld, wd)
            if cb >= 2:
                wt("tensor", s_ve, cv_t[cb - 2])

            def mmq(eng, buf=buf, bk=bk, xnoff=xnoff, ncols=ncols):
                last = None
                for k in range(16):
                    last = eng.matmul(bk[:, 0:ncols], buf[:, k * 128:(k + 1) * 128],
                                      XN[:, k * 512 + xnoff:k * 512 + xnoff + ncols],
                                      start=(k == 0), stop=(k == 15))
                return last
            mm_t[cb] = em("tensor", mmq, inc=s_mm)
            wt("vector", s_mm, mm_t[cb])
            if cb >= 2:
                wt("vector", s_sc, si_t[cb - 2])
            if want_conv:
                def fconv(eng, bk=bk, cb=cb):
                    o = JUNK[:, (cb % 2) * 128:(cb % 2) * 128 + 128]
                    w3 = CONVW[:, cb * 4 + 3:cb * 4 + 4]
                    eng.tensor_scalar(o, bk[:, 0:128], w3, None, ALU.mult)
                    for j in range(3):
                        w = CONVW[:, cb * 4 + j:cb * 4 + j + 1]
                        sh = 3 - j
                        eng.scalar_tensor_tensor(o[:, sh:128], bk[:, 0:128 - sh], w,
                                                 o[:, sh:128], ALU.mult, ALU.add)
                        eng.scalar_tensor_tensor(o[:, 0:sh],
                                                 QKVP[:, cb * 4 + 4 - sh:cb * 4 + 4], w,
                                                 o[:, 0:sh], ALU.mult, ALU.add)
                    return eng.tensor_copy(QKVP[:, cb * 4 + 1:cb * 4 + 4], bk[:, 125:128])
                cv_t[cb] = em("vector", fconv, inc=s_ve)
                wt("scalar", s_ve, cv_t[cb])
                si_t[cb] = em("scalar", lambda eng, cb=cb:
                              eng.activation(QKVC[:, cb * 128:(cb + 1) * 128],
                                             JUNK[:, (cb % 2) * 128:(cb % 2) * 128 + 128],
                                             AF.Silu), inc=s_sc)
            else:
                cv_t[cb] = em("vector", lambda eng, bk=bk, cb=cb, ncols=ncols:
                              eng.tensor_copy(QKVP[:, cb * 4 + 1:cb * 4 + 4],
                                              bk[:, ncols - 3:ncols]), inc=s_ve)
                si_t[cb] = P.n(s_sc)
        wt("tensor", s_sc)
        wt("tensor", s_ve)

    ln_stats(0, HALO)
    make_xn(0, HALO)
    qkv_block(0, HALO, want_conv=False)

    for ch in range(n_ch):
        tg = (ch * C) // 512
        if (ch * C) % 512 == 0:
            xb = HALO + tg * 512
            n = min(512, T_local - tg * 512)
            ln_stats(xb, n)
            make_xn(xb, n)
        xnoff = (ch * C) % 512
        qkv_block(xnoff, C, want_conv=True)

        wt("tensor", s_sc)
        def mmab(eng, xnoff=xnoff):
            last = None
            for k in range(16):
                last = eng.matmul(PS[3][0:32, 0:C], WAB[:, k * 32:(k + 1) * 32],
                                  XN[:, k * 512 + xnoff:k * 512 + xnoff + C],
                                  start=(k == 0), stop=(k == 15))
            return last
        m = em("tensor", mmab, inc=s_mm)
        wt("vector", s_mm, m)
        v = em("vector", lambda eng: eng.tensor_copy(ABR[0:32, 0:C], PS[3][0:32, 0:C]), inc=s_ve)
        wt("tensor", s_ve, v)
        m = em("tensor", lambda eng: eng.transpose(PS[4][:, 0:32], ABR[0:32, 0:C],
                                                   cs["idf"][0:32, 0:32]), inc=s_mm)
        wt("vector", s_mm, m)
        wt("vector", s_sc)
        v = em("vector", lambda eng: eng.tensor_add(ASIG[:, 0:32], PS[4][:, 0:32],
                                                    BABB[:, 0:32]), inc=s_ve)
        wt("scalar", s_ve, v)
        sc = em("scalar", lambda eng: eng.activation(ASIG[:, 0:32], ASIG[:, 0:32],
                                                     AF.Sigmoid), inc=s_sc)
        sc = em("scalar", lambda eng: eng.activation(GAMS[:, 0:NH], ASIG[:, 0:NH], AF.Ln),
                inc=s_sc)
        wt("tensor", s_sc, sc)
        em("tensor", lambda eng: eng.matmul(PS[3][:, 0:NH], cs["trii"][:, :], GAMS[:, 0:NH],
                                            start=True, stop=True), inc=s_mm)
        m = em("tensor", lambda eng: eng.matmul(PS[4][:, 0:NH], cs["trir"][:, :], GAMS[:, 0:NH],
                                                start=True, stop=True), inc=s_mm)
        wt("vector", s_mm, m)
        wt("vector", s_sc)

        def fg(eng):
            eng.tensor_copy(GAMS[:, NH:2 * NH], PS[3][:, 0:NH])
            eng.tensor_copy(GAMS[:, 2 * NH:3 * NH], PS[4][:, 0:NH])
            return eng.tensor_add(GAMS[:, 3 * NH:4 * NH], GAMS[:, NH:2 * NH],
                                  GAMS[:, 2 * NH:3 * NH])
        v = em("vector", fg, inc=s_ve)
        wt("scalar", s_ve, v)
        em("scalar", lambda eng: eng.activation(GAMS[:, 4 * NH:5 * NH], GAMS[:, NH:2 * NH],
                                                AF.Exp), inc=s_sc)
        em("scalar", lambda eng: eng.activation(GAMS[:, 5 * NH:6 * NH], GAMS[:, 3 * NH:4 * NH],
                                                AF.Exp), inc=s_sc)
        sc = em("scalar", lambda eng: eng.activation(GAMS[:, 6 * NH:7 * NH],
                                                     GAMS[:, 2 * NH:3 * NH], AF.Exp), inc=s_sc)
        wt("tensor", s_sc, sc)
        wt("vector", s_sc, sc)

        for h in range(NH):
            kn = QKVC[:, (NH + h) * 128:(NH + h + 1) * 128]
            vn = QKVC[:, (2 * NH + h) * 128:(2 * NH + h + 1) * 128]
            qn = QKVC[:, h * 128:(h + 1) * 128]
            m = em("tensor", lambda eng, kn=kn: eng.transpose(PSB0[:, 0:128], kn,
                                                              cs["idb"][:, :]), inc=s_mm)
            m = em("tensor", lambda eng, vn=vn: eng.transpose(PSB0[:, 128:256], vn,
                                                              cs["idb"][:, :]), inc=s_mm)
            m = em("tensor", lambda eng, qn=qn: eng.transpose(PSB0[:, 256:384], qn,
                                                              cs["idb"][:, :]), inc=s_mm)
            wt("vector", s_mm, m)

            def fev(eng):
                eng.tensor_copy(TRN[:, 0:128], PSB0[:, 0:128])
                eng.tensor_copy(TRN[:, 128:256], PSB0[:, 128:256])
                return eng.tensor_copy(TRN[:, 256:384], PSB0[:, 256:384])
            v = em("vector", fev, inc=s_ve)
            wt("scalar", s_ve, v)
            wt("tensor", s_ve, v)
            sc = em("scalar", lambda eng, h=h:
                    eng.activation(JUNK[:, 0:128], TRN[:, 0:128], AF.Square,
                                   accum_out=NK2[:, h:h + 1]), inc=s_sc)
            sc = em("scalar", lambda eng, h=h:
                    eng.activation(JUNK[:, 128:256], TRN[:, 256:384], AF.Square,
                                   accum_out=NQ2[:, h:h + 1]), inc=s_sc)
            wt("vector", s_sc, sc)
            emit_head(P, L, ch, h)

        wt("sync", s_ve)
        dm("sync", out["spill_pt"][ch * NH * 128:(ch + 1) * NH * 128, :]
           .rearrange("(h p) t -> p h t", p=128),
           SPF1[:, :].rearrange("p (h t) -> p h t", h=NH), s_st)
        dm("sync", out["spill_op"][ch * NH * 128:(ch + 1) * NH * 128, :]
           .rearrange("(h p) t -> p h t", p=128),
           SPF2[:, :].rearrange("p (h t) -> p h t", h=NH), s_st)
        wt("vector", s_st)

    for h in range(NH):
        v = em("vector", lambda eng, h=h: eng.tensor_copy(SPF1[:, h * 128:(h + 1) * 128],
                                                          GACCT[:, h * 128:(h + 1) * 128]), inc=s_ve)
        v = em("vector", lambda eng, h=h: eng.tensor_copy(SPF2[:, h * 128:(h + 1) * 128],
                                                          HACC[:, h * 128:(h + 1) * 128]), inc=s_ve)
    wt("sync", s_ve)
    dm("sync", out["ght_out"].rearrange("(h p) t -> p h t", p=128),
       SPF1[:, :].rearrange("p (h t) -> p h t", h=NH), s_st)
    dm("sync", out["h_out"].rearrange("(h p) t -> p h t", p=128),
       SPF2[:, :].rearrange("p (h t) -> p h t", h=NH), s_st)


def emit_head(P, L, ch, h):
    g = lambda n: L[n]
    em, wt = P.em, P.wt
    s_mm, s_ve, s_sc = g("s_mm"), g("s_ve"), g("s_sc")
    cs, PS, PSB0 = g("cs"), g("PS"), g("PSB0")
    QKVC, TRN, XS, PW_P, PW_T = g("QKVC"), g("TRN"), g("XS"), g("PW_P"), g("PW_T")
    EOT, QTT, RSOL, KTL, GT_C = g("EOT"), g("QTT"), g("RSOL"), g("KTL"), g("GT_C")
    GACC, HACC, GACCT, PWB = g("GACC"), g("HACC"), g("GACCT"), g("PWB")
    DNAT, DTRN, EOX, EARG, TT1, TT2 = g("DNAT"), g("DTRN"), g("EOX"), g("EARG"), g("TT1"), g("TT2")
    ASIG, GAMS = g("ASIG"), g("GAMS")
    NK2, NQ2, LNK, LNQ, INVK, INVQ = g("NK2"), g("NQ2"), g("LNK"), g("LNQ"), g("INVK"), g("INVQ")
    GPC, BGIK, RTIK, PACK, RROW = g("GPC"), g("BGIK"), g("RTIK"), g("PACK"), g("RROW")
    SPF1, SPF2, JUNK = g("SPF1"), g("SPF2"), g("JUNK")

    Kt, Vt, Qt = TRN[:, 0:128], TRN[:, 128:256], TRN[:, 256:384]
    kn = QKVC[:, (g("NH") if False else NH + h) * 128:(NH + h + 1) * 128]
    qn = QKVC[:, h * 128:(h + 1) * 128]
    gcol = GAMS[:, NH + h:NH + h + 1]
    gamc = GAMS[:, 4 * NH + h:4 * NH + h + 1]
    gamCc = GAMS[:, 5 * NH + h:5 * NH + h + 1]
    ratc = GAMS[:, 6 * NH + h:6 * NH + h + 1]
    beta = ASIG[:, NH + h:NH + h + 1]

    def fc1(eng, h=h):
        eng.tensor_copy(LNK[:, h:h + 1], NK2[:, h:h + 1])
        return eng.tensor_copy(LNQ[:, h:h + 1], NQ2[:, h:h + 1])
    v = em("vector", fc1, inc=s_ve)
    wt("scalar", s_ve, v)
    sc = em("scalar", lambda eng, h=h: eng.activation(LNK[:, h:h + 1], LNK[:, h:h + 1], AF.Ln),
            inc=s_sc)
    sc = em("scalar", lambda eng, h=h: eng.activation(LNQ[:, h:h + 1], LNQ[:, h:h + 1], AF.Ln),
            inc=s_sc)
    sc = em("scalar", lambda eng, h=h: eng.activation(NK2[:, h:h + 1], NK2[:, h:h + 1], AF.Sqrt),
            inc=s_sc)
    sc = em("scalar", lambda eng, h=h: eng.activation(NQ2[:, h:h + 1], NQ2[:, h:h + 1], AF.Sqrt),
            inc=s_sc)
    sc = em("scalar", lambda eng, h=h: eng.activation(GPC[:, h:h + 1], ASIG[:, NH + h:NH + h + 1],
                                                      AF.Ln), inc=s_sc)
    wt("vector", s_sc, sc)

    def fc2(eng, h=h):
        eng.reciprocal(INVK[:, h:h + 1], NK2[:, h:h + 1])
        eng.tensor_scalar(INVK[:, h:h + 1], INVK[:, h:h + 1], INVCAP, None, ALU.min)
        eng.reciprocal(INVQ[:, h:h + 1], NQ2[:, h:h + 1])
        eng.tensor_scalar(INVQ[:, h:h + 1], INVQ[:, h:h + 1], INVCAP, None, ALU.min)
        eng.scalar_tensor_tensor(PACK[:, h:h + 1], LNK[:, h:h + 1], 0.5, gcol, ALU.mult, ALU.add)
        eng.tensor_add(GPC[:, h:h + 1], GPC[:, h:h + 1], gcol)
        eng.scalar_tensor_tensor(GPC[:, h:h + 1], LNK[:, h:h + 1], -0.5, GPC[:, h:h + 1],
                                 ALU.mult, ALU.add)
        eng.scalar_tensor_tensor(PACK[:, NH + h:NH + h + 1], LNQ[:, h:h + 1], -0.5, gcol,
                                 ALU.mult, ALU.add)
        eng.tensor_mul(PACK[:, 2 * NH + h:2 * NH + h + 1], gamc, INVQ[:, h:h + 1])
        eng.tensor_mul(BGIK[:, h:h + 1], beta, gamc)
        eng.tensor_mul(BGIK[:, h:h + 1], BGIK[:, h:h + 1], INVK[:, h:h + 1])
        return eng.tensor_mul(RTIK[:, h:h + 1], ratc, INVK[:, h:h + 1])
    v = em("vector", fc2, inc=s_ve)
    wt("tensor", s_ve, v)

    def bcast(col_ap, ps_out):
        wt("tensor", s_ve)
        m = em("tensor", lambda eng, col_ap=col_ap: eng.matmul(PS[6][0:1, 0:128], col_ap,
                                                               cs["idf"][:, :],
                                                               start=True, stop=True), inc=s_mm)
        wt("vector", s_mm, m)
        v = em("vector", lambda eng: eng.tensor_copy(RROW[0:1, 0:128], PS[6][0:1, 0:128]),
               inc=s_ve)
        wt("tensor", s_ve, v)
        return em("tensor", lambda eng, ps_out=ps_out:
                  eng.matmul(ps_out, cs["on1f"][0:1, :], RROW[0:1, 0:128],
                             start=True, stop=True), inc=s_mm)

    m = bcast(PACK[:, h:h + 1], PS[3][:, 0:128])
    wt("vector", s_mm, m)
    wt("vector", s_sc)

    def fearg(eng):
        eng.tensor_scalar(EARG[:, :], PS[3][:, 0:128], GPC[:, h:h + 1], -CLAMP,
                          ALU.subtract, ALU.max)
        return eng.tensor_tensor(EARG[:, :], EARG[:, :], cs["bias_nat"][:, :], ALU.add)
    v = em("vector", fearg, inc=s_ve)
    wt("scalar", s_ve, v)
    wt("tensor", s_ve, v)
    sc = em("scalar", lambda eng: eng.activation(DNAT[:, :], EARG[:, :], AF.Exp, scale=-1.0),
            inc=s_sc)
    m = em("tensor", lambda eng: eng.transpose(PS[4][:, 0:128], EARG[:, :], cs["idf"][:, :]),
           inc=s_mm)
    wt("scalar", s_mm, m)
    sc = em("scalar", lambda eng: eng.activation(DTRN[:, :], PS[4][:, 0:128], AF.Exp,
                                                 scale=-1.0), inc=s_sc)
    m = bcast(PACK[:, NH + h:NH + h + 1], PS[3][:, 0:128])
    wt("vector", s_mm, m)
    wt("vector", s_sc)

    def feargo(eng, h=h):
        eng.tensor_scalar(EARG[:, :], PS[3][:, 0:128], PACK[:, h:h + 1], CLAMP,
                          ALU.subtract, ALU.min)
        return eng.tensor_tensor(EARG[:, :], EARG[:, :], cs["bias_eot"][:, :],
                                 ALU.subtract)
    v = em("vector", feargo, inc=s_ve)
    wt("scalar", s_ve, v)
    sc = em("scalar", lambda eng: eng.activation(EOX[:, :], EARG[:, :], AF.Exp), inc=s_sc)

    wt("tensor", s_sc)
    em("tensor", lambda eng, kn=kn: eng.matmul(PS[5][:, 0:128], kn, kn, start=True, stop=True),
       inc=s_mm)
    m = em("tensor", lambda eng, kn=kn, qn=qn: eng.matmul(PS[5][:, 128:256], kn, qn,
                                                          start=True, stop=True), inc=s_mm)
    wt("vector", s_mm, m)

    wt("vector", s_sc)

    def fseed(eng):
        eng.scalar_tensor_tensor(PW_P[:, :], PS[5][:, 0:128], -1.0, DNAT[:, :],
                                 ALU.mult, ALU.mult)
        eng.scalar_tensor_tensor(PW_T[:, :], PS[5][:, 0:128], -1.0, DTRN[:, :],
                                 ALU.mult, ALU.mult)
        eng.scalar_tensor_tensor(EOT[:, :], PS[5][:, 128:256], 1.0, EOX[:, :],
                                 ALU.mult, ALU.mult)
        eng.tensor_scalar(RSOL[:, 0:128], Vt, ASIG[:, NH + h:NH + h + 1], None, ALU.mult)
        eng.tensor_scalar(RSOL[:, 128:256], Kt, BGIK[:, h:h + 1], None, ALU.mult)
        return eng.tensor_scalar(KTL[:, :], Kt, RTIK[:, h:h + 1], None, ALU.mult)
    v = em("vector", fseed, inc=s_ve)
    wt("tensor", s_ve, v)

    src = RSOL
    for j in range(7):
        wt("tensor", s_ve)
        wt("tensor", s_sc)
        em("tensor", lambda eng, src=src: eng.matmul(PS[7][:, 0:256], cs["idb"][:, :],
                                                     src[:, :], start=True, stop=False),
           inc=s_mm)
        m = em("tensor", lambda eng, src=src: eng.matmul(PS[7][:, 0:256], PW_T[:, :],
                                                         src[:, :], start=False, stop=True),
               inc=s_mm)
        if j < 6:
            em("tensor", lambda eng: eng.matmul(PS[6][:, 0:128], PW_T[:, :], PW_P[:, :],
                                                start=True, stop=True), inc=s_mm)
            m = em("tensor", lambda eng: eng.matmul(PS[6][:, 128:256], PW_P[:, :], PW_T[:, :],
                                                    start=True, stop=True), inc=s_mm)
        wt("scalar", s_mm, m)
        sc = em("scalar", lambda eng: eng.activation(XS[:, :], PS[7][:, 0:256], AF.Copy),
                inc=s_sc)
        if j < 6:
            sc = em("scalar", lambda eng: eng.activation(PWB[:, :], PS[6][:, 0:256],
                                                         AF.Copy), inc=s_sc)
        src = XS
    wt("tensor", s_sc)

    m = bcast(PACK[:, 2 * NH + h:2 * NH + h + 1], PS[3][:, 0:128])
    m2 = em("tensor", lambda eng: eng.matmul(PS[4][:, 0:128], XS[:, 128:256], EOT[:, :],
                                             start=True, stop=True), inc=s_mm)
    wt("vector", s_mm, m2)

    def fqt(eng, qn=qn):
        eng.tensor_tensor(TT1[:, :], qn, PS[3][:, 0:128], ALU.mult)
        return eng.tensor_tensor(QTT[:, :], TT1[:, :], PS[4][:, 0:128], ALU.subtract)
    v = em("vector", fqt, inc=s_ve)
    wt("tensor", s_ve, v)

    hA = HACC[:, h * 128:(h + 1) * 128]
    gA = GACC[:, h * 128:(h + 1) * 128]
    gAT = GACCT[:, h * 128:(h + 1) * 128]
    em("tensor", lambda eng: eng.matmul(PS[3][:, 0:128], XS[:, 0:128], EOT[:, :],
                                        start=True, stop=False), inc=s_mm)
    em("tensor", lambda eng, hA=hA: eng.matmul(PS[3][:, 0:128], hA, QTT[:, :],
                                               start=False, stop=True), inc=s_mm)
    m = em("tensor", lambda eng, gA=gA: eng.matmul(PS[4][:, 0:128], gA, QTT[:, :],
                                                   start=True, stop=True), inc=s_mm)
    wt("vector", s_mm, m)
    v = em("vector", lambda eng, h=h: eng.tensor_copy(SPF2[:, h * 128:(h + 1) * 128],
                                                      PS[3][:, 0:128]), inc=s_ve)
    v = em("vector", lambda eng, h=h: eng.tensor_copy(SPF1[:, h * 128:(h + 1) * 128],
                                                      PS[4][:, 0:128]), inc=s_ve)
    wt("tensor", s_ve, v)

    em("tensor", lambda eng: eng.matmul(PS[3][:, 0:128], KTL[:, :], XS[:, 0:128],
                                        start=True, stop=False), inc=s_mm)
    m = em("tensor", lambda eng: eng.matmul(PS[4][:, 0:128], XS[:, 128:256], KTL[:, :],
                                            start=True, stop=True), inc=s_mm)
    wt("vector", s_mm, m)

    def fgt(eng):
        eng.tensor_scalar(TT2[:, :], cs["idf"][:, :], gamCc, None, ALU.mult)
        return eng.scalar_tensor_tensor(GT_C[:, :], PS[4][:, 0:128], -1.0, TT2[:, :],
                                        ALU.mult, ALU.add)
    v = em("vector", fgt, inc=s_ve)
    wt("tensor", s_ve, v)
    em("tensor", lambda eng, hA=hA: eng.matmul(PS[3][:, 0:128], GT_C[:, :], hA,
                                               start=False, stop=True), inc=s_mm)
    em("tensor", lambda eng, gA=gA: eng.matmul(PS[5][:, 0:128], GT_C[:, :], gA,
                                               start=True, stop=True), inc=s_mm)
    m = em("tensor", lambda eng, gA=gA: eng.matmul(PS[5][:, 128:256], gA, GT_C[:, :],
                                                   start=True, stop=True), inc=s_mm)
    wt("vector", s_mm, m)
    v = em("vector", lambda eng, hA=hA: eng.tensor_copy(hA, PS[3][:, 0:128]), inc=s_ve)
    v = em("vector", lambda eng, gA=gA: eng.tensor_copy(gA, PS[5][:, 0:128]), inc=s_ve)
    v = em("vector", lambda eng, gAT=gAT: eng.tensor_copy(gAT, PS[5][:, 128:256]), inc=s_ve)
    wt("tensor", s_ve, v)
    wt("scalar", s_ve, v)


def emit_B(P, L):
    g = lambda n: L[n]
    em, wt, dm = P.em, P.wt, P.dma
    nc = g("nc")
    s_ld, s_wd, s_mm, s_ve, s_sc, s_gp, s_st = (g("s_ld"), g("s_wd"), g("s_mm"),
                                                g("s_ve"), g("s_sc"), g("s_gp"), g("s_st"))
    cs, PS = g("cs"), g("PS")
    XT, XN, JUNK = g("XT"), g("XN"), g("JUNK")
    TH, T_local, n_tg, n_ch = g("TH"), g("T_local"), g("n_tg"), g("n_ch")
    ln_stats, make_xn, ffn = g("ln_stats"), g("make_xn"), g("ffn")
    inp, out = g("inp"), g("out")
    cwg, cwo = g("cwg"), g("cwo")
    ZIN, GHS, GROW, OFIX = g("ZIN"), g("GHS"), g("GROW"), g("OFIX")
    BC1, x2scr = g("BC1"), g("x2scr")
    SPB1, SPB2, SPF1, SPF2 = g("SPB1"), g("SPB2"), g("SPF1"), g("SPF2")
    WSL = g("WSL")
    blocks = g("blocks")

    wt("vector", s_gp)
    wt("tensor", s_gp)
    wt("scalar", s_gp)

    em("vector", lambda eng: eng.memset(ZIN[:, :], 0.0), inc=s_ve)
    for j in range(NCORES - 1):
        base = j * 2 * NH * 128
        for h in range(NH):
            t = dm("gpsimd", GHS[:, 0:128],
                   inp["ghseq"][base + h * 128:base + (h + 1) * 128, :], s_gp)
            t = dm("gpsimd", GHS[:, 128:256],
                   inp["ghseq"][base + NH * 128 + h * 128:base + NH * 128 + (h + 1) * 128, :],
                   s_gp)
            wt("tensor", s_gp, t)
            wt("tensor", s_ve)
            em("tensor", lambda eng, h=h: eng.matmul(PS[1][:, 0:128], GHS[:, 0:128],
                                                     ZIN[:, h * 128:(h + 1) * 128],
                                                     start=True, stop=False), inc=s_mm)
            m = em("tensor", lambda eng: eng.matmul(PS[1][:, 0:128], cs["idb"][:, :],
                                                    GHS[:, 128:256], start=False, stop=True),
                   inc=s_mm)
            wt("vector", s_mm, m)
            v = em("vector", lambda eng, h=h: eng.tensor_copy(ZIN[:, h * 128:(h + 1) * 128],
                                                              PS[1][:, 0:128]), inc=s_ve)
            wt("gpsimd", s_ve, v)
    g("full_barrier")("zin")

    for tg in range(n_tg):
        g("full_barrier")("tg%d" % tg)
        xb = HALO + tg * 512
        n = min(512, T_local - tg * 512)
        ln_stats(xb, n)
        make_xn(xb, n)
        for dblk in range(DK):
            buf = WSL[dblk % 2]
            wt("scalar", s_sc)
            wt("scalar", s_ve)
            t = dm("scalar", buf[:, :], cwg.ap()[dblk, :, :], s_wd)
            wt("tensor", s_wd, t)
            wt("tensor", s_sc)

            def mmg(eng, buf=buf, n=n):
                last = None
                for k in range(DK):
                    last = eng.matmul(PS[1][:, 0:n], buf[:, k * 128:(k + 1) * 128],
                                      XN[:, k * 512:k * 512 + n],
                                      start=(k == 0), stop=(k == DK - 1))
                return last
            m = em("tensor", mmg, inc=s_mm)
            wt("scalar", s_mm, m)
            em("scalar", lambda eng, dblk=dblk, n=n:
               eng.activation(GROW[:, dblk * 512:dblk * 512 + n], PS[1][:, 0:n], AF.Silu),
               inc=s_sc)
        sc = P.n(s_sc)
        wt("tensor", s_sc, sc)
        wt("vector", s_sc, sc)
        for cc in range(4):
            ch = tg * 4 + cc
            if ch >= n_ch:
                break
            t = dm("gpsimd", SPB1[:, :].rearrange("p (h t) -> p h t", h=NH),
                   inp["spill_pt"][ch * NH * 128:(ch + 1) * NH * 128, :]
                   .rearrange("(h p) t -> p h t", p=128), s_gp)
            t = dm("gpsimd", SPB2[:, :].rearrange("p (h t) -> p h t", h=NH),
                   inp["spill_op"][ch * NH * 128:(ch + 1) * NH * 128, :]
                   .rearrange("(h p) t -> p h t", p=128), s_gp)
            wt("tensor", s_gp, t)
            for h in range(NH):
                em("tensor", lambda eng, h=h:
                   eng.matmul(PS[2][:, 0:128], ZIN[:, h * 128:(h + 1) * 128],
                              SPB1[:, h * 128:(h + 1) * 128], start=True, stop=False), inc=s_mm)
                m = em("tensor", lambda eng, h=h:
                       eng.matmul(PS[2][:, 0:128], cs["idb"][:, :],
                                  SPB2[:, h * 128:(h + 1) * 128], start=False, stop=True),
                       inc=s_mm)
                wt("vector", s_mm, m)
                v = em("vector", lambda eng, h=h, cc=cc:
                       eng.tensor_tensor(OFIX[:, h * 512 + cc * 128:h * 512 + (cc + 1) * 128],
                                         PS[2][:, 0:128],
                                         GROW[:, h * 512 + cc * 128:h * 512 + (cc + 1) * 128],
                                         ALU.mult), inc=s_ve)
                wt("tensor", s_ve, v)
            wt("gpsimd", s_ve)
        for dblk in range(DK):
            buf = WSL[dblk % 2]
            wt("scalar", s_ve)
            t = dm("scalar", buf[:, :], cwo.ap()[dblk, :, :], s_wd)
            wt("tensor", s_wd, t)

            def mmo(eng, buf=buf, n=n):
                last = None
                for k in range(DK):
                    last = eng.matmul(PS[1][:, 0:n], buf[:, k * 128:(k + 1) * 128],
                                      OFIX[:, k * 512:k * 512 + n],
                                      start=(k == 0), stop=(k == DK - 1))
                return last
            m = em("tensor", mmo, inc=s_mm)
            wt("vector", s_mm, m)

            wt("sync", s_ve)
            t = dm("sync", BC1[:, 0:n], inp["x1T"][dblk * 128:(dblk + 1) * 128,
                                                    xb:xb + n], s_ld)
            wt("vector", s_ld, t)

            def fr(eng, dblk=dblk, xb=xb, n=n):
                xx = XT[:, dblk * TH + xb:dblk * TH + xb + n]
                eng.tensor_add(JUNK[:, 0:n], BC1[:, 0:n], PS[1][:, 0:n])
                return eng.tensor_copy(xx, JUNK[:, 0:n])
            v = em("vector", fr, inc=s_ve)
            wt("sync", s_ve, v)
            dm("sync", x2scr.ap()[dblk * 128:(dblk + 1) * 128, xb - HALO:xb - HALO + n],
               JUNK[:, 0:n], s_st)
            wt("vector", s_st)
            wt("tensor", s_ve, v)
        g("full_barrier")("wo_tg%d" % tg)
        ln_stats(xb, n)
        make_xn(xb, n)
        ffn(xb, n, out_store=out["yT"])


_T_OVERRIDE = None
_NCORES_OVERRIDE = None


def kernel(L0_attn_x, L0_attn_W_qkv, L0_attn_conv_w, L0_attn_W_ab, L0_attn_b_ab,
           L0_attn_W_g, L0_attn_W_o, L0_ffn_w1, L0_ffn_w2, L0_ffn_w3):
    ncores = _NCORES_OVERRIDE or NCORES
    T = L0_attn_x.shape[1]
    Tl = _T_OVERRIDE or (T // ncores)
    x0T = np.ascontiguousarray(np.asarray(L0_attn_x)[0].T)
    w1T = np.ascontiguousarray(np.asarray(L0_ffn_w1).T)
    w2T = np.ascontiguousarray(np.asarray(L0_ffn_w2).T)
    w3T = np.ascontiguousarray(np.asarray(L0_ffn_w3).T)
    wqkvT = np.ascontiguousarray(np.asarray(L0_attn_W_qkv).T)
    wabT = np.ascontiguousarray(np.asarray(L0_attn_W_ab).T)
    wgT = np.ascontiguousarray(np.asarray(L0_attn_W_g).T)
    woT = np.ascontiguousarray(np.asarray(L0_attn_W_o).T)
    convw = np.ascontiguousarray(np.asarray(L0_attn_conv_w))
    bab = np.ascontiguousarray(np.asarray(L0_attn_b_ab)[None, :]).astype(np.float32)

    zh = np.zeros((D, HALO), dtype=np.float32)
    in_a = []
    for c in range(ncores):
        lo = c * Tl
        if lo >= HALO:
            xs = x0T[:, lo - HALO:lo + Tl]
        else:
            xs = np.concatenate([zh[:, :HALO - lo], x0T[:, 0:lo + Tl]], axis=1)
        in_a.append(dict(xT=np.ascontiguousarray(xs), w1T=w1T, w2T=w2T, w3T=w3T,
                         wqkvT=wqkvT, wabT=wabT, convw=convw, bab=bab,
                         wgT=wgT, woT=woT))

    nc_a = build_program("A", Tl)
    res_a = run_bass_kernel_spmd(nc_a, in_a, core_ids=list(range(ncores)))

    ident = np.tile(np.eye(128, dtype=np.float32), (NH, 1))
    zero = np.zeros((NH * 128, 128), dtype=np.float32)
    in_b = []
    for c in range(ncores):
        seq = []
        for j in range(NCORES - 1):
            if j < c:
                seq.append(res_a.results[j]["ght_out"])
                seq.append(res_a.results[j]["h_out"])
            else:
                seq.append(ident)
                seq.append(zero)
        ra = res_a.results[c]
        in_b.append(dict(x1T=ra["x1T_out"],
                         cw1=ra["cw1"], cw2=ra["cw2"], cw3=ra["cw3"],
                         cwg=ra["cwg"], cwo=ra["cwo"],
                         spill_pt=ra["spill_pt"], spill_op=ra["spill_op"],
                         ghseq=np.ascontiguousarray(np.concatenate(seq, axis=0))))

    nc_b = build_program("B", Tl)
    res_b = run_bass_kernel_spmd(nc_b, in_b, core_ids=list(range(ncores)))

    outv = np.concatenate([res_b.results[c]["yT"].T for c in range(ncores)], axis=0)
    return outv[None, :, :].astype(np.float32)
